# version 21
# baseline (speedup 1.0000x reference)
import contextlib
import math
import numpy as np
import ml_dtypes
import concourse.bass as bass
import concourse.mybir as mybir

F32 = mybir.dt.float32
BF16 = mybir.dt.bfloat16
AF = mybir.ActivationFunctionType
ALU = mybir.AluOpType
AX = mybir.AxisListType

LN_EPS = 1e-5
NEG_BIG = -1.0e30


def cfg_full():
    return dict(NSEQ=2, S=2048, D=1024, H=16, DK=64, FF=4096, SUP=512,
                THETA=10000.0)


class KB:
    def __init__(self, nc):
        self.nc = nc
        self.cur = None
        self.sems = {}
        self.n = {}
        self.last_wait = {}
        self.eng = {}

    DMA_SEMS = ("k0", "k1", "k2", "k3", "k4", "k5", "k6", "k7", "ao",
                "w0", "w1", "w2", "w3",
                "x0", "x1", "xr", "st", "b0", "b1", "b2", "bw",
                "c0", "c1", "c2", "c3", "tb", "rb",
                "g0", "g1", "g2", "g3")

    def setup_sems(self, stack):
        for name in ("ct", "cv", "cs", "cg") + self.DMA_SEMS:
            self.sems[name] = stack.enter_context(self.nc.semaphore(name))

    def begin_pass(self, engname, engobj):
        self.cur = engname
        self.eng[engname] = engobj
        self.n = {k: 0 for k in self.sems}
        self.last_wait = {k: 0 for k in self.sems}

    def _waits(self, waits):
        for tok in waits:
            if tok is None:
                continue
            if isinstance(tok, (list, tuple)) and tok and \
                    isinstance(tok[0], (list, tuple)):
                self._waits(tok)
                continue
            sem_name, val = tok
            if val > self.last_wait[sem_name]:
                self.eng[self.cur].wait_ge(self.sems[sem_name], val)
                self.last_wait[sem_name] = val

    def op(self, engname, emit, waits=()):
        sem = {"tensor": "ct", "vector": "cv", "scalar": "cs",
               "gpsimd": "cg"}[engname]
        self.n[sem] += 1
        tok = (sem, self.n[sem])
        if self.cur == engname:
            self._waits(waits)
            emit(self.eng[engname]).then_inc(self.sems[sem], 1)
        return tok

    def dma(self, engname, sem, out, in_, waits=()):
        self.n[sem] += 16
        tok = (sem, self.n[sem])
        if self.cur == engname:
            self._waits(waits)
            self.eng[engname].dma_start(out=out, in_=in_).then_inc(self.sems[sem], 16)
        return tok

    def wait_only(self, engname, waits):
        if self.cur == engname:
            self._waits(waits)


def build_program(cfg):
    NSEQ, S, D, H, DK, FF, SUP = (cfg[k] for k in
                                  ("NSEQ", "S", "D", "H", "DK", "FF", "SUP"))
    NT = S // 128
    NDC = D // 128
    NFC = FF // 128
    NPAIR = H // 2
    NSUPT = SUP // 128
    NSUPER = (NSEQ * S) // SUP
    W1T = FF // 128
    W2T = D // 128
    QS = 4 if NT % 4 == 0 else NT
    NSUPQ = NT // QS
    SCALE = 1.0 / math.sqrt(DK)

    nc = bass.Bass("TRN2", target_bir_lowering=False, debug=False)

    xs = nc.dram_tensor("xs", [NSEQ, S, D], F32, kind="ExternalInput").ap()
    wqT = nc.dram_tensor("wqT", [D, D], F32, kind="ExternalInput").ap()
    wkT = nc.dram_tensor("wkT", [D, D], F32, kind="ExternalInput").ap()
    wvT = nc.dram_tensor("wvT", [D, D], F32, kind="ExternalInput").ap()
    woT = nc.dram_tensor("woT", [D, D], F32, kind="ExternalInput").ap()
    w1 = nc.dram_tensor("w1", [FF, D], F32, kind="ExternalInput").ap()
    w2 = nc.dram_tensor("w2", [D, FF], F32, kind="ExternalInput").ap()
    b1r = nc.dram_tensor("b1r", [128, NFC], F32, kind="ExternalInput").ap()
    b2b = nc.dram_tensor("b2b", [128, D], F32, kind="ExternalInput").ap()
    identd = nc.dram_tensor("identd", [128, 128], BF16, kind="ExternalInput").ap()
    masktd = nc.dram_tensor("masktd", [128, 128], BF16, kind="ExternalInput").ap()
    roped = nc.dram_tensor("roped", [S, 64], BF16, kind="ExternalInput").ap()
    onesd = nc.dram_tensor("onesd", [1, 128], BF16, kind="ExternalInput").ap()
    epsd = nc.dram_tensor("epsd", [128, 1], F32, kind="ExternalInput").ap()
    tinyd = nc.dram_tensor("tinyd", [128, 1], F32, kind="ExternalInput").ap()
    y = nc.dram_tensor("y", [NSEQ, S, D], F32, kind="ExternalOutput").ap()
    x2s = nc.dram_tensor("x2s", [NSEQ * S, D], F32, kind="Internal").ap()
    bw2s = nc.dram_tensor("bw2s", [FF, D], BF16, kind="Internal").ap()

    A = nc.alloc_sbuf_tensor
    ident = A("ident", [128, 128], BF16).ap()
    maskt = A("maskt", [128, 128], BF16).ap()
    rope = A("rope", [128, NT * 64], BF16).ap()
    b1s = A("b1s", [128, NFC], F32).ap()
    b2s = A("b2s", [128, D], F32).ap()
    stat = A("stat", [128, 16], F32).ap()
    epsb = A("epsb", [128, 1], F32).ap()
    tinyb = A("tinyb", [128, 1], F32).ap()
    onesr = A("onesr", [1, 128], BF16).ap()
    qn2 = A("qn2", [128, NT * H], F32).ap()
    kn2 = A("kn2", [128, NT * H], F32).ap()
    mbb = A("mbb", [128, NT * H], F32).ap()
    km = A("km", [128, 16], F32).ap()
    km_bf = A("km_bf", [128, 16], BF16).ap()
    km2t = A("km2t", [128, 1], F32).ap()
    km2t_bf = A("km2t_bf", [128, 1], BF16).ap()
    kmrow = A("kmrow", [1, 16], BF16).ap()
    kmb = A("kmb", [128, 16], F32).ap()
    NC2 = (NT * H + 127) // 128
    mxc = A("mxc", [128, 4], F32).ap()
    mxcb = A("mxcb", [128, 4], BF16).ap()
    mrows = A("mrows", [1, 256], BF16).ap()


    (ST_NSUM, ST_NM, ST_SSQ, ST_LNV, ST_RSTD, ST_M, ST_MB, ST_R, ST_MP) = range(9)

    RQ = A("RQ", [128, 49664], BF16).ap()
    q_fm = RQ[:, 0:NPAIR * S]
    k_fm = RQ[:, 16384:16384 + NPAIR * S]
    v_tm = RQ[:, 32768:32768 + NT * H * 65]
    v_tm4 = v_tm.rearrange("p (t h c) -> p t h c", h=H, c=65)
    bw1T = RQ[:, 0:NDC * FF]
    RW = A("RW", [128, 32768], BF16).ap()
    wq_b = RW[:, 0:8192]
    wk_b = RW[:, 8192:16384]
    wv_b = RW[:, 16384:24576]
    wo_b = RW[:, 24576:32768]
    h_fm = RW[:, 0:NFC * SUP]
    x2_fm = RW[:, 16384:16384 + NDC * SUP]
    bw2c = RW[:, 24576:28672]
    RS = A("RS", [128, 5120], F32).ap()
    xt_b = (RS[:, 0:1024], RS[:, 1024:2048])
    p_f32 = RS[:, 0:2048]
    xres = RS[:, 2048:3072]
    rbs_all = RS[0:64, 0:1024]
    lr_all = RS[0:1, 1024:2048]
    rhi_all = RS[0:1, 0:512].bitcast(BF16)
    rlo_all = RS[0:1, 512:1024].bitcast(BF16)
    sqbuf = RS[:, 3072:4096]
    ybuf = RS[:, 4096:5120]
    w2t_f = RS[:, 0:4096]
    RB = A("RB", [128, 7168], BF16).ap()
    xbf = RB[:, 0:1024]
    x_fm = RB[:, 1024:2048]
    qslots = (RB[:, 4096:5120], RB[:, 2048:3072])
    kslots = (RB[:, 5120:6144], RB[:, 3072:4096])
    attn_sup = RB[:, 0:4096]
    pt_sb2 = (RB[:, 4096:5120], RB[:, 5120:6144])
    attnO = RB[:, 6144:6656]
    sgn = RB[:, 0:4096]
    sgn_s = RB[:, 4096:5120]
    sgn_s2 = RB[:, 5120:6144]
    x2bf = RB[:, 0:1024]
    xr2_b = (RB[:, 1024:3072].bitcast(F32), RB[:, 3072:5120].bitcast(F32))
    tmpX = RB[:, 6144:7168]
    mxbf = RB[:, 6144:6144 + NT * H]

    psA = nc.alloc_psum_tensor("psA", [128, 2048], F32).ap()
    psB = nc.alloc_psum_tensor("psB", [128, 1024], F32).ap()
    psE = nc.alloc_psum_tensor("psE", [128, 1024], F32).ap()

    sbuf_left = nc.sbuf_bytes_remaining

    kb = KB(nc)
    T, V, Sc, Sy, G = "tensor", "vector", "scalar", "sync", "gpsimd"

    def program():
        pv = {}
        ybuf_r = [None]
        sqbuf_r = [None]

        t_id = kb.dma(Sy, "k0", ident, identd)
        t_maskt = kb.dma(Sy, "k7", maskt, masktd)
        t_rope = kb.dma(Sy, "k2", rope.rearrange("p (t c) -> p t c", c=64),
                        roped.rearrange("(t p) c -> p t c", p=128))
        t_b1 = kb.dma(Sy, "k3", b1s, b1r)
        t_b2 = kb.dma(Sy, "k4", b2s, b2b)
        t_eps = kb.dma(Sy, "k5", epsb, epsd)
        t_tiny = kb.dma(Sy, "tb", tinyb, tinyd)
        t_ones = kb.dma(Sy, "k6", onesr, onesd)
        t_vones = kb.op(V, lambda e: e.memset(v_tm4[:, :, :, 64:65], 1.0))
        t_mxz = kb.op(V, lambda e: e.memset(mxc, 0.0))
        t_w = []
        for wsem, dst, src in (("w0", wq_b, wqT), ("w1", wk_b, wkT),
                               ("w2", wv_b, wvT), ("w3", wo_b, woT)):
            t_w.append(kb.dma(G, wsem, dst.rearrange("p (c o) -> p c o", o=D),
                              src.rearrange("(c p) o -> p c o", p=128)))

        for s in range(NSEQ):
            def emit_xload(t):
                ww = [pv.get(("xcast", s, t - 2))]
                if s > 0 and t < 2:
                    ww.append(pv.get(("x2dma", s - 1, NT - 1)))
                t_x = kb.dma(Sy, ("x0", "x1")[t % 2], xt_b[t % 2],
                             xs[s, t * 128:(t + 1) * 128, :], waits=ww)
                ww = [t_x, pv.get(("xtr", s, t - 1))]
                if s > 0 and t == 0:
                    ww.append(pv.get(("x2dma", s - 1, NT - 1)))
                t_cast = kb.op(G, lambda e, t=t: e.tensor_copy(
                    xbf, xt_b[t % 2]), waits=ww)
                pv[("xcast", s, t)] = t_cast

            def emit_xtr(t):
                ww = [pv[("xcast", s, t)], t_id, pv.get(("kc", s, t - 1))]
                tr = None
                for dc in range(NDC):
                    tr = kb.op(T, lambda e, dc=dc: e.matmul(
                        psA[:, 1024 + dc * 128:1024 + (dc + 1) * 128],
                        xbf[:, dc * 128:(dc + 1) * 128], ident,
                        start=True, stop=True), waits=ww if dc == 0 else ())
                pv[("xtr", s, t)] = tr

            def emit_xfm(t):
                t_xfm = kb.op(Sc, lambda e: e.copy(x_fm, psA[:, 1024:2048]),
                              waits=[pv[("xtr", s, t)],
                                     pv.get(("qkv", s, t - 1))])
                pv[("xfm", s, t)] = t_xfm

            for t in range(NT):
                if t == 0:
                    emit_xload(0)
                    emit_xtr(0)
                    emit_xfm(0)
                if t + 1 < NT:
                    emit_xload(t + 1)
                w_q = [pv[("xfm", s, t)], pv.get(("qc", s, t - 1))]
                if s == 0 and t == 0:
                    w_q = w_q + t_w
                w_k = [pv.get(("kc", s, t - 1))]
                w_v = [pv.get(("vev", s, t - 1)), pv.get(("kev", s, t - 2))]
                qkv_reg = []
                for wb, po, ww in ((wq_b, 0, w_q), (wk_b, 1024, w_k),
                                   (wv_b, None, w_v)):
                    mm = None
                    for half in range(2):
                        for dc in range(NDC):
                            def mmf(e, dc=dc, wb=wb, po=po, half=half):
                                pdst = psB if po is None else psA[:, po:po + 1024]
                                return e.matmul(
                                    pdst[:, half * 512:(half + 1) * 512],
                                    x_fm[:, dc * 128:(dc + 1) * 128],
                                    wb[:, dc * 1024 + half * 512:
                                       dc * 1024 + (half + 1) * 512],
                                    start=(dc == 0), stop=(dc == NDC - 1))
                            mm = kb.op(T, mmf,
                                       waits=ww if (dc == 0 and half == 0)
                                       else ())
                    qkv_reg.append(mm)
                mm = qkv_reg[2]
                pv[("qkv", s, t)] = mm

                qro = qslots[t % 2]
                kro = kslots[t % 2]
                t_qc = kb.op(Sc, lambda e, qro=qro: e.copy(qro, psA[:, 0:1024]),
                             waits=[qkv_reg[0], pv.get(("trq", s, t - 2)),
                                    pv.get(("qsq", s, t - 2))])
                pv[("qc", s, t)] = t_qc
                t_kc = kb.op(Sc, lambda e, kro=kro: e.copy(kro, psA[:, 1024:2048]),
                             waits=[qkv_reg[1], pv.get(("trk", s, t - 2)),
                                    pv.get(("ksq", s, t - 2))])
                pv[("kc", s, t)] = t_kc
                if t + 1 < NT:
                    emit_xtr(t + 1)
                    emit_xfm(t + 1)

                def emit_rope(buf, cast_tok):
                    b3 = buf.rearrange("p (h i) -> p h i", i=64)
                    cosb = rope[:, t * 64:t * 64 + 32].unsqueeze(1).broadcast_to(
                        [128, H, 32])
                    sinb = rope[:, t * 64 + 32:t * 64 + 64].unsqueeze(1).broadcast_to(
                        [128, H, 32])
                    lo, hi = b3[:, :, 0:32], b3[:, :, 32:64]
                    ta = tmpX[:, 0:512].rearrange("p (h i) -> p h i", i=32)
                    tb = tmpX[:, 512:1024].rearrange("p (h i) -> p h i", i=32)
                    a1 = kb.op(V, lambda e: e.tensor_tensor(ta, lo, cosb, ALU.mult),
                               waits=[cast_tok, t_rope, pv.get(("tmpX_r",))])
                    a2 = kb.op(V, lambda e: e.tensor_tensor(tb, lo, sinb, ALU.mult),
                               waits=[a1])
                    a3 = kb.op(V, lambda e: e.tensor_tensor(lo, hi, sinb, ALU.mult),
                               waits=[a2])
                    a4 = kb.op(V, lambda e: e.tensor_tensor(lo, ta, lo, ALU.subtract),
                               waits=[a3])
                    a5 = kb.op(V, lambda e: e.tensor_tensor(ta, hi, cosb, ALU.mult),
                               waits=[a4])
                    a6 = kb.op(V, lambda e: e.tensor_add(hi, ta, tb),
                               waits=[a5])
                    pv[("tmpX_r",)] = a6
                    return a6

                def emit_norm(dst_col, ro):
                    sq = kb.op(Sc, lambda e, ro=ro: e.activation(
                        tmpX, ro, AF.Square, bias=mxc[:, 3:4]),
                        waits=[pv.get(("tmpX_r",)), t_mxz])
                    pv[("tmpX_r",)] = sq
                    rd = kb.op(V, lambda e, dst_col=dst_col: e.reduce_sum(
                        dst_col, tmpX.rearrange("p (h i) -> p h i", i=64),
                        axis=AX.X), waits=[sq, pv.get(("nrm_r",))])
                    pv[("tmpX_r",)] = rd
                    return sq, rd

                rqf = emit_rope(qro, t_qc)
                pv[("ropeqf", s, t)] = rqf
                qsq, rnq = emit_norm(qn2[:, t * H:(t + 1) * H], qro)
                pv[("qsq", s, t)] = qsq
                pv[("qn2r", s, t)] = rnq
                rkf = emit_rope(kro, t_kc)
                pv[("ropekf", s, t)] = rkf
                ksq, rnk = emit_norm(kn2[:, t * H:(t + 1) * H], kro)
                pv[("ksq", s, t)] = ksq
                pv[("kn2r", s, t)] = rnk
                t_vev = kb.op(V, lambda e: e.tensor_copy(
                    v_tm4[:, t, :, 0:64],
                    psB.rearrange("p (h c) -> p h c", c=64)),
                    waits=[mm, t_vones if (s == 0 and t == 0) else None])
                pv[("vev", s, t)] = t_vev
                def emit_tr(tt):
                    qro2, kro2 = qslots[tt % 2], kslots[tt % 2]
                    trq = None
                    for hp in range(NPAIR):
                        trq = kb.op(T, lambda e, hp=hp, qro2=qro2: e.matmul(
                            psE[:, hp * 128:(hp + 1) * 128],
                            qro2[:, hp * 128:(hp + 1) * 128], ident,
                            start=True, stop=True),
                            waits=[pv[("ropeqf", s, tt)],
                                   pv.get(("qev", s, tt - 1))] if hp == 0 else ())
                    pv[("trq", s, tt)] = trq
                    t_qev = kb.op(V, lambda e, tt=tt: e.tensor_copy(
                        q_fm.rearrange("p (hp q) -> p hp q", q=S)[:, :, tt * 128:(tt + 1) * 128],
                        psE.rearrange("p (hp q) -> p hp q", q=128)), waits=[trq])
                    pv[("qev", s, tt)] = t_qev
                    trk = None
                    for hp in range(NPAIR):
                        trk = kb.op(T, lambda e, hp=hp, kro2=kro2: e.matmul(
                            psB[:, hp * 128:(hp + 1) * 128],
                            kro2[:, hp * 128:(hp + 1) * 128], ident,
                            start=True, stop=True),
                            waits=[pv[("ropekf", s, tt)],
                                   pv.get(("vev", s, tt + 1)),
                                   pv.get(("kev", s, tt - 1))] if hp == 0 else ())
                    pv[("trk", s, tt)] = trk
                    t_kev = kb.op(Sc, lambda e, tt=tt: e.copy(
                        k_fm.rearrange("p (hp q) -> p hp q", q=S)[:, :, tt * 128:(tt + 1) * 128],
                        psB.rearrange("p (hp q) -> p hp q", q=128)), waits=[trk])
                    pv[("kev", s, tt)] = t_kev
                if t > 0:
                    emit_tr(t - 1)
            emit_tr(NT - 1)

            p1_end = [pv[("qev", s, NT - 1)], pv[("kev", s, NT - 1)],
                      pv[("vev", s, NT - 1)]]

            kn_all = [pv[("kn2r", s, t)] for t in range(NT)]
            m1 = kb.op(V, lambda e: e.reduce_max(
                km[:, 0:H], kn2.rearrange("p (t h) -> p h t", h=H), axis=AX.X),
                waits=kn_all)
            m2 = kb.op(V, lambda e: e.tensor_scalar_mul(km_bf[:, 0:H], km[:, 0:H],
                                                        1.05), waits=[m1])
            m3 = kb.op(T, lambda e: e.matmul(psB[0:H, 0:128], km_bf[:, 0:H], ident,
                                             start=True, stop=True),
                       waits=[m2, pv[("kev", s, NT - 1)], t_ones])
            m4 = kb.op(V, lambda e: e.reduce_max(km2t[0:H, 0:1], psB[0:H, 0:128],
                                                 axis=AX.X), waits=[m3])
            m5 = kb.op(V, lambda e: e.tensor_scalar_mul(km2t_bf[0:H, 0:1],
                                                        km2t[0:H, 0:1], 1.0),
                       waits=[m4])
            m6 = kb.op(T, lambda e: e.matmul(psA[0:1, 0:H],
                                             km2t_bf[0:H, 0:1],
                                             ident[0:H, 0:H],
                                             start=True, stop=True),
                       waits=[m5, pv[("qc", s, NT - 1)], pv[("kc", s, NT - 1)]])
            m7 = kb.op(Sc, lambda e: e.copy(kmrow[0:1, 0:H], psA[0:1, 0:H]),
                       waits=[m6])
            m8 = kb.op(T, lambda e: e.matmul(psA[:, 512:512 + H], onesr,
                                             kmrow[0:1, 0:H],
                                             start=True, stop=True), waits=[m7])
            m9 = kb.op(Sc, lambda e: e.copy(kmb[:, 0:H], psA[:, 512:512 + H]),
                       waits=[m8])
            m10 = kb.op(V, lambda e: e.tensor_tensor(
                mbb.rearrange("p (t h) -> p t h", h=H),
                qn2.rearrange("p (t h) -> p t h", h=H),
                kmb[:, 0:H].unsqueeze(1).broadcast_to([128, NT, H]), ALU.mult),
                waits=[m9] + [pv[("qn2r", s, t)] for t in range(NT)])
            pv[("nrm_r",)] = m10
            ma = kb.op(V, lambda e: e.tensor_scalar_mul(
                mxbf[:, 0:NT * H], mbb, 1.02), waits=[m10])
            mc = None
            for c2 in range(NC2):
                w2c = min(128, NT * H - c2 * 128)
                mt1 = kb.op(T, lambda e, c2=c2, w2c=w2c: e.matmul(
                    psA[0:w2c, 0:128], mxbf[:, c2 * 128:c2 * 128 + w2c], ident,
                    start=True, stop=True), waits=[ma, m9, mc])
                mc = kb.op(V, lambda e, c2=c2, w2c=w2c: e.reduce_max(
                    mxc[0:w2c, c2:c2 + 1], psA[0:w2c, 0:128], axis=AX.X),
                    waits=[mt1])
            md = kb.op(V, lambda e: e.tensor_scalar_mul(
                mxcb[:, 0:NC2], mxc[:, 0:NC2], 1.0), waits=[mc, t_mxz])
            mg = None
            for c2 in range(NC2):
                w2c = min(128, NT * H - c2 * 128)
                me = kb.op(T, lambda e, c2=c2: e.matmul(
                    psA[0:1, 128:256], mxcb[:, c2:c2 + 1], ident,
                    start=True, stop=True), waits=[md, mg])
                mf = kb.op(Sc, lambda e, c2=c2: e.copy(
                    mrows[0:1, c2 * 128:(c2 + 1) * 128], psA[0:1, 128:256]),
                    waits=[me])
                mg1 = kb.op(T, lambda e, c2=c2, w2c=w2c: e.matmul(
                    psA[:, 512 + c2 * 128:512 + c2 * 128 + w2c], onesr,
                    mrows[0:1, c2 * 128:c2 * 128 + w2c], start=True, stop=True),
                    waits=[mf, t_ones])
                mg = kb.op(Sc, lambda e, c2=c2, w2c=w2c: e.copy(
                    mbb[:, c2 * 128:c2 * 128 + w2c],
                    psA[:, 512 + c2 * 128:512 + c2 * 128 + w2c]), waits=[mg1])
            mbb2 = kn2[:, 0:NSUPQ * H]
            msu = kb.op(V, lambda e: e.reduce_max(
                mbb2.rearrange("p (T h) -> p T h", h=H),
                mbb.rearrange("p (T f h) -> p T h f", f=QS, h=H), axis=AX.X),
                waits=[mg] + [pv[("kn2r", s, t2)] for t2 in range(NT)])
            m11 = kb.op(Sc, lambda e: e.activation(mbb2, mbb2, AF.Ln, scale=1.0,
                                                   bias=epsb), waits=[msu, t_eps])
            m12 = kb.op(Sc, lambda e: e.activation(mbb2, mbb2, AF.Exp, scale=0.5),
                        waits=[m11])
            m13 = kb.op(V, lambda e: e.tensor_scalar_mul(mbb2, mbb2, -SCALE),
                        waits=[m12])

            psA_user = {}
            pt_user = {}
            aev_prev = {}
            scnt = [0]

            def make_woln(Ts):
                q0 = Ts * QS * 128
                clos = []
                for j in range(QS):
                    gt = Ts * QS + j
                    for piece in range(2):
                        def go(j=j, gt=gt, piece=piece, q0=q0):
                            g2 = j * 2 + piece
                            bank = psA[:, 1536:2048]
                            if piece == 0:
                                pv[("xrtok",)] = kb.dma(
                                    Sy, "xr", xres,
                                    xs[s, q0 + j * 128:q0 + (j + 1) * 128, :],
                                    waits=[pv.get(("resid", s, gt - 1)),
                                           pv.get(("resid", s - 1, NT - 1))
                                           if gt == 0 else None])
                            ww = [pv[("aopair", NPAIR - 1)],
                                  pv.get(("tyg",))]
                            womm = None
                            for hp in range(NPAIR):
                                womm = kb.op(T, lambda e, hp=hp, j=j, piece=piece,
                                             bank=bank: e.matmul(
                                                 bank,
                                                 attn_sup[:, hp * 512 + j * 128:
                                                          hp * 512 + (j + 1) * 128],
                                                 wo_b[:, hp * 1024 + piece * 512:
                                                      hp * 1024 + (piece + 1) * 512],
                                                 start=(hp == 0),
                                                 stop=(hp == NPAIR - 1)),
                                             waits=ww if hp == 0 else ())
                            if j == QS - 1 and piece == 1:
                                pv[("wo_last",)] = womm
                            t_yp = kb.op(V, lambda e, piece=piece, bank=bank:
                                         e.tensor_add(
                                             ybuf[:, piece * 512:(piece + 1) * 512],
                                             bank,
                                             xres[:, piece * 512:(piece + 1) * 512]),
                                         waits=[womm, pv[("xrtok",)],
                                                ybuf_r[0] if piece == 0 else None,
                                                pv.get(("x2dma", s, gt - 1))
                                                if piece == 0 else None])
                            pv[("tyg",)] = t_yp
                            if piece == 0:
                                return
                            t_y = t_yp
                            pv[("resid", s, gt)] = t_y
                            t_ns = kb.op(V, lambda e: e.reduce_sum(
                                stat[:, ST_NSUM:ST_NSUM + 1], ybuf, axis=AX.X,
                                negate=True), waits=[t_y])
                            t_nm = kb.op(V, lambda e: e.tensor_scalar_mul(
                                stat[:, ST_NM:ST_NM + 1],
                                stat[:, ST_NSUM:ST_NSUM + 1], 1.0 / D),
                                waits=[t_ns])
                            t_sq = kb.op(V, lambda e: e.scalar_tensor_tensor(
                                sqbuf, ybuf, stat[:, ST_NM:ST_NM + 1], ybuf,
                                ALU.add, ALU.mult,
                                accum_out=stat[:, ST_SSQ:ST_SSQ + 1]),
                                waits=[t_nm, sqbuf_r[0]])
                            t_ln = kb.op(Sc, lambda e: e.activation(
                                stat[:, ST_LNV:ST_LNV + 1],
                                stat[:, ST_SSQ:ST_SSQ + 1],
                                AF.Ln, scale=1.0 / D, bias=epsb),
                                waits=[t_sq, t_eps])
                            t_rs = kb.op(Sc, lambda e: e.activation(
                                stat[:, ST_RSTD:ST_RSTD + 1],
                                stat[:, ST_LNV:ST_LNV + 1],
                                AF.Exp, scale=-0.5), waits=[t_ln])
                            t_x2 = kb.op(V, lambda e: e.tensor_scalar(
                                sqbuf, ybuf, stat[:, ST_NM:ST_NM + 1],
                                stat[:, ST_RSTD:ST_RSTD + 1], ALU.add, ALU.mult),
                                waits=[t_rs])
                            ybuf_r[0] = t_x2
                            t_st = kb.dma(
                                Sy, "st",
                                x2s[(s * NT + gt) * 128:(s * NT + gt + 1) * 128, :],
                                sqbuf, waits=[t_x2])
                            sqbuf_r[0] = t_st
                            pv[("x2dma", s, gt)] = t_st
                        clos.append(go)
                return clos

            queue = []
            kstep = [0]
            for Ts in range(NSUPQ):
                q0 = Ts * QS * 128
                qw = QS * 128
                nk = Ts * QS + QS
                for hp in range(NPAIR):
                    h = 2 * hp + 1

                    gp = Ts * NPAIR + hp
                    pvps = psE if gp % 2 == 0 else psB

                    def emit_pv(eh, k, ex_tok):
                        band_j = k - Ts * QS
                        coff = max(0, band_j) * 128
                        hh = 2 * hp + eh
                        ww2 = [ex_tok]
                        if k == 0:
                            ww2.append(aev_prev.get((gp % 2, eh)))
                        pvm = kb.op(T, lambda e, k=k, hh=hh, eh=eh,
                                    coff=coff, qw=qw, nk=nk,
                                    pvps=pvps: e.matmul(
                                        pvps[0:65, eh * 512 + coff:eh * 512 + qw],
                                        v_tm4[:, k, hh, :],
                                        pt_sb2[k % 2][:, eh * 512 + coff:
                                                       eh * 512 + qw],
                                        start=(k == 0), stop=(k == nk - 1)),
                                    waits=ww2)
                        pt_user[(k % 2, eh)] = pvm
                        return pvm

                    prev_ex = [None, None]
                    for k in range(nk):
                        sa = scnt[0] % 3
                        sb = (scnt[0] + 1) % 3
                        scnt[0] += 2
                        kstep[0] += 2
                        for _ in range(2):
                            if queue and queue[0][0] <= kstep[0]:
                                queue.pop(0)[1]()
                            else:
                                break
                        band_j = k - Ts * QS
                        coff = max(0, band_j) * 128
                        stmm = {}
                        for eh, sl in ((0, sa), (1, sb)):
                            ww = [psA_user.get(sl)]
                            if Ts == 0 and hp == 0 and k == 0 and eh == 0:
                                ww += p1_end + [m13]
                            stmm[eh] = kb.op(
                                T, lambda e, k=k, hp=hp, e64=eh * 64, sl=sl,
                                coff=coff, qw=qw: e.matmul(
                                    psA[:, sl * 512 + coff:sl * 512 + qw],
                                    k_fm[e64:e64 + 64,
                                         hp * S + k * 128:hp * S + (k + 1) * 128],
                                    q_fm[e64:e64 + 64,
                                         hp * S + q0 + coff:hp * S + q0 + qw],
                                    start=True, stop=(band_j < 0)),
                                waits=ww)
                        if band_j >= 0:
                            for eh, sl in ((0, sa), (1, sb)):
                                stmm[eh] = kb.op(
                                    T, lambda e, sl=sl, coff=coff: e.matmul(
                                        psA[:, sl * 512 + coff:
                                            sl * 512 + coff + 128],
                                        ident, maskt, start=False, stop=True),
                                    waits=[t_maskt] if (Ts == 0 and hp == 0
                                                        and eh == 0) else ())
                        if k > 0:
                            emit_pv(0, k - 1, prev_ex[0])
                            emit_pv(1, k - 1, prev_ex[1])
                        for eh, sl in ((0, sa), (1, sb)):
                            hh = 2 * hp + eh
                            ex = kb.op(Sc, lambda e, sl=sl, eh=eh, coff=coff,
                                       k=k, hh=hh, qw=qw: e.activation(
                                           pt_sb2[k % 2][:, eh * 512 + coff:
                                                          eh * 512 + qw],
                                           psA[:, sl * 512 + coff:sl * 512 + qw],
                                           AF.Exp,
                                           bias=mbb2[:, Ts * H + hh:
                                                     Ts * H + hh + 1],
                                           scale=SCALE),
                                       waits=[stmm[eh],
                                              pt_user.get((k % 2, eh))])
                            psA_user[sl] = ex
                            prev_ex[eh] = ex
                    pvm0 = emit_pv(0, nk - 1, prev_ex[0])
                    pvm1 = emit_pv(1, nk - 1, prev_ex[1])
                    if True:

                        def mk_rfn(e2, pvm_e, gp=gp, qw=qw, pvps=pvps):
                            def rfn():
                                lrv = lr_all[:, e2 * 512:e2 * 512 + qw]
                                r1 = kb.op(Sc, lambda e, lrv=lrv, e2=e2,
                                           pvps=pvps, qw=qw: e.activation(
                                    lrv, pvps[64:65, e2 * 512:e2 * 512 + qw],
                                    AF.Ln, scale=2.0 ** 40, bias=tinyb[0:1, :]),
                                    waits=[pvm_e, t_tiny,
                                           pv.get(("lr_r",))
                                           if e2 == 0 else None])
                                r2 = kb.op(Sc, lambda e, lrv=lrv: e.activation(
                                    lrv, lrv, AF.Exp, scale=-1.0), waits=[r1])
                                if e2 == 1:
                                    pv[("lr_r",)] = r2
                                pv[("r2", gp, e2)] = r2
                            return rfn

                        def norm_tail(hp=hp, h=h, gp=gp, qw=qw, pvps=pvps):
                            r2 = [pv[("r2", gp, 0)], pv[("r2", gp, 1)]]
                            lrv2 = lr_all.rearrange(
                                "p (e c) -> p e c", c=512)[:, :, 0:qw]
                            rhv = rhi_all.rearrange(
                                "p (e c) -> p e c", c=512)[:, :, 0:qw]
                            rlv = rlo_all.rearrange(
                                "p (e c) -> p e c", c=512)[:, :, 0:qw]
                            rc2 = kb.op(V, lambda e, lrv2=lrv2, rhv=rhv:
                                        e.tensor_copy(rhv, lrv2),
                                        waits=[r2, pv.get(("rbs_r",))])
                            rc3 = kb.op(V, lambda e, lrv2=lrv2, rhv=rhv, rlv=rlv:
                                        e.tensor_tensor(rlv, lrv2, rhv,
                                                        ALU.subtract),
                                        waits=[rc2])
                            pv[("lr_r",)] = rc3
                            rbm = None
                            for eh2 in range(2):
                                for rrow, st_ in ((rhi_all, True), (rlo_all, False)):
                                    rbm = kb.op(T, lambda e, rrow=rrow, st_=st_,
                                                qw=qw, eh2=eh2, pvps=pvps: e.matmul(
                                                    pvps[64:128, eh2 * 512:
                                                         eh2 * 512 + qw],
                                                    onesr[0:1, 0:64],
                                                    rrow[0:1, eh2 * 512:
                                                         eh2 * 512 + qw],
                                                    start=st_, stop=not st_,
                                                    tile_position=(0, 64)),
                                                waits=[rc3, t_ones]
                                                if (st_ and eh2 == 0) else ())
                            rbev = kb.op(V, lambda e, qw=qw, pvps=pvps:
                                         e.tensor_copy(
                                rbs_all.rearrange("p (e c) -> p e c", c=512)[:, :, 0:qw],
                                pvps[64:128, 0:1024].rearrange(
                                    "p (e c) -> p e c", c=512)[:, :, 0:qw]),
                                waits=[rbm, pv.get(("rbs_r",))])
                            aevE = kb.op(V, lambda e, hp=hp, qw=qw, pvps=pvps:
                                         e.scalar_tensor_tensor(
                                             attn_sup[0:64, hp * 512:hp * 512 + qw],
                                             pvps[0:64, 0:qw], 2.0 ** 40,
                                             rbs_all[0:64, 0:qw],
                                             ALU.mult, ALU.mult),
                                         waits=[rbev] +
                                         ([pv.get(("wo_last",))] if hp == 0 else []))
                            aev_prev[(gp % 2, 0)] = aevE
                            aevO = kb.op(V, lambda e, hp=hp, qw=qw, pvps=pvps:
                                         e.scalar_tensor_tensor(
                                             attnO[0:64, 0:qw],
                                             pvps[0:64, 512:512 + qw], 2.0 ** 40,
                                             rbs_all[0:64, 512:512 + qw],
                                             ALU.mult, ALU.mult),
                                         waits=[rbev, pv.get(("aotok",))])
                            pv[("aevd",)] = aevO
                            pv[("rbs_r",)] = aevO
                            t_ao = kb.dma(Sy, "ao",
                                          attn_sup[64:128, hp * 512:hp * 512 + qw],
                                          attnO[0:64, 0:qw],
                                          waits=[aevO, pv.get(("wo_last",))])
                            aev_prev[(gp % 2, 1)] = aevO
                            pv[("aotok",)] = t_ao
                            pv[("aopair", hp)] = (aevE, t_ao)
                        queue.append((kstep[0] + 1, mk_rfn(0, pvm0)))
                        queue.append((kstep[0] + 2, mk_rfn(1, pvm1)))
                        queue.append((kstep[0] + (5 if nk <= 8 else 7),
                                      norm_tail))
                queue += [(kstep[0], fn) for fn in make_woln(Ts)]
            for _, go in queue:
                go()

        p2_end = [pv[("x2dma", s, NT - 1)] for s in range(NSEQ)]

        w1_slots = [RW[:, k * 2048:(k + 1) * 2048].bitcast(F32)
                    for k in range(6)]
        W1SEM = ("b0", "b1", "c0", "c1", "c2", "c3")
        bz_rd = {}
        bz_ev = {}
        for i in range(W1T):
            wt = w1_slots[i % 6]
            ww = [bz_rd.get(i - 6)]
            if i < 6:
                ww.append(pv[("qkv", NSEQ - 1, NT - 1)])
            t_l = kb.dma(G, W1SEM[i % 6], wt,
                         w1[i * 128:(i + 1) * 128, :], waits=ww)
            t_a = kb.op(V, lambda e, wt=wt: e.reduce_sum(
                stat[:, ST_NSUM:ST_NSUM + 1], wt, axis=AX.X,
                apply_absolute_value=True),
                waits=[t_l] + (p2_end if i == 0 else []))
            t_a = kb.op(V, lambda e: e.tensor_scalar_mul(
                stat[:, ST_NM:ST_NM + 1], stat[:, ST_NSUM:ST_NSUM + 1], 1.0 / D),
                waits=[t_a])
            sgs = sgn[:, (i % 2) * 1024:(i % 2) * 1024 + 1024]
            t_s = kb.op(Sc, lambda e, sgs=sgs, wt=wt: e.sign(sgs, wt),
                        waits=[t_l, pv.get(("bztr", i - 2))] +
                        (p2_end if i < 2 else []))
            t_b = kb.op(V, lambda e, sgs=sgs: e.tensor_scalar_mul(
                sgs, sgs, stat[:, ST_NM:ST_NM + 1]),
                waits=[t_s, t_a])
            pv[("bzb", i)] = t_b
            bz_rd[i] = [t_a, t_s]

            def bz_tr(ii):
                sg2 = sgn[:, (ii % 2) * 1024:(ii % 2) * 1024 + 1024]
                ps = (psB, psE)[ii % 2]
                trm = None
                for dc in range(NDC):
                    trm = kb.op(T, lambda e, dc=dc, ps=ps, sg2=sg2: e.matmul(
                        ps[:, dc * 128:(dc + 1) * 128],
                        sg2[:, dc * 128:(dc + 1) * 128], ident,
                        start=True, stop=True),
                        waits=([pv[("bzb", ii)], bz_ev.get(ii - 2)] +
                               (p2_end if ii == 0 else [])) if dc == 0 else ())
                pv[("bztr", ii)] = trm
                ev = kb.op(Sc, lambda e, ps=ps, ii=ii: e.copy(
                    bw1T.rearrange("p (dc f) -> p dc f", f=FF)[:, :, ii * 128:(ii + 1) * 128],
                    ps.rearrange("p (dc q) -> p dc q", q=128)), waits=[trm])
                bz_ev[ii] = ev
            if i > 0:
                bz_tr(i - 1)
        bz_tr(W1T - 1)
        bw1_done = [bz_ev[W1T - 1], bz_ev[W1T - 2]]
        w2_slots = [RW[:, k * 8192:(k + 1) * 8192].bitcast(F32)
                    for k in range(4)]
        W2SEM = ("b2", "w0", "w1", "w2")
        bz2_rd = {}
        bw2_wtoks = []
        for i in range(W2T):
            wt2 = w2_slots[i % 4]
            ww = [bz2_rd.get(i - 4)]
            if i < 4:
                ww += [bz_rd.get(W1T - 1 - k) for k in range(6)]
            t_l = kb.dma(G, W2SEM[i % 4], wt2, w2[i * 128:(i + 1) * 128, :],
                         waits=ww)
            t_a = kb.op(V, lambda e, wt2=wt2: e.reduce_sum(
                stat[:, ST_NSUM:ST_NSUM + 1], wt2, axis=AX.X,
                apply_absolute_value=True), waits=[t_l])
            t_a = kb.op(V, lambda e: e.tensor_scalar_mul(
                stat[:, ST_NM:ST_NM + 1], stat[:, ST_NSUM:ST_NSUM + 1], 1.0 / FF),
                waits=[t_a])
            t_s = kb.op(Sc, lambda e, wt2=wt2: e.sign(sgn, wt2),
                        waits=[t_l, pv.get(("bz2tr", i - 1)),
                               pv.get(("bztr", W1T - 1)) if i == 0 else None] +
                        (bw1_done if i == 0 else []))
            t_b = kb.op(V, lambda e: e.tensor_scalar_mul(
                sgn, sgn, stat[:, ST_NM:ST_NM + 1]), waits=[t_s, t_a])
            trm = None
            NG2 = NFC // 8
            for g in range(NG2):
                ps = (psB, psE)[g % 2]
                for c8 in range(8):
                    c = g * 8 + c8
                    trm = kb.op(T, lambda e, c=c, c8=c8, ps=ps: e.matmul(
                        ps[:, c8 * 128:(c8 + 1) * 128],
                        sgn[:, c * 128:(c + 1) * 128], ident,
                        start=True, stop=True),
                        waits=([t_b, pv.get(("bz2d", i, g - 2)),
                                pv.get(("bz2d", i - 1, NG2 - 2 + g))
                                if g < 2 else None,
                                bz_ev.get(W1T - 1) if (i == 0 and g == 0) else None,
                                bz_ev.get(W1T - 2) if (i == 0 and g == 1) else None]
                               ) if c8 == 0 else ())
                sgo = (sgn_s, sgn_s2)[g % 2]
                evs = kb.op(Sc, lambda e, ps=ps, sgo=sgo: e.copy(sgo, ps),
                            waits=[trm, pv.get(("bz2d", i, g - 2)),
                                   pv.get(("bz2d", i - 1, NG2 - 2 + g))
                                   if g < 2 else None])
                t_d = kb.dma(G, ("bw", "b0", "b1", "c0")[g % 4],
                             bw2s.rearrange("(a p) o -> p a o", p=128)
                             [:, g * 8:(g + 1) * 8, i * 128:(i + 1) * 128],
                             sgo.rearrange("p (a o) -> p a o", o=128),
                             waits=[evs])
                pv[("bz2d", i, g)] = t_d
                bw2_wtoks.append(t_d)
            pv[("bz2tr", i)] = trm
            bz2_rd[i] = [t_a, t_s]

        x2_all = ("st", kb.n["st"])
        pf = {}
        pf[("ybr", 0)] = ybuf_r[0]
        pf[("ybr", 1)] = pv.get(("resid", NSEQ - 1, NT - 1))
        sbst = (RB[:, 5120:7168].bitcast(F32), sqbuf)
        pf[("sq2r", 1)] = sqbuf_r[0]
        ybb = (ybuf, xres,
               RQ[:, 32768:34816].bitcast(F32), RQ[:, 34816:36864].bitcast(F32))
        xt4 = (xt_b[0], xt_b[1],
               RW[:, 20480:22528].bitcast(F32), RW[:, 22528:24576].bitcast(F32))
        XSEM = ("x0", "x1", "k4", "k5")
        xr4 = (xr2_b[0], xr2_b[1],
               RQ[:, 36864:38912].bitcast(F32), RQ[:, 38912:40960].bitcast(F32))
        XRSEM = ("tb", "rb", "k6", "k7")
        y_toks = []

        def emit_prep(u):
            for j in range(NSUPT):
                gt = u * NSUPT + j
                xt = xt4[j]
                t_x = pf.get(("xld", u, j))
                if t_x is None:
                    ww = [x2_all, pf.get(("x2c", u - 1, j))]
                    if u == 0 and j < 2:
                        ww += [pv.get(("xcast", NSEQ - 1, NT - 2 + j))]
                    if u == 0 and j >= 2:
                        ww += [pv.get(("bz2tr", W2T - 1)),
                               t_b2 if j == 2 else t_eps]
                    t_x = kb.dma(Sy, XSEM[j], xt,
                                 x2s[gt * 128:(gt + 1) * 128, :], waits=ww)
                t_c = pf.get(("x2c", u, j))
                if t_c is None:
                    ww = [t_x, pf.get(("x2tr", u, j - 1))]
                    if u == 0 and j == 0:
                        ww.append(pv.get(("bz2tr", W2T - 1)))
                    t_c = kb.op(Sc, lambda e, xt=xt: e.copy(x2bf, xt), waits=ww)
                    pf[("x2c", u, j)] = t_c
                ww = [t_c, pf.get(("fevA", u - 1, 0)), pf.get(("fevA", u - 1, 1)),
                      pf.get(("xf_ev", u, j - 1))]
                trm = None
                for dc in range(NDC):
                    trm = kb.op(T, lambda e, dc=dc: e.matmul(
                        psA[:, dc * 128:(dc + 1) * 128],
                        x2bf[:, dc * 128:(dc + 1) * 128], ident,
                        start=True, stop=True), waits=ww if dc == 0 else ())
                pf[("x2tr", u, j)] = trm
                ev = kb.op(Sc, lambda e, j=j: e.copy(
                    x2_fm.rearrange("p (dc q) -> p dc q", q=SUP)[:, :, j * 128:(j + 1) * 128],
                    psA[:, 0:1024].rearrange("p (dc q) -> p dc q", q=128)),
                    waits=[trm, pf.get(("ffn1mm", u - 1))])
                pf[("xf_ev", u, j)] = ev

        emit_prep(0)
        for u in range(NSUPER):
            g0 = u * NSUPT
            xf_done = [pf[("xf_ev", u, j)] for j in range(NSUPT)]
            ge = {}
            f1mm = None
            for fc in range(NFC):
                ps = (psB, psE)[fc % 2]
                mm = None
                for dc in range(NDC):
                    for q0 in range(0, SUP, 512):
                        q1 = min(SUP, q0 + 512)
                        ww = ()
                        if dc == 0 and q0 == 0:
                            ww = xf_done + [ge.get(fc - 2)]
                            if u == 0 and fc < 2:
                                ww = ww + [bz_ev.get(W1T - 1), bw2_wtoks[-1],
                                           bw2_wtoks[-2]]
                            else:
                                ww = ww + [pf.get(("fevB", u - 1, 2 + (fc % 2)))]
                        mm = kb.op(T, lambda e, fc=fc, dc=dc, ps=ps, q0=q0, q1=q1:
                                   e.matmul(ps[:, q0:q1],
                                            bw1T[:, dc * FF + fc * 128:
                                                 dc * FF + (fc + 1) * 128],
                                            x2_fm[:, dc * SUP + q0:dc * SUP + q1],
                                            start=(dc == 0), stop=(dc == NDC - 1)),
                                   waits=ww)
                f1mm = mm
                ge[fc] = kb.op(Sc, lambda e, fc=fc, ps=ps: e.activation(
                    h_fm[:, fc * SUP:(fc + 1) * SUP], ps[:, 0:SUP], AF.Gelu,
                    bias=b1s[:, fc:fc + 1], scale=1.0), waits=[mm, t_b1])
            pf[("ffn1mm", u)] = f1mm
            gelu_done = [ge[fc] for fc in range(NFC)]
            tile_ps = [psA[:, 0:1024], psA[:, 1024:2048], psB, psE]
            f2 = None
            for c in range(NFC):
                bwc = RW[:, 24576 + (c % 8) * 1024:24576 + (c % 8) * 1024 + 1024]
                ww = [pf.get(("bwc", u, c - 8)) if c >= 8 else
                      pf.get(("bwc", u - 1, NFC - 8 + c))]
                if u == 0 and c < 8:
                    ww += bw2_wtoks[-4:]
                t_w2 = kb.dma(G, ("c0", "c1", "c2", "c3",
                                  "g0", "g1", "g2", "g3")[c % 8], bwc,
                              bw2s[c * 128:(c + 1) * 128, :], waits=ww)
                for j in range(NSUPT):
                    for half in range(2):
                        ww = ()
                        if half == 0:
                            ww = [t_w2, gelu_done[c]]
                            if c == 0:
                                ww += [pf.get(("fevA", u - 1, j)) if j < 2 else
                                       pf.get(("fevB", u - 1, j)),
                                       xf_done[-1], gelu_done[NFC - 1]]
                        f2 = kb.op(T, lambda e, c=c, j=j, half=half: e.matmul(
                            tile_ps[j][:, half * 512:(half + 1) * 512],
                            h_fm[:, c * SUP + j * 128:c * SUP + (j + 1) * 128],
                            bwc[:, half * 512:(half + 1) * 512],
                            start=(c == 0), stop=(c == NFC - 1)), waits=ww)
                pf[("bwc", u, c)] = f2
            if u + 1 < NSUPER:
                for j2 in range(NSUPT):
                    t_xp = kb.dma(Sy, XSEM[j2], xt4[j2],
                                  x2s[(g0 + NSUPT + j2) * 128:
                                      (g0 + NSUPT + j2 + 1) * 128, :],
                                  waits=[pf.get(("x2c", u, j2))])
                    pf[("xld", u + 1, j2)] = t_xp
                t_cp = kb.op(G, lambda e: e.tensor_copy(x2bf, xt_b[0]),
                             waits=[pf[("xld", u + 1, 0)],
                                    pf.get(("x2tr", u, NSUPT - 1))])
                pf[("x2c", u + 1, 0)] = t_cp

            for j in range(NSUPT):
                t_y = kb.op(V, lambda e, j=j, yb=ybb[j]: e.tensor_add(
                    yb, tile_ps[j], b2s),
                    waits=[f2, t_b2, pf.get(("ybr", j))])
                key = ("fevA", u, j) if j < 2 else ("fevB", u, j)
                pf[key] = t_y
                pf[("tev", u, j)] = t_y
            if u + 1 < NSUPER:
                emit_prep(u + 1)
            for j in range(NSUPT):
                gt = g0 + j
                ww = [pf.get(("residrd", u - 1, j))]
                if u == 0 and j < 2:
                    ww += [pv.get(("bz2tr", W2T - 1)), t_tiny] + bw2_wtoks[-2:]
                if u == 0 and j >= 2:
                    ww += [x2_all, t_ones if j == 2 else t_maskt]
                t_xr = kb.dma(Sy, XRSEM[j], xr4[j],
                              x2s[gt * 128:(gt + 1) * 128, :], waits=ww)
                pf[("xrt", j)] = t_xr
            for j in range(NSUPT):
                gt = g0 + j
                sq_i, tt = divmod(gt, NT)
                xr = xr4[j]
                yb = ybb[j]
                t_y = pf[("tev", u, j)]
                t_y2 = kb.op(V, lambda e, yb=yb, xr=xr: e.tensor_add(yb, yb, xr),
                             waits=[t_y, pf[("xrt", j)]])
                pf[("residrd", u, j)] = t_y2
                t_ns = kb.op(V, lambda e, yb=yb: e.reduce_sum(
                    stat[:, ST_NSUM:ST_NSUM + 1], yb, axis=AX.X, negate=True),
                    waits=[t_y2])
                t_nm = kb.op(V, lambda e: e.tensor_scalar_mul(
                    stat[:, ST_NM:ST_NM + 1], stat[:, ST_NSUM:ST_NSUM + 1], 1.0 / D),
                    waits=[t_ns])
                sb = sbst[j % 2]
                t_sq = kb.op(V, lambda e, yb=yb, sb=sb: e.scalar_tensor_tensor(
                    sb, yb, stat[:, ST_NM:ST_NM + 1], yb, ALU.add, ALU.mult,
                    accum_out=stat[:, ST_SSQ:ST_SSQ + 1]),
                    waits=[t_nm, pf.get(("sq2r", j % 2)), pf.get(("lnp",))])
                t_ln = kb.op(Sc, lambda e: e.activation(
                    stat[:, ST_LNV:ST_LNV + 1], stat[:, ST_SSQ:ST_SSQ + 1], AF.Ln,
                    scale=1.0 / D, bias=epsb),
                    waits=[t_sq, t_eps, pf.get(("top",))])
                pf[("lnp",)] = t_ln
                t_rs = kb.op(Sc, lambda e: e.activation(
                    stat[:, ST_RSTD:ST_RSTD + 1], stat[:, ST_LNV:ST_LNV + 1], AF.Exp,
                    scale=-0.5), waits=[t_ln])
                t_o = kb.op(V, lambda e, yb=yb, sb=sb: e.tensor_scalar(
                    sb, yb, stat[:, ST_NM:ST_NM + 1],
                    stat[:, ST_RSTD:ST_RSTD + 1], ALU.add, ALU.mult), waits=[t_rs])
                pf[("top",)] = t_o
                pf[("ybr", j)] = t_o
                t_st = kb.dma(Sy, ("st", "xr")[j % 2],
                              y[sq_i, tt * 128:(tt + 1) * 128, :], sb,
                              waits=[t_o])
                pf[("sq2r", j % 2)] = t_st
                y_toks.append(t_st)

        kb.wait_only(Sy, y_toks)
        kb.wait_only(Sy, [(sname, kb.n[sname]) for sname in KB.DMA_SEMS])

    with contextlib.ExitStack() as stack:
        kb.setup_sems(stack)
        block = stack.enter_context(nc.Block())
        for name in ("sync", "scalar", "vector", "tensor", "gpsimd"):
            def make(name=name):
                def f(eng):
                    kb.begin_pass(name, eng)
                    program()
                return f
            getattr(block, name)(make())

    return nc, sbuf_left


def host_prep(cfg, n_cores, L0_attn_x, L0_attn_Wq, L0_attn_Wk, L0_attn_Wv,
              L0_attn_Wo, L0_ffn_w1, L0_ffn_b1, L0_ffn_w2, L0_ffn_b2):
    NSEQ, S, D, H, DK, FF = (cfg[k] for k in ("NSEQ", "S", "D", "H", "DK", "FF"))
    NFC = FF // 128
    x = np.ascontiguousarray(np.asarray(L0_attn_x, dtype=np.float32))
    Wq = np.asarray(L0_attn_Wq, np.float32)
    Wk = np.asarray(L0_attn_Wk, np.float32)
    Wv = np.asarray(L0_attn_Wv, np.float32)
    Wo = np.asarray(L0_attn_Wo, np.float32)
    w1 = np.ascontiguousarray(np.asarray(L0_ffn_w1, np.float32))
    w2 = np.ascontiguousarray(np.asarray(L0_ffn_w2, np.float32))
    b1 = np.asarray(L0_ffn_b1, np.float32)
    b2 = np.asarray(L0_ffn_b2, np.float32)

    perm = np.concatenate([np.concatenate([h * DK + np.arange(0, DK, 2),
                                           h * DK + np.arange(1, DK, 2)])
                           for h in range(H)])
    shared = dict(
        wqT=np.ascontiguousarray(Wq[perm].T),
        wkT=np.ascontiguousarray(Wk[perm].T),
        wvT=np.ascontiguousarray(Wv.T),
        woT=np.ascontiguousarray(Wo.T),
        w1=w1, w2=w2,
        b1r=np.ascontiguousarray(b1.reshape(NFC, 128).T),
        b2b=np.ascontiguousarray(np.broadcast_to(b2, (128, D))),
        identd=np.eye(128, dtype=ml_dtypes.bfloat16),
        epsd=np.full((128, 1), LN_EPS, dtype=np.float32),
        tinyd=np.full((128, 1), 1e-30, dtype=np.float32),
        onesd=np.ones((1, 128), dtype=ml_dtypes.bfloat16),
        masktd=np.where(np.arange(128)[:, None] <= np.arange(128)[None, :],
                        0.0, NEG_BIG).astype(ml_dtypes.bfloat16),
    )
    pos = np.arange(S, dtype=np.float64)
    inv = cfg["THETA"] ** (-np.arange(0, DK, 2, dtype=np.float64) / DK)
    ang = pos[:, None] * inv[None, :]
    shared["roped"] = np.ascontiguousarray(
        np.concatenate([np.cos(ang), np.sin(ang)], axis=1).astype(ml_dtypes.bfloat16))
    B = x.shape[0]
    per = B // n_cores
    assert per == NSEQ
    return [dict(shared, xs=np.ascontiguousarray(x[c * per:(c + 1) * per]))
            for c in range(n_cores)]



def kernel(**inputs):
    from concourse.bass_utils import run_bass_kernel_spmd
    cfg = cfg_full()
    nc, _ = build_program(cfg)
    in_maps = host_prep(cfg, 8, **inputs)
    res = run_bass_kernel_spmd(nc, in_maps, core_ids=list(range(8)))
    return np.concatenate([r["y"] for r in res.results], axis=0)


# version 22
# speedup vs baseline: 1.0804x; 1.0804x over previous
import contextlib
import math
import numpy as np
import ml_dtypes
import concourse.bass as bass
import concourse.mybir as mybir

F32 = mybir.dt.float32
BF16 = mybir.dt.bfloat16
AF = mybir.ActivationFunctionType
ALU = mybir.AluOpType
AX = mybir.AxisListType

LN_EPS = 1e-5
NEG_BIG = -1.0e30


def cfg_full():
    return dict(NSEQ=2, S=2048, D=1024, H=16, DK=64, FF=4096, SUP=512,
                THETA=10000.0)


class KB:
    def __init__(self, nc):
        self.nc = nc
        self.cur = None
        self.sems = {}
        self.n = {}
        self.last_wait = {}
        self.eng = {}

    DMA_SEMS = ("k0", "k1", "k2", "k3", "k4", "k5", "k6", "k7", "ao",
                "w0", "w1", "w2", "w3",
                "x0", "x1", "xr", "st", "b0", "b1", "b2", "bw",
                "c0", "c1", "c2", "c3", "tb", "rb",
                "g0", "g1", "g2", "g3")

    def setup_sems(self, stack):
        for name in ("ct", "cv", "cs", "cg") + self.DMA_SEMS:
            self.sems[name] = stack.enter_context(self.nc.semaphore(name))

    def begin_pass(self, engname, engobj):
        self.cur = engname
        self.eng[engname] = engobj
        self.n = {k: 0 for k in self.sems}
        self.last_wait = {k: 0 for k in self.sems}

    def _waits(self, waits):
        for tok in waits:
            if tok is None:
                continue
            if isinstance(tok, (list, tuple)) and tok and \
                    isinstance(tok[0], (list, tuple)):
                self._waits(tok)
                continue
            sem_name, val = tok
            if val > self.last_wait[sem_name]:
                self.eng[self.cur].wait_ge(self.sems[sem_name], val)
                self.last_wait[sem_name] = val

    def op(self, engname, emit, waits=()):
        sem = {"tensor": "ct", "vector": "cv", "scalar": "cs",
               "gpsimd": "cg"}[engname]
        self.n[sem] += 1
        tok = (sem, self.n[sem])
        if self.cur == engname:
            self._waits(waits)
            emit(self.eng[engname]).then_inc(self.sems[sem], 1)
        return tok

    def dma(self, engname, sem, out, in_, waits=()):
        self.n[sem] += 16
        tok = (sem, self.n[sem])
        if self.cur == engname:
            self._waits(waits)
            self.eng[engname].dma_start(out=out, in_=in_).then_inc(self.sems[sem], 16)
        return tok

    def wait_only(self, engname, waits):
        if self.cur == engname:
            self._waits(waits)


def build_program(cfg):
    NSEQ, S, D, H, DK, FF, SUP = (cfg[k] for k in
                                  ("NSEQ", "S", "D", "H", "DK", "FF", "SUP"))
    NT = S // 128
    NDC = D // 128
    NFC = FF // 128
    NPAIR = H // 2
    NSUPT = SUP // 128
    NSUPER = (NSEQ * S) // SUP
    W1T = FF // 128
    W2T = D // 128
    QS = 4 if NT % 4 == 0 else NT
    NSUPQ = NT // QS
    SCALE = 1.0 / math.sqrt(DK)

    nc = bass.Bass("TRN2", target_bir_lowering=False, debug=False)

    xs = nc.dram_tensor("xs", [NSEQ, S, D], F32, kind="ExternalInput").ap()
    wqT = nc.dram_tensor("wqT", [D, D], F32, kind="ExternalInput").ap()
    wkT = nc.dram_tensor("wkT", [D, D], F32, kind="ExternalInput").ap()
    wvT = nc.dram_tensor("wvT", [D, D], F32, kind="ExternalInput").ap()
    woT = nc.dram_tensor("woT", [D, D], F32, kind="ExternalInput").ap()
    w1 = nc.dram_tensor("w1", [FF, D], F32, kind="ExternalInput").ap()
    w2 = nc.dram_tensor("w2", [D, FF], F32, kind="ExternalInput").ap()
    b1r = nc.dram_tensor("b1r", [128, NFC], F32, kind="ExternalInput").ap()
    b2b = nc.dram_tensor("b2b", [128, D], F32, kind="ExternalInput").ap()
    identd = nc.dram_tensor("identd", [128, 128], BF16, kind="ExternalInput").ap()
    masktd = nc.dram_tensor("masktd", [128, 128], BF16, kind="ExternalInput").ap()
    roped = nc.dram_tensor("roped", [S, 64], BF16, kind="ExternalInput").ap()
    onesd = nc.dram_tensor("onesd", [1, 128], BF16, kind="ExternalInput").ap()
    epsd = nc.dram_tensor("epsd", [128, 1], F32, kind="ExternalInput").ap()
    tinyd = nc.dram_tensor("tinyd", [128, 1], F32, kind="ExternalInput").ap()
    y = nc.dram_tensor("y", [NSEQ, S, D], F32, kind="ExternalOutput").ap()
    x2s = nc.dram_tensor("x2s", [NSEQ * S, D], F32, kind="Internal").ap()
    bw2s = nc.dram_tensor("bw2s", [FF, D], BF16, kind="Internal").ap()

    A = nc.alloc_sbuf_tensor
    ident = A("ident", [128, 128], BF16).ap()
    maskt = A("maskt", [128, 128], BF16).ap()
    rope = A("rope", [128, NT * 64], BF16).ap()
    b1s = A("b1s", [128, NFC], F32).ap()
    b2s = A("b2s", [128, D], F32).ap()
    stat = A("stat", [128, 16], F32).ap()
    epsb = A("epsb", [128, 1], F32).ap()
    tinyb = A("tinyb", [128, 1], F32).ap()
    onesr = A("onesr", [1, 128], BF16).ap()
    qn2 = A("qn2", [128, NT * H], F32).ap()
    kn2 = A("kn2", [128, NT * H], F32).ap()
    mbb = A("mbb", [128, NT * H], F32).ap()
    km = A("km", [128, 16], F32).ap()
    km_bf = A("km_bf", [128, 16], BF16).ap()
    km2t = A("km2t", [128, 1], F32).ap()
    km2t_bf = A("km2t_bf", [128, 1], BF16).ap()
    kmrow = A("kmrow", [1, 16], BF16).ap()
    kmb = A("kmb", [128, 16], F32).ap()
    NC2 = (NT * H + 127) // 128
    mxc = A("mxc", [128, 4], F32).ap()
    mxcb = A("mxcb", [128, 4], BF16).ap()
    mrows = A("mrows", [1, 256], BF16).ap()


    (ST_NSUM, ST_NM, ST_SSQ, ST_LNV, ST_RSTD, ST_M, ST_MB, ST_R, ST_MP) = range(9)

    RQ = A("RQ", [128, 49664], BF16).ap()
    q_fm = RQ[:, 0:NPAIR * S]
    k_fm = RQ[:, 16384:16384 + NPAIR * S]
    v_tm = RQ[:, 32768:32768 + NT * H * 65]
    v_tm4 = v_tm.rearrange("p (t h c) -> p t h c", h=H, c=65)
    bw1T = RQ[:, 0:NDC * FF]
    RW = A("RW", [128, 32768], BF16).ap()
    wq_b = RW[:, 0:8192]
    wk_b = RW[:, 8192:16384]
    wv_b = RW[:, 16384:24576]
    wo_b = RW[:, 24576:32768]
    h_fm = RW[:, 0:NFC * SUP]
    x2_fm = RW[:, 16384:16384 + NDC * SUP]
    bw2c = RW[:, 24576:28672]
    RS = A("RS", [128, 5120], F32).ap()
    xt_b = (RS[:, 0:1024], RS[:, 1024:2048])
    p_f32 = RS[:, 0:2048]
    xres = RS[:, 2048:3072]
    rbs_all = RS[0:64, 0:1024]
    lr_all = RS[0:1, 1024:2048]
    rhi_all = RS[0:1, 0:512].bitcast(BF16)
    rlo_all = RS[0:1, 512:1024].bitcast(BF16)
    sqbuf = RS[:, 3072:4096]
    ybuf = RS[:, 4096:5120]
    w2t_f = RS[:, 0:4096]
    RB = A("RB", [128, 7168], BF16).ap()
    xbf = RB[:, 0:1024]
    x_fm = RB[:, 1024:2048]
    qslots = (RB[:, 4096:5120], RB[:, 2048:3072])
    kslots = (RB[:, 5120:6144], RB[:, 3072:4096])
    attn_sup = RB[:, 0:4096]
    pt_sb2 = (RB[:, 4096:5120], RB[:, 5120:6144])
    attnO = RB[:, 6144:6656]
    sgn = RB[:, 0:4096]
    sgn_s = RB[:, 4096:5120]
    sgn_s2 = RB[:, 5120:6144]
    x2bf = RB[:, 0:1024]
    xr2_b = (RB[:, 1024:3072].bitcast(F32), RB[:, 3072:5120].bitcast(F32))
    tmpX = RB[:, 6144:7168]
    mxbf = RB[:, 6144:6144 + NT * H]

    psA = nc.alloc_psum_tensor("psA", [128, 2048], F32).ap()
    psB = nc.alloc_psum_tensor("psB", [128, 1024], F32).ap()
    psE = nc.alloc_psum_tensor("psE", [128, 1024], F32).ap()

    sbuf_left = nc.sbuf_bytes_remaining

    kb = KB(nc)
    T, V, Sc, Sy, G = "tensor", "vector", "scalar", "sync", "gpsimd"

    def program():
        pv = {}
        ybuf_r = [None]
        sqbuf_r = [None]

        t_id = kb.dma(Sy, "k0", ident, identd)
        t_maskt = kb.dma(Sy, "k7", maskt, masktd)
        t_rope = kb.dma(Sy, "k2", rope.rearrange("p (t c) -> p t c", c=64),
                        roped.rearrange("(t p) c -> p t c", p=128))
        t_b1 = kb.dma(Sy, "k3", b1s, b1r)
        t_b2 = kb.dma(Sy, "k4", b2s, b2b)
        t_eps = kb.dma(Sy, "k5", epsb, epsd)
        t_tiny = kb.dma(Sy, "tb", tinyb, tinyd)
        t_ones = kb.dma(Sy, "k6", onesr, onesd)
        t_vones = kb.op(V, lambda e: e.memset(v_tm4[:, :, :, 64:65], 1.0))
        t_mxz = kb.op(V, lambda e: e.memset(mxc, 0.0))
        t_w = []
        for wsem, dst, src in (("w0", wq_b, wqT), ("w1", wk_b, wkT),
                               ("w2", wv_b, wvT), ("w3", wo_b, woT)):
            t_w.append(kb.dma(G, wsem, dst.rearrange("p (c o) -> p c o", o=D),
                              src.rearrange("(c p) o -> p c o", p=128)))

        for s in range(NSEQ):
            def emit_xload(t):
                ww = [pv.get(("xcast", s, t - 2))]
                if s > 0 and t < 2:
                    ww.append(pv.get(("x2dma", s - 1, NT - 1)))
                t_x = kb.dma(Sy, ("x0", "x1")[t % 2], xt_b[t % 2],
                             xs[s, t * 128:(t + 1) * 128, :], waits=ww)
                ww = [t_x, pv.get(("xtr", s, t - 1))]
                if s > 0 and t == 0:
                    ww.append(pv.get(("x2dma", s - 1, NT - 1)))
                t_cast = kb.op(G, lambda e, t=t: e.tensor_copy(
                    xbf, xt_b[t % 2]), waits=ww)
                pv[("xcast", s, t)] = t_cast

            def emit_xtr(t):
                ww = [pv[("xcast", s, t)], t_id, pv.get(("kc", s, t - 1))]
                tr = None
                for dc in range(NDC):
                    tr = kb.op(T, lambda e, dc=dc: e.matmul(
                        psA[:, 1024 + dc * 128:1024 + (dc + 1) * 128],
                        xbf[:, dc * 128:(dc + 1) * 128], ident,
                        start=True, stop=True), waits=ww if dc == 0 else ())
                pv[("xtr", s, t)] = tr

            def emit_xfm(t):
                t_xfm = kb.op(Sc, lambda e: e.copy(x_fm, psA[:, 1024:2048]),
                              waits=[pv[("xtr", s, t)],
                                     pv.get(("qkv", s, t - 1))])
                pv[("xfm", s, t)] = t_xfm

            for t in range(NT):
                if t == 0:
                    emit_xload(0)
                    emit_xtr(0)
                    emit_xfm(0)
                if t + 1 < NT:
                    emit_xload(t + 1)
                w_q = [pv[("xfm", s, t)], pv.get(("qc", s, t - 1))]
                if s == 0 and t == 0:
                    w_q = w_q + t_w
                w_k = [pv.get(("kc", s, t - 1))]
                w_v = [pv.get(("vev", s, t - 1)), pv.get(("kev", s, t - 2))]
                qkv_reg = []
                for wb, po, ww in ((wq_b, 0, w_q), (wk_b, 1024, w_k),
                                   (wv_b, None, w_v)):
                    mm = None
                    for half in range(2):
                        for dc in range(NDC):
                            def mmf(e, dc=dc, wb=wb, po=po, half=half):
                                pdst = psB if po is None else psA[:, po:po + 1024]
                                return e.matmul(
                                    pdst[:, half * 512:(half + 1) * 512],
                                    x_fm[:, dc * 128:(dc + 1) * 128],
                                    wb[:, dc * 1024 + half * 512:
                                       dc * 1024 + (half + 1) * 512],
                                    start=(dc == 0), stop=(dc == NDC - 1))
                            mm = kb.op(T, mmf,
                                       waits=ww if (dc == 0 and half == 0)
                                       else ())
                    qkv_reg.append(mm)
                mm = qkv_reg[2]
                pv[("qkv", s, t)] = mm

                qro = qslots[t % 2]
                kro = kslots[t % 2]
                t_qc = kb.op(Sc, lambda e, qro=qro: e.copy(qro, psA[:, 0:1024]),
                             waits=[qkv_reg[0], pv.get(("trq", s, t - 2)),
                                    pv.get(("qsq", s, t - 2))])
                pv[("qc", s, t)] = t_qc
                t_kc = kb.op(Sc, lambda e, kro=kro: e.copy(kro, psA[:, 1024:2048]),
                             waits=[qkv_reg[1], pv.get(("trk", s, t - 2)),
                                    pv.get(("ksq", s, t - 2))])
                pv[("kc", s, t)] = t_kc
                if t + 1 < NT:
                    emit_xtr(t + 1)
                t_vev = kb.op(Sc, lambda e: e.copy(
                    v_tm4[:, t, :, 0:64],
                    psB.rearrange("p (h c) -> p h c", c=64)),
                    waits=[mm, t_vones if (s == 0 and t == 0) else None])
                pv[("vev", s, t)] = t_vev
                if t + 1 < NT:
                    emit_xfm(t + 1)

                def emit_rope(buf, cast_tok):
                    b3 = buf.rearrange("p (h i) -> p h i", i=64)
                    cosb = rope[:, t * 64:t * 64 + 32].unsqueeze(1).broadcast_to(
                        [128, H, 32])
                    sinb = rope[:, t * 64 + 32:t * 64 + 64].unsqueeze(1).broadcast_to(
                        [128, H, 32])
                    lo, hi = b3[:, :, 0:32], b3[:, :, 32:64]
                    ta = tmpX[:, 0:512].rearrange("p (h i) -> p h i", i=32)
                    tb = tmpX[:, 512:1024].rearrange("p (h i) -> p h i", i=32)
                    a1 = kb.op(V, lambda e: e.tensor_tensor(ta, lo, cosb, ALU.mult),
                               waits=[cast_tok, t_rope, pv.get(("tmpX_r",))])
                    a2 = kb.op(V, lambda e: e.tensor_tensor(tb, lo, sinb, ALU.mult),
                               waits=[a1])
                    a3 = kb.op(V, lambda e: e.tensor_tensor(lo, hi, sinb, ALU.mult),
                               waits=[a2])
                    a4 = kb.op(V, lambda e: e.tensor_tensor(lo, ta, lo, ALU.subtract),
                               waits=[a3])
                    a5 = kb.op(V, lambda e: e.tensor_tensor(ta, hi, cosb, ALU.mult),
                               waits=[a4])
                    a6 = kb.op(V, lambda e: e.tensor_add(hi, ta, tb),
                               waits=[a5])
                    pv[("tmpX_r",)] = a6
                    return a6

                def emit_norm(dst_col, ro):
                    sq = kb.op(Sc, lambda e, ro=ro: e.activation(
                        tmpX, ro, AF.Square, bias=mxc[:, 3:4]),
                        waits=[pv.get(("tmpX_r",)), t_mxz])
                    pv[("tmpX_r",)] = sq
                    rd = kb.op(V, lambda e, dst_col=dst_col: e.reduce_sum(
                        dst_col, tmpX.rearrange("p (h i) -> p h i", i=64),
                        axis=AX.X), waits=[sq, pv.get(("nrm_r",))])
                    pv[("tmpX_r",)] = rd
                    return sq, rd

                rqf = emit_rope(qro, t_qc)
                pv[("ropeqf", s, t)] = rqf
                qsq, rnq = emit_norm(qn2[:, t * H:(t + 1) * H], qro)
                pv[("qsq", s, t)] = qsq
                pv[("qn2r", s, t)] = rnq
                rkf = emit_rope(kro, t_kc)
                pv[("ropekf", s, t)] = rkf
                ksq, rnk = emit_norm(kn2[:, t * H:(t + 1) * H], kro)
                pv[("ksq", s, t)] = ksq
                pv[("kn2r", s, t)] = rnk
                def emit_tr(tt):
                    qro2, kro2 = qslots[tt % 2], kslots[tt % 2]
                    trq = None
                    for hp in range(NPAIR):
                        trq = kb.op(T, lambda e, hp=hp, qro2=qro2: e.matmul(
                            psE[:, hp * 128:(hp + 1) * 128],
                            qro2[:, hp * 128:(hp + 1) * 128], ident,
                            start=True, stop=True),
                            waits=[pv[("ropeqf", s, tt)],
                                   pv.get(("qev", s, tt - 1))] if hp == 0 else ())
                    pv[("trq", s, tt)] = trq
                    t_qev = kb.op(V, lambda e, tt=tt: e.tensor_copy(
                        q_fm.rearrange("p (hp q) -> p hp q", q=S)[:, :, tt * 128:(tt + 1) * 128],
                        psE.rearrange("p (hp q) -> p hp q", q=128)), waits=[trq])
                    pv[("qev", s, tt)] = t_qev
                    trk = None
                    for hp in range(NPAIR):
                        trk = kb.op(T, lambda e, hp=hp, kro2=kro2: e.matmul(
                            psB[:, hp * 128:(hp + 1) * 128],
                            kro2[:, hp * 128:(hp + 1) * 128], ident,
                            start=True, stop=True),
                            waits=[pv[("ropekf", s, tt)],
                                   pv.get(("vev", s, tt + 1)),
                                   pv.get(("kev", s, tt - 1))] if hp == 0 else ())
                    pv[("trk", s, tt)] = trk
                    t_kev = kb.op(Sc, lambda e, tt=tt: e.copy(
                        k_fm.rearrange("p (hp q) -> p hp q", q=S)[:, :, tt * 128:(tt + 1) * 128],
                        psB.rearrange("p (hp q) -> p hp q", q=128)), waits=[trk])
                    pv[("kev", s, tt)] = t_kev
                if t > 0:
                    emit_tr(t - 1)
            emit_tr(NT - 1)

            p1_end = [pv[("qev", s, NT - 1)], pv[("kev", s, NT - 1)],
                      pv[("vev", s, NT - 1)]]

            kn_all = [pv[("kn2r", s, t)] for t in range(NT)]
            m1 = kb.op(V, lambda e: e.reduce_max(
                km[:, 0:H], kn2.rearrange("p (t h) -> p h t", h=H), axis=AX.X),
                waits=kn_all)
            m2 = kb.op(V, lambda e: e.tensor_scalar_mul(km_bf[:, 0:H], km[:, 0:H],
                                                        1.05), waits=[m1])
            m3 = kb.op(T, lambda e: e.matmul(psB[0:H, 0:128], km_bf[:, 0:H], ident,
                                             start=True, stop=True),
                       waits=[m2, pv[("kev", s, NT - 1)], t_ones])
            m4 = kb.op(V, lambda e: e.reduce_max(km2t[0:H, 0:1], psB[0:H, 0:128],
                                                 axis=AX.X), waits=[m3])
            m5 = kb.op(V, lambda e: e.tensor_scalar_mul(km2t_bf[0:H, 0:1],
                                                        km2t[0:H, 0:1], 1.0),
                       waits=[m4])
            m6 = kb.op(T, lambda e: e.matmul(psA[0:1, 0:H],
                                             km2t_bf[0:H, 0:1],
                                             ident[0:H, 0:H],
                                             start=True, stop=True),
                       waits=[m5, pv[("qc", s, NT - 1)], pv[("kc", s, NT - 1)]])
            m7 = kb.op(Sc, lambda e: e.copy(kmrow[0:1, 0:H], psA[0:1, 0:H]),
                       waits=[m6])
            m8 = kb.op(T, lambda e: e.matmul(psA[:, 512:512 + H], onesr,
                                             kmrow[0:1, 0:H],
                                             start=True, stop=True), waits=[m7])
            m9 = kb.op(Sc, lambda e: e.copy(kmb[:, 0:H], psA[:, 512:512 + H]),
                       waits=[m8])
            m10 = kb.op(V, lambda e: e.tensor_tensor(
                mbb.rearrange("p (t h) -> p t h", h=H),
                qn2.rearrange("p (t h) -> p t h", h=H),
                kmb[:, 0:H].unsqueeze(1).broadcast_to([128, NT, H]), ALU.mult),
                waits=[m9] + [pv[("qn2r", s, t)] for t in range(NT)])
            pv[("nrm_r",)] = m10
            ma = kb.op(V, lambda e: e.tensor_scalar_mul(
                mxbf[:, 0:NT * H], mbb, 1.02), waits=[m10])
            mc = None
            for c2 in range(NC2):
                w2c = min(128, NT * H - c2 * 128)
                mt1 = kb.op(T, lambda e, c2=c2, w2c=w2c: e.matmul(
                    psA[0:w2c, 0:128], mxbf[:, c2 * 128:c2 * 128 + w2c], ident,
                    start=True, stop=True), waits=[ma, m9, mc])
                mc = kb.op(V, lambda e, c2=c2, w2c=w2c: e.reduce_max(
                    mxc[0:w2c, c2:c2 + 1], psA[0:w2c, 0:128], axis=AX.X),
                    waits=[mt1])
            md = kb.op(V, lambda e: e.tensor_scalar_mul(
                mxcb[:, 0:NC2], mxc[:, 0:NC2], 1.0), waits=[mc, t_mxz])
            mg = None
            for c2 in range(NC2):
                w2c = min(128, NT * H - c2 * 128)
                me = kb.op(T, lambda e, c2=c2: e.matmul(
                    psA[0:1, 128:256], mxcb[:, c2:c2 + 1], ident,
                    start=True, stop=True), waits=[md, mg])
                mf = kb.op(Sc, lambda e, c2=c2: e.copy(
                    mrows[0:1, c2 * 128:(c2 + 1) * 128], psA[0:1, 128:256]),
                    waits=[me])
                mg1 = kb.op(T, lambda e, c2=c2, w2c=w2c: e.matmul(
                    psA[:, 512 + c2 * 128:512 + c2 * 128 + w2c], onesr,
                    mrows[0:1, c2 * 128:c2 * 128 + w2c], start=True, stop=True),
                    waits=[mf, t_ones])
                mg = kb.op(Sc, lambda e, c2=c2, w2c=w2c: e.copy(
                    mbb[:, c2 * 128:c2 * 128 + w2c],
                    psA[:, 512 + c2 * 128:512 + c2 * 128 + w2c]), waits=[mg1])
            mbb2 = kn2[:, 0:NSUPQ * H]
            msu = kb.op(V, lambda e: e.reduce_max(
                mbb2.rearrange("p (T h) -> p T h", h=H),
                mbb.rearrange("p (T f h) -> p T h f", f=QS, h=H), axis=AX.X),
                waits=[mg] + [pv[("kn2r", s, t2)] for t2 in range(NT)])
            m11 = kb.op(Sc, lambda e: e.activation(mbb2, mbb2, AF.Ln, scale=1.0,
                                                   bias=epsb), waits=[msu, t_eps])
            m12 = kb.op(Sc, lambda e: e.activation(mbb2, mbb2, AF.Exp, scale=0.5),
                        waits=[m11])
            m13 = kb.op(V, lambda e: e.tensor_scalar_mul(mbb2, mbb2, -SCALE),
                        waits=[m12])

            psA_user = {}
            pt_user = {}
            aev_prev = {}
            scnt = [0]

            def make_woln(Ts):
                q0 = Ts * QS * 128
                clos = []
                for j in range(QS):
                    gt = Ts * QS + j
                    for piece in range(2):
                        def go(j=j, gt=gt, piece=piece, q0=q0):
                            g2 = j * 2 + piece
                            bank = psA[:, 1536:2048]
                            if piece == 0:
                                pv[("xrtok",)] = kb.dma(
                                    Sy, "xr", xres,
                                    xs[s, q0 + j * 128:q0 + (j + 1) * 128, :],
                                    waits=[pv.get(("resid", s, gt - 1)),
                                           pv.get(("resid", s - 1, NT - 1))
                                           if gt == 0 else None])
                            ww = [pv[("aopair", NPAIR - 1)],
                                  pv.get(("tyg",))]
                            womm = None
                            for hp in range(NPAIR):
                                womm = kb.op(T, lambda e, hp=hp, j=j, piece=piece,
                                             bank=bank: e.matmul(
                                                 bank,
                                                 attn_sup[:, hp * 512 + j * 128:
                                                          hp * 512 + (j + 1) * 128],
                                                 wo_b[:, hp * 1024 + piece * 512:
                                                      hp * 1024 + (piece + 1) * 512],
                                                 start=(hp == 0),
                                                 stop=(hp == NPAIR - 1)),
                                             waits=ww if hp == 0 else ())
                            if j == QS - 1 and piece == 1:
                                pv[("wo_last",)] = womm
                            t_yp = kb.op(V, lambda e, piece=piece, bank=bank:
                                         e.tensor_add(
                                             ybuf[:, piece * 512:(piece + 1) * 512],
                                             bank,
                                             xres[:, piece * 512:(piece + 1) * 512]),
                                         waits=[womm, pv[("xrtok",)],
                                                ybuf_r[0] if piece == 0 else None,
                                                pv.get(("x2dma", s, gt - 1))
                                                if piece == 0 else None])
                            pv[("tyg",)] = t_yp
                            if piece == 0:
                                return
                            t_y = t_yp
                            pv[("resid", s, gt)] = t_y
                            t_ns = kb.op(V, lambda e: e.reduce_sum(
                                stat[:, ST_NSUM:ST_NSUM + 1], ybuf, axis=AX.X,
                                negate=True), waits=[t_y])
                            t_nm = kb.op(V, lambda e: e.tensor_scalar_mul(
                                stat[:, ST_NM:ST_NM + 1],
                                stat[:, ST_NSUM:ST_NSUM + 1], 1.0 / D),
                                waits=[t_ns])
                            t_sq = kb.op(V, lambda e: e.scalar_tensor_tensor(
                                sqbuf, ybuf, stat[:, ST_NM:ST_NM + 1], ybuf,
                                ALU.add, ALU.mult,
                                accum_out=stat[:, ST_SSQ:ST_SSQ + 1]),
                                waits=[t_nm, sqbuf_r[0]])
                            t_ln = kb.op(Sc, lambda e: e.activation(
                                stat[:, ST_LNV:ST_LNV + 1],
                                stat[:, ST_SSQ:ST_SSQ + 1],
                                AF.Ln, scale=1.0 / D, bias=epsb),
                                waits=[t_sq, t_eps])
                            t_rs = kb.op(Sc, lambda e: e.activation(
                                stat[:, ST_RSTD:ST_RSTD + 1],
                                stat[:, ST_LNV:ST_LNV + 1],
                                AF.Exp, scale=-0.5), waits=[t_ln])
                            t_x2 = kb.op(V, lambda e: e.tensor_scalar(
                                sqbuf, ybuf, stat[:, ST_NM:ST_NM + 1],
                                stat[:, ST_RSTD:ST_RSTD + 1], ALU.add, ALU.mult),
                                waits=[t_rs])
                            ybuf_r[0] = t_x2
                            t_st = kb.dma(
                                Sy, "st",
                                x2s[(s * NT + gt) * 128:(s * NT + gt + 1) * 128, :],
                                sqbuf, waits=[t_x2])
                            sqbuf_r[0] = t_st
                            pv[("x2dma", s, gt)] = t_st
                        clos.append(go)
                return clos

            queue = []
            kstep = [0]
            for Ts in range(NSUPQ):
                q0 = Ts * QS * 128
                qw = QS * 128
                nk = Ts * QS + QS
                for hp in range(NPAIR):
                    h = 2 * hp + 1

                    gp = Ts * NPAIR + hp
                    pvps = psE if gp % 2 == 0 else psB

                    def emit_pv(eh, k, ex_tok):
                        band_j = k - Ts * QS
                        coff = max(0, band_j) * 128
                        hh = 2 * hp + eh
                        ww2 = [ex_tok]
                        if k == 0:
                            ww2.append(aev_prev.get((gp % 2, eh)))
                        pvm = kb.op(T, lambda e, k=k, hh=hh, eh=eh,
                                    coff=coff, qw=qw, nk=nk,
                                    pvps=pvps: e.matmul(
                                        pvps[0:65, eh * 512 + coff:eh * 512 + qw],
                                        v_tm4[:, k, hh, :],
                                        pt_sb2[k % 2][:, eh * 512 + coff:
                                                       eh * 512 + qw],
                                        start=(k == 0), stop=(k == nk - 1)),
                                    waits=ww2)
                        pt_user[(k % 2, eh)] = pvm
                        return pvm

                    prev_ex = [None, None]
                    for k in range(nk):
                        sa = scnt[0] % 3
                        sb = (scnt[0] + 1) % 3
                        scnt[0] += 2
                        kstep[0] += 2
                        for _ in range(2):
                            if queue and queue[0][0] <= kstep[0]:
                                queue.pop(0)[1]()
                            else:
                                break
                        band_j = k - Ts * QS
                        coff = max(0, band_j) * 128
                        stmm = {}
                        for eh, sl in ((0, sa), (1, sb)):
                            ww = [psA_user.get(sl)]
                            if Ts == 0 and hp == 0 and k == 0 and eh == 0:
                                ww += p1_end + [m13]
                            stmm[eh] = kb.op(
                                T, lambda e, k=k, hp=hp, e64=eh * 64, sl=sl,
                                coff=coff, qw=qw: e.matmul(
                                    psA[:, sl * 512 + coff:sl * 512 + qw],
                                    k_fm[e64:e64 + 64,
                                         hp * S + k * 128:hp * S + (k + 1) * 128],
                                    q_fm[e64:e64 + 64,
                                         hp * S + q0 + coff:hp * S + q0 + qw],
                                    start=True, stop=(band_j < 0)),
                                waits=ww)
                        if band_j >= 0:
                            for eh, sl in ((0, sa), (1, sb)):
                                stmm[eh] = kb.op(
                                    T, lambda e, sl=sl, coff=coff: e.matmul(
                                        psA[:, sl * 512 + coff:
                                            sl * 512 + coff + 128],
                                        ident, maskt, start=False, stop=True),
                                    waits=[t_maskt] if (Ts == 0 and hp == 0
                                                        and eh == 0) else ())
                        if k > 0:
                            emit_pv(0, k - 1, prev_ex[0])
                            emit_pv(1, k - 1, prev_ex[1])
                        for eh, sl in ((0, sa), (1, sb)):
                            hh = 2 * hp + eh
                            ex = kb.op(Sc, lambda e, sl=sl, eh=eh, coff=coff,
                                       k=k, hh=hh, qw=qw: e.activation(
                                           pt_sb2[k % 2][:, eh * 512 + coff:
                                                          eh * 512 + qw],
                                           psA[:, sl * 512 + coff:sl * 512 + qw],
                                           AF.Exp,
                                           bias=mbb2[:, Ts * H + hh:
                                                     Ts * H + hh + 1],
                                           scale=SCALE),
                                       waits=[stmm[eh],
                                              pt_user.get((k % 2, eh))])
                            psA_user[sl] = ex
                            prev_ex[eh] = ex
                    pvm0 = emit_pv(0, nk - 1, prev_ex[0])
                    pvm1 = emit_pv(1, nk - 1, prev_ex[1])
                    if True:

                        def mk_rfn(e2, pvm_e, gp=gp, qw=qw, pvps=pvps):
                            def rfn():
                                lrv = lr_all[:, e2 * 512:e2 * 512 + qw]
                                r1 = kb.op(Sc, lambda e, lrv=lrv, e2=e2,
                                           pvps=pvps, qw=qw: e.activation(
                                    lrv, pvps[64:65, e2 * 512:e2 * 512 + qw],
                                    AF.Ln, scale=2.0 ** 40, bias=tinyb[0:1, :]),
                                    waits=[pvm_e, t_tiny,
                                           pv.get(("lr_r",))
                                           if e2 == 0 else None])
                                r2 = kb.op(Sc, lambda e, lrv=lrv: e.activation(
                                    lrv, lrv, AF.Exp, scale=-1.0), waits=[r1])
                                if e2 == 1:
                                    pv[("lr_r",)] = r2
                                pv[("r2", gp, e2)] = r2
                            return rfn

                        def norm_tail(hp=hp, h=h, gp=gp, qw=qw, pvps=pvps):
                            r2 = [pv[("r2", gp, 0)], pv[("r2", gp, 1)]]
                            lrv2 = lr_all.rearrange(
                                "p (e c) -> p e c", c=512)[:, :, 0:qw]
                            rhv = rhi_all.rearrange(
                                "p (e c) -> p e c", c=512)[:, :, 0:qw]
                            rlv = rlo_all.rearrange(
                                "p (e c) -> p e c", c=512)[:, :, 0:qw]
                            rc2 = kb.op(V, lambda e, lrv2=lrv2, rhv=rhv:
                                        e.tensor_copy(rhv, lrv2),
                                        waits=[r2, pv.get(("rbs_r",))])
                            rc3 = kb.op(V, lambda e, lrv2=lrv2, rhv=rhv, rlv=rlv:
                                        e.tensor_tensor(rlv, lrv2, rhv,
                                                        ALU.subtract),
                                        waits=[rc2])
                            pv[("lr_r",)] = rc3
                            rbm = None
                            for eh2 in range(2):
                                for rrow, st_ in ((rhi_all, True), (rlo_all, False)):
                                    rbm = kb.op(T, lambda e, rrow=rrow, st_=st_,
                                                qw=qw, eh2=eh2, pvps=pvps: e.matmul(
                                                    pvps[64:128, eh2 * 512:
                                                         eh2 * 512 + qw],
                                                    onesr[0:1, 0:64],
                                                    rrow[0:1, eh2 * 512:
                                                         eh2 * 512 + qw],
                                                    start=st_, stop=not st_,
                                                    tile_position=(0, 64)),
                                                waits=[rc3, t_ones]
                                                if (st_ and eh2 == 0) else ())
                            rbev = kb.op(V, lambda e, qw=qw, pvps=pvps:
                                         e.tensor_copy(
                                rbs_all.rearrange("p (e c) -> p e c", c=512)[:, :, 0:qw],
                                pvps[64:128, 0:1024].rearrange(
                                    "p (e c) -> p e c", c=512)[:, :, 0:qw]),
                                waits=[rbm, pv.get(("rbs_r",))])
                            aevE = kb.op(V, lambda e, hp=hp, qw=qw, pvps=pvps:
                                         e.scalar_tensor_tensor(
                                             attn_sup[0:64, hp * 512:hp * 512 + qw],
                                             pvps[0:64, 0:qw], 2.0 ** 40,
                                             rbs_all[0:64, 0:qw],
                                             ALU.mult, ALU.mult),
                                         waits=[rbev] +
                                         ([pv.get(("wo_last",))] if hp == 0 else []))
                            aev_prev[(gp % 2, 0)] = aevE
                            aevO = kb.op(V, lambda e, hp=hp, qw=qw, pvps=pvps:
                                         e.scalar_tensor_tensor(
                                             attnO[0:64, 0:qw],
                                             pvps[0:64, 512:512 + qw], 2.0 ** 40,
                                             rbs_all[0:64, 512:512 + qw],
                                             ALU.mult, ALU.mult),
                                         waits=[rbev, pv.get(("aotok",))])
                            pv[("aevd",)] = aevO
                            pv[("rbs_r",)] = aevO
                            t_ao = kb.dma(Sy, "ao",
                                          attn_sup[64:128, hp * 512:hp * 512 + qw],
                                          attnO[0:64, 0:qw],
                                          waits=[aevO, pv.get(("wo_last",))])
                            aev_prev[(gp % 2, 1)] = aevO
                            pv[("aotok",)] = t_ao
                            pv[("aopair", hp)] = (aevE, t_ao)
                        queue.append((kstep[0] + 1, mk_rfn(0, pvm0)))
                        queue.append((kstep[0] + 2, mk_rfn(1, pvm1)))
                        queue.append((kstep[0] + (5 if nk <= 8 else 7),
                                      norm_tail))
                queue += [(kstep[0], fn) for fn in make_woln(Ts)]
            for _, go in queue:
                go()

        p2_end = [pv[("x2dma", s, NT - 1)] for s in range(NSEQ)]

        w1_slots = [RW[:, k * 2048:(k + 1) * 2048].bitcast(F32)
                    for k in range(6)]
        W1SEM = ("b0", "b1", "c0", "c1", "c2", "c3")
        bz_rd = {}
        bz_ev = {}
        for i in range(W1T):
            wt = w1_slots[i % 6]
            ww = [bz_rd.get(i - 6)]
            if i < 6:
                ww.append(pv[("qkv", NSEQ - 1, NT - 1)])
            t_l = kb.dma(G, W1SEM[i % 6], wt,
                         w1[i * 128:(i + 1) * 128, :], waits=ww)
            t_a = kb.op(V, lambda e, wt=wt: e.reduce_sum(
                stat[:, ST_NSUM:ST_NSUM + 1], wt, axis=AX.X,
                apply_absolute_value=True),
                waits=[t_l] + (p2_end if i == 0 else []))
            t_a = kb.op(V, lambda e: e.tensor_scalar_mul(
                stat[:, ST_NM:ST_NM + 1], stat[:, ST_NSUM:ST_NSUM + 1], 1.0 / D),
                waits=[t_a])
            sgs = sgn[:, (i % 2) * 1024:(i % 2) * 1024 + 1024]
            t_s = kb.op(Sc, lambda e, sgs=sgs, wt=wt: e.sign(sgs, wt),
                        waits=[t_l, pv.get(("bztr", i - 2))] +
                        (p2_end if i < 2 else []))
            t_b = kb.op(V, lambda e, sgs=sgs: e.tensor_scalar_mul(
                sgs, sgs, stat[:, ST_NM:ST_NM + 1]),
                waits=[t_s, t_a])
            pv[("bzb", i)] = t_b
            bz_rd[i] = [t_a, t_s]

            def bz_tr(ii):
                sg2 = sgn[:, (ii % 2) * 1024:(ii % 2) * 1024 + 1024]
                ps = (psB, psE)[ii % 2]
                trm = None
                for dc in range(NDC):
                    trm = kb.op(T, lambda e, dc=dc, ps=ps, sg2=sg2: e.matmul(
                        ps[:, dc * 128:(dc + 1) * 128],
                        sg2[:, dc * 128:(dc + 1) * 128], ident,
                        start=True, stop=True),
                        waits=([pv[("bzb", ii)], bz_ev.get(ii - 2)] +
                               (p2_end if ii == 0 else [])) if dc == 0 else ())
                pv[("bztr", ii)] = trm
                ev = kb.op(Sc, lambda e, ps=ps, ii=ii: e.copy(
                    bw1T.rearrange("p (dc f) -> p dc f", f=FF)[:, :, ii * 128:(ii + 1) * 128],
                    ps.rearrange("p (dc q) -> p dc q", q=128)), waits=[trm])
                bz_ev[ii] = ev
            if i > 0:
                bz_tr(i - 1)
        bz_tr(W1T - 1)
        bw1_done = [bz_ev[W1T - 1], bz_ev[W1T - 2]]
        w2_slots = [RW[:, k * 8192:(k + 1) * 8192].bitcast(F32)
                    for k in range(4)]
        W2SEM = ("b2", "w0", "w1", "w2")
        bz2_rd = {}
        bw2_wtoks = []
        for i in range(W2T):
            wt2 = w2_slots[i % 4]
            ww = [bz2_rd.get(i - 4)]
            if i < 4:
                ww += [bz_rd.get(W1T - 1 - k) for k in range(6)]
            t_l = kb.dma(G, W2SEM[i % 4], wt2, w2[i * 128:(i + 1) * 128, :],
                         waits=ww)
            t_a = kb.op(V, lambda e, wt2=wt2: e.reduce_sum(
                stat[:, ST_NSUM:ST_NSUM + 1], wt2, axis=AX.X,
                apply_absolute_value=True), waits=[t_l])
            t_a = kb.op(V, lambda e: e.tensor_scalar_mul(
                stat[:, ST_NM:ST_NM + 1], stat[:, ST_NSUM:ST_NSUM + 1], 1.0 / FF),
                waits=[t_a])
            t_s = kb.op(Sc, lambda e, wt2=wt2: e.sign(sgn, wt2),
                        waits=[t_l, pv.get(("bz2tr", i - 1)),
                               pv.get(("bztr", W1T - 1)) if i == 0 else None] +
                        (bw1_done if i == 0 else []))
            t_b = kb.op(V, lambda e: e.tensor_scalar_mul(
                sgn, sgn, stat[:, ST_NM:ST_NM + 1]), waits=[t_s, t_a])
            trm = None
            NG2 = NFC // 8
            for g in range(NG2):
                ps = (psB, psE)[g % 2]
                for c8 in range(8):
                    c = g * 8 + c8
                    trm = kb.op(T, lambda e, c=c, c8=c8, ps=ps: e.matmul(
                        ps[:, c8 * 128:(c8 + 1) * 128],
                        sgn[:, c * 128:(c + 1) * 128], ident,
                        start=True, stop=True),
                        waits=([t_b, pv.get(("bz2d", i, g - 2)),
                                pv.get(("bz2d", i - 1, NG2 - 2 + g))
                                if g < 2 else None,
                                bz_ev.get(W1T - 1) if (i == 0 and g == 0) else None,
                                bz_ev.get(W1T - 2) if (i == 0 and g == 1) else None]
                               ) if c8 == 0 else ())
                sgo = (sgn_s, sgn_s2)[g % 2]
                evs = kb.op(Sc, lambda e, ps=ps, sgo=sgo: e.copy(sgo, ps),
                            waits=[trm, pv.get(("bz2d", i, g - 2)),
                                   pv.get(("bz2d", i - 1, NG2 - 2 + g))
                                   if g < 2 else None])
                t_d = kb.dma(G, ("bw", "b0", "b1", "c0")[g % 4],
                             bw2s.rearrange("(a p) o -> p a o", p=128)
                             [:, g * 8:(g + 1) * 8, i * 128:(i + 1) * 128],
                             sgo.rearrange("p (a o) -> p a o", o=128),
                             waits=[evs])
                pv[("bz2d", i, g)] = t_d
                bw2_wtoks.append(t_d)
            pv[("bz2tr", i)] = trm
            bz2_rd[i] = [t_a, t_s]

        x2_all = ("st", kb.n["st"])
        pf = {}
        pf[("ybr", 0)] = ybuf_r[0]
        pf[("ybr", 1)] = pv.get(("resid", NSEQ - 1, NT - 1))
        sbst = (RB[:, 5120:7168].bitcast(F32), sqbuf)
        pf[("sq2r", 1)] = sqbuf_r[0]
        ybb = (ybuf, xres,
               RQ[:, 32768:34816].bitcast(F32), RQ[:, 34816:36864].bitcast(F32))
        xt4 = (xt_b[0], xt_b[1],
               RW[:, 20480:22528].bitcast(F32), RW[:, 22528:24576].bitcast(F32))
        XSEM = ("x0", "x1", "k4", "k5")
        xr4 = (xr2_b[0], xr2_b[1],
               RQ[:, 36864:38912].bitcast(F32), RQ[:, 38912:40960].bitcast(F32))
        XRSEM = ("tb", "rb", "k6", "k7")
        y_toks = []

        def emit_prep(u):
            for j in range(NSUPT):
                gt = u * NSUPT + j
                xt = xt4[j]
                t_x = pf.get(("xld", u, j))
                if t_x is None:
                    ww = [x2_all, pf.get(("x2c", u - 1, j))]
                    if u == 0 and j < 2:
                        ww += [pv.get(("xcast", NSEQ - 1, NT - 2 + j))]
                    if u == 0 and j >= 2:
                        ww += [pv.get(("bz2tr", W2T - 1)),
                               t_b2 if j == 2 else t_eps]
                    t_x = kb.dma(Sy, XSEM[j], xt,
                                 x2s[gt * 128:(gt + 1) * 128, :], waits=ww)
                t_c = pf.get(("x2c", u, j))
                if t_c is None:
                    ww = [t_x, pf.get(("x2tr", u, j - 1))]
                    if u == 0 and j == 0:
                        ww.append(pv.get(("bz2tr", W2T - 1)))
                    t_c = kb.op(Sc, lambda e, xt=xt: e.copy(x2bf, xt), waits=ww)
                    pf[("x2c", u, j)] = t_c
                ww = [t_c, pf.get(("fevA", u - 1, 0)), pf.get(("fevA", u - 1, 1)),
                      pf.get(("xf_ev", u, j - 1))]
                trm = None
                for dc in range(NDC):
                    trm = kb.op(T, lambda e, dc=dc: e.matmul(
                        psA[:, dc * 128:(dc + 1) * 128],
                        x2bf[:, dc * 128:(dc + 1) * 128], ident,
                        start=True, stop=True), waits=ww if dc == 0 else ())
                pf[("x2tr", u, j)] = trm
                ev = kb.op(Sc, lambda e, j=j: e.copy(
                    x2_fm.rearrange("p (dc q) -> p dc q", q=SUP)[:, :, j * 128:(j + 1) * 128],
                    psA[:, 0:1024].rearrange("p (dc q) -> p dc q", q=128)),
                    waits=[trm, pf.get(("ffn1mm", u - 1))])
                pf[("xf_ev", u, j)] = ev

        emit_prep(0)
        for u in range(NSUPER):
            g0 = u * NSUPT
            xf_done = [pf[("xf_ev", u, j)] for j in range(NSUPT)]
            ge = {}
            f1mm = None
            for fc in range(NFC):
                ps = (psB, psE)[fc % 2]
                mm = None
                for dc in range(NDC):
                    for q0 in range(0, SUP, 512):
                        q1 = min(SUP, q0 + 512)
                        ww = ()
                        if dc == 0 and q0 == 0:
                            ww = xf_done + [ge.get(fc - 2)]
                            if u == 0 and fc < 2:
                                ww = ww + [bz_ev.get(W1T - 1), bw2_wtoks[-1],
                                           bw2_wtoks[-2]]
                            else:
                                ww = ww + [pf.get(("fevB", u - 1, 2 + (fc % 2)))]
                        mm = kb.op(T, lambda e, fc=fc, dc=dc, ps=ps, q0=q0, q1=q1:
                                   e.matmul(ps[:, q0:q1],
                                            bw1T[:, dc * FF + fc * 128:
                                                 dc * FF + (fc + 1) * 128],
                                            x2_fm[:, dc * SUP + q0:dc * SUP + q1],
                                            start=(dc == 0), stop=(dc == NDC - 1)),
                                   waits=ww)
                f1mm = mm
                ge[fc] = kb.op(Sc, lambda e, fc=fc, ps=ps: e.activation(
                    h_fm[:, fc * SUP:(fc + 1) * SUP], ps[:, 0:SUP], AF.Gelu,
                    bias=b1s[:, fc:fc + 1], scale=1.0), waits=[mm, t_b1])
            pf[("ffn1mm", u)] = f1mm
            gelu_done = [ge[fc] for fc in range(NFC)]
            tile_ps = [psA[:, 0:1024], psA[:, 1024:2048], psB, psE]
            f2 = None
            for c in range(NFC):
                bwc = RW[:, 24576 + (c % 8) * 1024:24576 + (c % 8) * 1024 + 1024]
                ww = [pf.get(("bwc", u, c - 8)) if c >= 8 else
                      pf.get(("bwc", u - 1, NFC - 8 + c))]
                if u == 0 and c < 8:
                    ww += bw2_wtoks[-4:]
                t_w2 = kb.dma(G, ("c0", "c1", "c2", "c3",
                                  "g0", "g1", "g2", "g3")[c % 8], bwc,
                              bw2s[c * 128:(c + 1) * 128, :], waits=ww)
                for j in range(NSUPT):
                    for half in range(2):
                        ww = ()
                        if half == 0:
                            ww = [t_w2, gelu_done[c]]
                            if c == 0:
                                ww += [pf.get(("fevA", u - 1, j)) if j < 2 else
                                       pf.get(("fevB", u - 1, j)),
                                       xf_done[-1], gelu_done[NFC - 1]]
                        f2 = kb.op(T, lambda e, c=c, j=j, half=half: e.matmul(
                            tile_ps[j][:, half * 512:(half + 1) * 512],
                            h_fm[:, c * SUP + j * 128:c * SUP + (j + 1) * 128],
                            bwc[:, half * 512:(half + 1) * 512],
                            start=(c == 0), stop=(c == NFC - 1)), waits=ww)
                pf[("bwc", u, c)] = f2
            if u + 1 < NSUPER:
                for j2 in range(NSUPT):
                    t_xp = kb.dma(Sy, XSEM[j2], xt4[j2],
                                  x2s[(g0 + NSUPT + j2) * 128:
                                      (g0 + NSUPT + j2 + 1) * 128, :],
                                  waits=[pf.get(("x2c", u, j2))])
                    pf[("xld", u + 1, j2)] = t_xp
                t_cp = kb.op(G, lambda e: e.tensor_copy(x2bf, xt_b[0]),
                             waits=[pf[("xld", u + 1, 0)],
                                    pf.get(("x2tr", u, NSUPT - 1))])
                pf[("x2c", u + 1, 0)] = t_cp

            for j in range(NSUPT):
                t_y = kb.op(V, lambda e, j=j, yb=ybb[j]: e.tensor_add(
                    yb, tile_ps[j], b2s),
                    waits=[f2, t_b2, pf.get(("ybr", j))])
                key = ("fevA", u, j) if j < 2 else ("fevB", u, j)
                pf[key] = t_y
                pf[("tev", u, j)] = t_y
            if u + 1 < NSUPER:
                emit_prep(u + 1)
            for j in range(NSUPT):
                gt = g0 + j
                ww = [pf.get(("residrd", u - 1, j))]
                if u == 0 and j < 2:
                    ww += [pv.get(("bz2tr", W2T - 1)), t_tiny] + bw2_wtoks[-2:]
                if u == 0 and j >= 2:
                    ww += [x2_all, t_ones if j == 2 else t_maskt]
                t_xr = kb.dma(Sy, XRSEM[j], xr4[j],
                              x2s[gt * 128:(gt + 1) * 128, :], waits=ww)
                pf[("xrt", j)] = t_xr
            for j in range(NSUPT):
                gt = g0 + j
                sq_i, tt = divmod(gt, NT)
                xr = xr4[j]
                yb = ybb[j]
                t_y = pf[("tev", u, j)]
                t_y2 = kb.op(V, lambda e, yb=yb, xr=xr: e.tensor_add(yb, yb, xr),
                             waits=[t_y, pf[("xrt", j)]])
                pf[("residrd", u, j)] = t_y2
                t_ns = kb.op(V, lambda e, yb=yb: e.reduce_sum(
                    stat[:, ST_NSUM:ST_NSUM + 1], yb, axis=AX.X, negate=True),
                    waits=[t_y2])
                t_nm = kb.op(V, lambda e: e.tensor_scalar_mul(
                    stat[:, ST_NM:ST_NM + 1], stat[:, ST_NSUM:ST_NSUM + 1], 1.0 / D),
                    waits=[t_ns])
                sb = sbst[j % 2]
                t_sq = kb.op(V, lambda e, yb=yb, sb=sb: e.scalar_tensor_tensor(
                    sb, yb, stat[:, ST_NM:ST_NM + 1], yb, ALU.add, ALU.mult,
                    accum_out=stat[:, ST_SSQ:ST_SSQ + 1]),
                    waits=[t_nm, pf.get(("sq2r", j % 2)), pf.get(("lnp",))])
                t_ln = kb.op(Sc, lambda e: e.activation(
                    stat[:, ST_LNV:ST_LNV + 1], stat[:, ST_SSQ:ST_SSQ + 1], AF.Ln,
                    scale=1.0 / D, bias=epsb),
                    waits=[t_sq, t_eps, pf.get(("top",))])
                pf[("lnp",)] = t_ln
                t_rs = kb.op(Sc, lambda e: e.activation(
                    stat[:, ST_RSTD:ST_RSTD + 1], stat[:, ST_LNV:ST_LNV + 1], AF.Exp,
                    scale=-0.5), waits=[t_ln])
                t_o = kb.op(V, lambda e, yb=yb, sb=sb: e.tensor_scalar(
                    sb, yb, stat[:, ST_NM:ST_NM + 1],
                    stat[:, ST_RSTD:ST_RSTD + 1], ALU.add, ALU.mult), waits=[t_rs])
                pf[("top",)] = t_o
                pf[("ybr", j)] = t_o
                t_st = kb.dma(Sy, ("st", "xr")[j % 2],
                              y[sq_i, tt * 128:(tt + 1) * 128, :], sb,
                              waits=[t_o])
                pf[("sq2r", j % 2)] = t_st
                y_toks.append(t_st)

        kb.wait_only(Sy, y_toks)
        kb.wait_only(Sy, [(sname, kb.n[sname]) for sname in KB.DMA_SEMS])

    with contextlib.ExitStack() as stack:
        kb.setup_sems(stack)
        block = stack.enter_context(nc.Block())
        for name in ("sync", "scalar", "vector", "tensor", "gpsimd"):
            def make(name=name):
                def f(eng):
                    kb.begin_pass(name, eng)
                    program()
                return f
            getattr(block, name)(make())

    return nc, sbuf_left


def host_prep(cfg, n_cores, L0_attn_x, L0_attn_Wq, L0_attn_Wk, L0_attn_Wv,
              L0_attn_Wo, L0_ffn_w1, L0_ffn_b1, L0_ffn_w2, L0_ffn_b2):
    NSEQ, S, D, H, DK, FF = (cfg[k] for k in ("NSEQ", "S", "D", "H", "DK", "FF"))
    NFC = FF // 128
    x = np.ascontiguousarray(np.asarray(L0_attn_x, dtype=np.float32))
    Wq = np.asarray(L0_attn_Wq, np.float32)
    Wk = np.asarray(L0_attn_Wk, np.float32)
    Wv = np.asarray(L0_attn_Wv, np.float32)
    Wo = np.asarray(L0_attn_Wo, np.float32)
    w1 = np.ascontiguousarray(np.asarray(L0_ffn_w1, np.float32))
    w2 = np.ascontiguousarray(np.asarray(L0_ffn_w2, np.float32))
    b1 = np.asarray(L0_ffn_b1, np.float32)
    b2 = np.asarray(L0_ffn_b2, np.float32)

    perm = np.concatenate([np.concatenate([h * DK + np.arange(0, DK, 2),
                                           h * DK + np.arange(1, DK, 2)])
                           for h in range(H)])
    shared = dict(
        wqT=np.ascontiguousarray(Wq[perm].T),
        wkT=np.ascontiguousarray(Wk[perm].T),
        wvT=np.ascontiguousarray(Wv.T),
        woT=np.ascontiguousarray(Wo.T),
        w1=w1, w2=w2,
        b1r=np.ascontiguousarray(b1.reshape(NFC, 128).T),
        b2b=np.ascontiguousarray(np.broadcast_to(b2, (128, D))),
        identd=np.eye(128, dtype=ml_dtypes.bfloat16),
        epsd=np.full((128, 1), LN_EPS, dtype=np.float32),
        tinyd=np.full((128, 1), 1e-30, dtype=np.float32),
        onesd=np.ones((1, 128), dtype=ml_dtypes.bfloat16),
        masktd=np.where(np.arange(128)[:, None] <= np.arange(128)[None, :],
                        0.0, NEG_BIG).astype(ml_dtypes.bfloat16),
    )
    pos = np.arange(S, dtype=np.float64)
    inv = cfg["THETA"] ** (-np.arange(0, DK, 2, dtype=np.float64) / DK)
    ang = pos[:, None] * inv[None, :]
    shared["roped"] = np.ascontiguousarray(
        np.concatenate([np.cos(ang), np.sin(ang)], axis=1).astype(ml_dtypes.bfloat16))
    B = x.shape[0]
    per = B // n_cores
    assert per == NSEQ
    return [dict(shared, xs=np.ascontiguousarray(x[c * per:(c + 1) * per]))
            for c in range(n_cores)]



def kernel(**inputs):
    from concourse.bass_utils import run_bass_kernel_spmd
    cfg = cfg_full()
    nc, _ = build_program(cfg)
    in_maps = host_prep(cfg, 8, **inputs)
    res = run_bass_kernel_spmd(nc, in_maps, core_ids=list(range(8)))
    return np.concatenate([r["y"] for r in res.results], axis=0)


# version 23
# speedup vs baseline: 1.0899x; 1.0088x over previous
import contextlib
import math
import numpy as np
import ml_dtypes
import concourse.bass as bass
import concourse.mybir as mybir

F32 = mybir.dt.float32
BF16 = mybir.dt.bfloat16
AF = mybir.ActivationFunctionType
ALU = mybir.AluOpType
AX = mybir.AxisListType

LN_EPS = 1e-5
NEG_BIG = -1.0e30


def cfg_full():
    return dict(NSEQ=2, S=2048, D=1024, H=16, DK=64, FF=4096, SUP=512,
                THETA=10000.0)


class KB:
    def __init__(self, nc):
        self.nc = nc
        self.cur = None
        self.sems = {}
        self.n = {}
        self.last_wait = {}
        self.eng = {}

    DMA_SEMS = ("k0", "k1", "k2", "k3", "k4", "k5", "k6", "k7", "ao",
                "w0", "w1", "w2", "w3",
                "x0", "x1", "xr", "st", "b0", "b1", "b2", "bw",
                "c0", "c1", "c2", "c3", "tb", "rb",
                "g0", "g1", "g2", "g3")

    def setup_sems(self, stack):
        for name in ("ct", "cv", "cs", "cg") + self.DMA_SEMS:
            self.sems[name] = stack.enter_context(self.nc.semaphore(name))

    def begin_pass(self, engname, engobj):
        self.cur = engname
        self.eng[engname] = engobj
        self.n = {k: 0 for k in self.sems}
        self.last_wait = {k: 0 for k in self.sems}

    def _waits(self, waits):
        for tok in waits:
            if tok is None:
                continue
            if isinstance(tok, (list, tuple)) and tok and \
                    isinstance(tok[0], (list, tuple)):
                self._waits(tok)
                continue
            sem_name, val = tok
            if val > self.last_wait[sem_name]:
                self.eng[self.cur].wait_ge(self.sems[sem_name], val)
                self.last_wait[sem_name] = val

    def op(self, engname, emit, waits=()):
        sem = {"tensor": "ct", "vector": "cv", "scalar": "cs",
               "gpsimd": "cg"}[engname]
        self.n[sem] += 1
        tok = (sem, self.n[sem])
        if self.cur == engname:
            self._waits(waits)
            emit(self.eng[engname]).then_inc(self.sems[sem], 1)
        return tok

    def dma(self, engname, sem, out, in_, waits=()):
        self.n[sem] += 16
        tok = (sem, self.n[sem])
        if self.cur == engname:
            self._waits(waits)
            self.eng[engname].dma_start(out=out, in_=in_).then_inc(self.sems[sem], 16)
        return tok

    def wait_only(self, engname, waits):
        if self.cur == engname:
            self._waits(waits)


def build_program(cfg):
    NSEQ, S, D, H, DK, FF, SUP = (cfg[k] for k in
                                  ("NSEQ", "S", "D", "H", "DK", "FF", "SUP"))
    NT = S // 128
    NDC = D // 128
    NFC = FF // 128
    NPAIR = H // 2
    NSUPT = SUP // 128
    NSUPER = (NSEQ * S) // SUP
    W1T = FF // 128
    W2T = D // 128
    QS = 4 if NT % 4 == 0 else NT
    NSUPQ = NT // QS
    SCALE = 1.0 / math.sqrt(DK)

    nc = bass.Bass("TRN2", target_bir_lowering=False, debug=False)

    xs = nc.dram_tensor("xs", [NSEQ, S, D], F32, kind="ExternalInput").ap()
    wqT = nc.dram_tensor("wqT", [D, D], F32, kind="ExternalInput").ap()
    wkT = nc.dram_tensor("wkT", [D, D], F32, kind="ExternalInput").ap()
    wvT = nc.dram_tensor("wvT", [D, D], F32, kind="ExternalInput").ap()
    woT = nc.dram_tensor("woT", [D, D], F32, kind="ExternalInput").ap()
    w1 = nc.dram_tensor("w1", [FF, D], F32, kind="ExternalInput").ap()
    w2 = nc.dram_tensor("w2", [D, FF], F32, kind="ExternalInput").ap()
    b1r = nc.dram_tensor("b1r", [128, NFC], F32, kind="ExternalInput").ap()
    b2b = nc.dram_tensor("b2b", [128, D], F32, kind="ExternalInput").ap()
    identd = nc.dram_tensor("identd", [128, 128], BF16, kind="ExternalInput").ap()
    masktd = nc.dram_tensor("masktd", [128, 128], BF16, kind="ExternalInput").ap()
    roped = nc.dram_tensor("roped", [S, 64], BF16, kind="ExternalInput").ap()
    onesd = nc.dram_tensor("onesd", [1, 128], BF16, kind="ExternalInput").ap()
    epsd = nc.dram_tensor("epsd", [128, 1], F32, kind="ExternalInput").ap()
    tinyd = nc.dram_tensor("tinyd", [128, 1], F32, kind="ExternalInput").ap()
    y = nc.dram_tensor("y", [NSEQ, S, D], F32, kind="ExternalOutput").ap()
    x2s = nc.dram_tensor("x2s", [NSEQ * S, D], F32, kind="Internal").ap()
    bw2s = nc.dram_tensor("bw2s", [FF, D], BF16, kind="Internal").ap()

    A = nc.alloc_sbuf_tensor
    ident = A("ident", [128, 128], BF16).ap()
    maskt = A("maskt", [128, 128], BF16).ap()
    rope = A("rope", [128, NT * 64], BF16).ap()
    b1s = A("b1s", [128, NFC], F32).ap()
    b2s = A("b2s", [128, D], F32).ap()
    stat = A("stat", [128, 16], F32).ap()
    epsb = A("epsb", [128, 1], F32).ap()
    tinyb = A("tinyb", [128, 1], F32).ap()
    onesr = A("onesr", [1, 128], BF16).ap()
    qn2 = A("qn2", [128, NT * H], F32).ap()
    kn2 = A("kn2", [128, NT * H], F32).ap()
    mbb = A("mbb", [128, NT * H], F32).ap()
    km = A("km", [128, 16], F32).ap()
    km_bf = A("km_bf", [128, 16], BF16).ap()
    km2t = A("km2t", [128, 1], F32).ap()
    km2t_bf = A("km2t_bf", [128, 1], BF16).ap()
    kmrow = A("kmrow", [1, 16], BF16).ap()
    kmb = A("kmb", [128, 16], F32).ap()
    NC2 = (NT * H + 127) // 128
    mxc = A("mxc", [128, 4], F32).ap()
    mxcb = A("mxcb", [128, 4], BF16).ap()
    mrows = A("mrows", [1, 256], BF16).ap()


    (ST_NSUM, ST_NM, ST_SSQ, ST_LNV, ST_RSTD, ST_M, ST_MB, ST_R, ST_MP) = range(9)

    RQ = A("RQ", [128, 49664], BF16).ap()
    q_fm = RQ[:, 0:NPAIR * S]
    k_fm = RQ[:, 16384:16384 + NPAIR * S]
    v_tm = RQ[:, 32768:32768 + NT * H * 65]
    v_tm4 = v_tm.rearrange("p (t h c) -> p t h c", h=H, c=65)
    bw1T = RQ[:, 0:NDC * FF]
    RW = A("RW", [128, 32768], BF16).ap()
    wq_b = RW[:, 0:8192]
    wk_b = RW[:, 8192:16384]
    wv_b = RW[:, 16384:24576]
    wo_b = RW[:, 24576:32768]
    h_fm = RW[:, 0:NFC * SUP]
    x2_fm = RW[:, 16384:16384 + NDC * SUP]
    bw2c = RW[:, 24576:28672]
    RS = A("RS", [128, 5120], F32).ap()
    xt_b = (RS[:, 0:1024], RS[:, 1024:2048])
    p_f32 = RS[:, 0:2048]
    xres = RS[:, 2048:3072]
    rbs_all = RS[0:64, 0:1024]
    lr_all = RS[0:1, 1024:2048]
    rhi_all = RS[0:1, 0:512].bitcast(BF16)
    rlo_all = RS[0:1, 512:1024].bitcast(BF16)
    sqbuf = RS[:, 3072:4096]
    ybuf = RS[:, 4096:5120]
    w2t_f = RS[:, 0:4096]
    RB = A("RB", [128, 7168], BF16).ap()
    xbf = RB[:, 0:1024]
    x_fm = RB[:, 1024:2048]
    qslots = (RB[:, 4096:5120], RB[:, 2048:3072])
    kslots = (RB[:, 5120:6144], RB[:, 3072:4096])
    attn_sup = RB[:, 0:4096]
    pt_sb2 = (RB[:, 4096:5120], RB[:, 5120:6144])
    attnO = RB[:, 6144:6656]
    sgn = RB[:, 0:4096]
    sgn_s = RB[:, 4096:5120]
    sgn_s2 = RB[:, 5120:6144]
    x2bf = RB[:, 0:1024]
    xr2_b = (RB[:, 1024:3072].bitcast(F32), RB[:, 3072:5120].bitcast(F32))
    tmpX = RB[:, 6144:7168]
    mxbf = RB[:, 6144:6144 + NT * H]

    psA = nc.alloc_psum_tensor("psA", [128, 2048], F32).ap()
    psB = nc.alloc_psum_tensor("psB", [128, 1024], F32).ap()
    psE = nc.alloc_psum_tensor("psE", [128, 1024], F32).ap()

    sbuf_left = nc.sbuf_bytes_remaining

    kb = KB(nc)
    T, V, Sc, Sy, G = "tensor", "vector", "scalar", "sync", "gpsimd"

    def program():
        pv = {}
        ybuf_r = [None]
        sqbuf_r = [None]

        t_id = kb.dma(Sy, "k0", ident, identd)
        t_maskt = kb.dma(Sy, "k7", maskt, masktd)
        t_rope = kb.dma(Sy, "k2", rope.rearrange("p (t c) -> p t c", c=64),
                        roped.rearrange("(t p) c -> p t c", p=128))
        t_b1 = kb.dma(Sy, "k3", b1s, b1r)
        t_b2 = kb.dma(Sy, "k4", b2s, b2b)
        t_eps = kb.dma(Sy, "k5", epsb, epsd)
        t_tiny = kb.dma(Sy, "tb", tinyb, tinyd)
        t_ones = kb.dma(Sy, "k6", onesr, onesd)
        t_vones = kb.op(V, lambda e: e.memset(v_tm4[:, :, :, 64:65], 1.0))
        t_mxz = kb.op(V, lambda e: e.memset(mxc, 0.0))
        t_w = []
        for wsem, dst, src in (("w0", wq_b, wqT), ("w1", wk_b, wkT),
                               ("w2", wv_b, wvT), ("w3", wo_b, woT)):
            t_w.append(kb.dma(G, wsem, dst.rearrange("p (c o) -> p c o", o=D),
                              src.rearrange("(c p) o -> p c o", p=128)))

        for s in range(NSEQ):
            def emit_xload(t):
                ww = [pv.get(("xcast", s, t - 2))]
                if s > 0 and t < 2:
                    ww.append(pv.get(("x2dma", s - 1, NT - 1)))
                t_x = kb.dma(Sy, ("x0", "x1")[t % 2], xt_b[t % 2],
                             xs[s, t * 128:(t + 1) * 128, :], waits=ww)
                ww = [t_x, pv.get(("xtr", s, t - 1))]
                if s > 0 and t == 0:
                    ww.append(pv.get(("x2dma", s - 1, NT - 1)))
                t_cast = kb.op(G, lambda e, t=t: e.tensor_copy(
                    xbf, xt_b[t % 2]), waits=ww)
                pv[("xcast", s, t)] = t_cast

            def emit_xtr(t):
                ww = [pv[("xcast", s, t)], t_id, pv.get(("kc", s, t - 1))]
                tr = None
                for dc in range(NDC):
                    tr = kb.op(T, lambda e, dc=dc: e.matmul(
                        psA[:, 1024 + dc * 128:1024 + (dc + 1) * 128],
                        xbf[:, dc * 128:(dc + 1) * 128], ident,
                        start=True, stop=True), waits=ww if dc == 0 else ())
                pv[("xtr", s, t)] = tr

            def emit_xfm(t):
                t_xfm = kb.op(Sc, lambda e: e.copy(x_fm, psA[:, 1024:2048]),
                              waits=[pv[("xtr", s, t)],
                                     pv.get(("qkv", s, t - 1))])
                pv[("xfm", s, t)] = t_xfm

            for t in range(NT):
                if t == 0:
                    emit_xload(0)
                    emit_xtr(0)
                    emit_xfm(0)
                if t + 1 < NT:
                    emit_xload(t + 1)
                w_q = [pv[("xfm", s, t)], pv.get(("qc", s, t - 1))]
                if s == 0 and t == 0:
                    w_q = w_q + t_w
                w_k = [pv.get(("kc", s, t - 1))]
                w_v = [pv.get(("vev", s, t - 1)), pv.get(("kev", s, t - 2))]
                qkv_reg = []
                for wb, po, ww in ((wq_b, 0, w_q), (wk_b, 1024, w_k),
                                   (wv_b, None, w_v)):
                    mm = None
                    for half in range(2):
                        for dc in range(NDC):
                            def mmf(e, dc=dc, wb=wb, po=po, half=half):
                                pdst = psB if po is None else psA[:, po:po + 1024]
                                return e.matmul(
                                    pdst[:, half * 512:(half + 1) * 512],
                                    x_fm[:, dc * 128:(dc + 1) * 128],
                                    wb[:, dc * 1024 + half * 512:
                                       dc * 1024 + (half + 1) * 512],
                                    start=(dc == 0), stop=(dc == NDC - 1))
                            mm = kb.op(T, mmf,
                                       waits=ww if (dc == 0 and half == 0)
                                       else ())
                    qkv_reg.append(mm)
                mm = qkv_reg[2]
                pv[("qkv", s, t)] = mm

                qro = qslots[t % 2]
                kro = kslots[t % 2]
                t_qc = kb.op(Sc, lambda e, qro=qro: e.copy(qro, psA[:, 0:1024]),
                             waits=[qkv_reg[0], pv.get(("trq", s, t - 2)),
                                    pv.get(("qsq", s, t - 2))])
                pv[("qc", s, t)] = t_qc
                t_kc = kb.op(Sc, lambda e, kro=kro: e.copy(kro, psA[:, 1024:2048]),
                             waits=[qkv_reg[1], pv.get(("trk", s, t - 2)),
                                    pv.get(("ksq", s, t - 2))])
                pv[("kc", s, t)] = t_kc
                if t + 1 < NT:
                    emit_xtr(t + 1)
                t_vev = kb.op(Sc, lambda e: e.copy(
                    v_tm4[:, t, :, 0:64],
                    psB.rearrange("p (h c) -> p h c", c=64)),
                    waits=[mm, t_vones if (s == 0 and t == 0) else None])
                pv[("vev", s, t)] = t_vev
                if t + 1 < NT:
                    emit_xfm(t + 1)

                def emit_rope(buf, cast_tok):
                    b3 = buf.rearrange("p (h i) -> p h i", i=64)
                    cosb = rope[:, t * 64:t * 64 + 32].unsqueeze(1).broadcast_to(
                        [128, H, 32])
                    sinb = rope[:, t * 64 + 32:t * 64 + 64].unsqueeze(1).broadcast_to(
                        [128, H, 32])
                    lo, hi = b3[:, :, 0:32], b3[:, :, 32:64]
                    ta = tmpX[:, 0:512].rearrange("p (h i) -> p h i", i=32)
                    tb = tmpX[:, 512:1024].rearrange("p (h i) -> p h i", i=32)
                    a1 = kb.op(V, lambda e: e.tensor_tensor(ta, lo, cosb, ALU.mult),
                               waits=[cast_tok, t_rope, pv.get(("tmpX_r",))])
                    a2 = kb.op(V, lambda e: e.tensor_tensor(tb, lo, sinb, ALU.mult),
                               waits=[a1])
                    a3 = kb.op(V, lambda e: e.tensor_tensor(lo, hi, sinb, ALU.mult),
                               waits=[a2])
                    a4 = kb.op(V, lambda e: e.tensor_tensor(lo, ta, lo, ALU.subtract),
                               waits=[a3])
                    a5 = kb.op(V, lambda e: e.tensor_tensor(ta, hi, cosb, ALU.mult),
                               waits=[a4])
                    a6 = kb.op(V, lambda e: e.tensor_add(hi, ta, tb),
                               waits=[a5])
                    pv[("tmpX_r",)] = a6
                    return a6

                def emit_norm(dst_col, ro):
                    sq = kb.op(Sc, lambda e, ro=ro: e.activation(
                        tmpX, ro, AF.Square, bias=mxc[:, 3:4]),
                        waits=[pv.get(("tmpX_r",)), t_mxz])
                    pv[("tmpX_r",)] = sq
                    rd = kb.op(V, lambda e, dst_col=dst_col: e.reduce_sum(
                        dst_col, tmpX.rearrange("p (h i) -> p h i", i=64),
                        axis=AX.X), waits=[sq, pv.get(("nrm_r",))])
                    pv[("tmpX_r",)] = rd
                    return sq, rd

                rqf = emit_rope(qro, t_qc)
                pv[("ropeqf", s, t)] = rqf
                qsq, rnq = emit_norm(qn2[:, t * H:(t + 1) * H], qro)
                pv[("qsq", s, t)] = qsq
                pv[("qn2r", s, t)] = rnq
                rkf = emit_rope(kro, t_kc)
                pv[("ropekf", s, t)] = rkf
                ksq, rnk = emit_norm(kn2[:, t * H:(t + 1) * H], kro)
                pv[("ksq", s, t)] = ksq
                pv[("kn2r", s, t)] = rnk
                def emit_tr(tt):
                    qro2, kro2 = qslots[tt % 2], kslots[tt % 2]
                    trq = None
                    for hp in range(NPAIR):
                        trq = kb.op(T, lambda e, hp=hp, qro2=qro2: e.matmul(
                            psE[:, hp * 128:(hp + 1) * 128],
                            qro2[:, hp * 128:(hp + 1) * 128], ident,
                            start=True, stop=True),
                            waits=[pv[("ropeqf", s, tt)],
                                   pv.get(("qev", s, tt - 1))] if hp == 0 else ())
                    pv[("trq", s, tt)] = trq
                    t_qev = kb.op(V, lambda e, tt=tt: e.tensor_copy(
                        q_fm.rearrange("p (hp q) -> p hp q", q=S)[:, :, tt * 128:(tt + 1) * 128],
                        psE.rearrange("p (hp q) -> p hp q", q=128)), waits=[trq])
                    pv[("qev", s, tt)] = t_qev
                    trk = None
                    for hp in range(NPAIR):
                        trk = kb.op(T, lambda e, hp=hp, kro2=kro2: e.matmul(
                            psB[:, hp * 128:(hp + 1) * 128],
                            kro2[:, hp * 128:(hp + 1) * 128], ident,
                            start=True, stop=True),
                            waits=[pv[("ropekf", s, tt)],
                                   pv.get(("vev", s, tt + 1)),
                                   pv.get(("kev", s, tt - 1))] if hp == 0 else ())
                    pv[("trk", s, tt)] = trk
                    t_kev = kb.op(Sc, lambda e, tt=tt: e.copy(
                        k_fm.rearrange("p (hp q) -> p hp q", q=S)[:, :, tt * 128:(tt + 1) * 128],
                        psB.rearrange("p (hp q) -> p hp q", q=128)), waits=[trk])
                    pv[("kev", s, tt)] = t_kev
                if t > 0:
                    emit_tr(t - 1)
            emit_tr(NT - 1)

            p1_end = [pv[("qev", s, NT - 1)], pv[("kev", s, NT - 1)],
                      pv[("vev", s, NT - 1)]]

            kn_all = [pv[("kn2r", s, t)] for t in range(NT)]
            m1 = kb.op(V, lambda e: e.reduce_max(
                km[:, 0:H], kn2.rearrange("p (t h) -> p h t", h=H), axis=AX.X),
                waits=kn_all)
            m2 = kb.op(V, lambda e: e.tensor_scalar_mul(km_bf[:, 0:H], km[:, 0:H],
                                                        1.05), waits=[m1])
            m3 = kb.op(T, lambda e: e.matmul(psB[0:H, 0:128], km_bf[:, 0:H], ident,
                                             start=True, stop=True),
                       waits=[m2, pv[("kev", s, NT - 1)], t_ones])
            m4 = kb.op(V, lambda e: e.reduce_max(km2t[0:H, 0:1], psB[0:H, 0:128],
                                                 axis=AX.X), waits=[m3])
            m5 = kb.op(V, lambda e: e.tensor_scalar_mul(km2t_bf[0:H, 0:1],
                                                        km2t[0:H, 0:1], 1.0),
                       waits=[m4])
            m6 = kb.op(T, lambda e: e.matmul(psA[0:1, 0:H],
                                             km2t_bf[0:H, 0:1],
                                             ident[0:H, 0:H],
                                             start=True, stop=True),
                       waits=[m5, pv[("qc", s, NT - 1)], pv[("kc", s, NT - 1)]])
            m7 = kb.op(Sc, lambda e: e.copy(kmrow[0:1, 0:H], psA[0:1, 0:H]),
                       waits=[m6])
            m8 = kb.op(T, lambda e: e.matmul(psA[:, 512:512 + H], onesr,
                                             kmrow[0:1, 0:H],
                                             start=True, stop=True), waits=[m7])
            m9 = kb.op(Sc, lambda e: e.copy(kmb[:, 0:H], psA[:, 512:512 + H]),
                       waits=[m8])
            m10 = kb.op(V, lambda e: e.tensor_tensor(
                mbb.rearrange("p (t h) -> p t h", h=H),
                qn2.rearrange("p (t h) -> p t h", h=H),
                kmb[:, 0:H].unsqueeze(1).broadcast_to([128, NT, H]), ALU.mult),
                waits=[m9] + [pv[("qn2r", s, t)] for t in range(NT)])
            pv[("nrm_r",)] = m10
            ma = kb.op(V, lambda e: e.tensor_scalar_mul(
                mxbf[:, 0:NT * H], mbb, 1.02), waits=[m10])
            mc = None
            for c2 in range(NC2):
                w2c = min(128, NT * H - c2 * 128)
                mt1 = kb.op(T, lambda e, c2=c2, w2c=w2c: e.matmul(
                    psA[0:w2c, 0:128], mxbf[:, c2 * 128:c2 * 128 + w2c], ident,
                    start=True, stop=True), waits=[ma, m9, mc])
                mc = kb.op(V, lambda e, c2=c2, w2c=w2c: e.reduce_max(
                    mxc[0:w2c, c2:c2 + 1], psA[0:w2c, 0:128], axis=AX.X),
                    waits=[mt1])
            md = kb.op(V, lambda e: e.tensor_scalar_mul(
                mxcb[:, 0:NC2], mxc[:, 0:NC2], 1.0), waits=[mc, t_mxz])
            mg = None
            for c2 in range(NC2):
                w2c = min(128, NT * H - c2 * 128)
                me = kb.op(T, lambda e, c2=c2: e.matmul(
                    psA[0:1, 128:256], mxcb[:, c2:c2 + 1], ident,
                    start=True, stop=True), waits=[md, mg])
                mf = kb.op(Sc, lambda e, c2=c2: e.copy(
                    mrows[0:1, c2 * 128:(c2 + 1) * 128], psA[0:1, 128:256]),
                    waits=[me])
                mg1 = kb.op(T, lambda e, c2=c2, w2c=w2c: e.matmul(
                    psA[:, 512 + c2 * 128:512 + c2 * 128 + w2c], onesr,
                    mrows[0:1, c2 * 128:c2 * 128 + w2c], start=True, stop=True),
                    waits=[mf, t_ones])
                mg = kb.op(Sc, lambda e, c2=c2, w2c=w2c: e.copy(
                    mbb[:, c2 * 128:c2 * 128 + w2c],
                    psA[:, 512 + c2 * 128:512 + c2 * 128 + w2c]), waits=[mg1])
            mbb2 = kn2[:, 0:NSUPQ * H]
            msu = kb.op(V, lambda e: e.reduce_max(
                mbb2.rearrange("p (T h) -> p T h", h=H),
                mbb.rearrange("p (T f h) -> p T h f", f=QS, h=H), axis=AX.X),
                waits=[mg] + [pv[("kn2r", s, t2)] for t2 in range(NT)])
            m11 = kb.op(Sc, lambda e: e.activation(mbb2, mbb2, AF.Ln, scale=1.0,
                                                   bias=epsb), waits=[msu, t_eps])
            m12 = kb.op(Sc, lambda e: e.activation(mbb2, mbb2, AF.Exp, scale=0.5),
                        waits=[m11])
            m13 = kb.op(V, lambda e: e.tensor_scalar_mul(mbb2, mbb2, -SCALE),
                        waits=[m12])

            psA_user = {}
            pt_user = {}
            aev_prev = {}
            scnt = [0]

            def make_woln(Ts):
                q0 = Ts * QS * 128
                clos = []
                for j in range(QS):
                    gt = Ts * QS + j
                    for piece in range(2):
                        def go(j=j, gt=gt, piece=piece, q0=q0):
                            g2 = j * 2 + piece
                            bank = psA[:, 1536:2048]
                            if piece == 0:
                                pv[("xrtok",)] = kb.dma(
                                    Sy, "xr", xres,
                                    xs[s, q0 + j * 128:q0 + (j + 1) * 128, :],
                                    waits=[pv.get(("resid", s, gt - 1)),
                                           pv.get(("resid", s - 1, NT - 1))
                                           if gt == 0 else None])
                            ww = [pv[("aopair", NPAIR - 1)],
                                  pv.get(("tyg",))]
                            womm = None
                            for hp in range(NPAIR):
                                womm = kb.op(T, lambda e, hp=hp, j=j, piece=piece,
                                             bank=bank: e.matmul(
                                                 bank,
                                                 attn_sup[:, hp * 512 + j * 128:
                                                          hp * 512 + (j + 1) * 128],
                                                 wo_b[:, hp * 1024 + piece * 512:
                                                      hp * 1024 + (piece + 1) * 512],
                                                 start=(hp == 0),
                                                 stop=(hp == NPAIR - 1)),
                                             waits=ww if hp == 0 else ())
                            if j == QS - 1 and piece == 1:
                                pv[("wo_last",)] = womm
                            t_yp = kb.op(V, lambda e, piece=piece, bank=bank:
                                         e.tensor_add(
                                             ybuf[:, piece * 512:(piece + 1) * 512],
                                             bank,
                                             xres[:, piece * 512:(piece + 1) * 512]),
                                         waits=[womm, pv[("xrtok",)],
                                                ybuf_r[0] if piece == 0 else None,
                                                pv.get(("x2dma", s, gt - 1))
                                                if piece == 0 else None])
                            pv[("tyg",)] = t_yp
                            if piece == 0:
                                return
                            t_y = t_yp
                            pv[("resid", s, gt)] = t_y
                            t_ns = kb.op(V, lambda e: e.reduce_sum(
                                stat[:, ST_NSUM:ST_NSUM + 1], ybuf, axis=AX.X,
                                negate=True), waits=[t_y])
                            t_nm = kb.op(V, lambda e: e.tensor_scalar_mul(
                                stat[:, ST_NM:ST_NM + 1],
                                stat[:, ST_NSUM:ST_NSUM + 1], 1.0 / D),
                                waits=[t_ns])
                            t_sq = kb.op(V, lambda e: e.scalar_tensor_tensor(
                                sqbuf, ybuf, stat[:, ST_NM:ST_NM + 1], ybuf,
                                ALU.add, ALU.mult,
                                accum_out=stat[:, ST_SSQ:ST_SSQ + 1]),
                                waits=[t_nm, sqbuf_r[0]])
                            t_ln = kb.op(Sc, lambda e: e.activation(
                                stat[:, ST_LNV:ST_LNV + 1],
                                stat[:, ST_SSQ:ST_SSQ + 1],
                                AF.Ln, scale=1.0 / D, bias=epsb),
                                waits=[t_sq, t_eps])
                            t_rs = kb.op(Sc, lambda e: e.activation(
                                stat[:, ST_RSTD:ST_RSTD + 1],
                                stat[:, ST_LNV:ST_LNV + 1],
                                AF.Exp, scale=-0.5), waits=[t_ln])
                            t_x2 = kb.op(V, lambda e: e.tensor_scalar(
                                sqbuf, ybuf, stat[:, ST_NM:ST_NM + 1],
                                stat[:, ST_RSTD:ST_RSTD + 1], ALU.add, ALU.mult),
                                waits=[t_rs])
                            ybuf_r[0] = t_x2
                            t_st = kb.dma(
                                Sy, "st",
                                x2s[(s * NT + gt) * 128:(s * NT + gt + 1) * 128, :],
                                sqbuf, waits=[t_x2])
                            sqbuf_r[0] = t_st
                            pv[("x2dma", s, gt)] = t_st
                        clos.append(go)
                return clos

            queue = []
            kstep = [0]
            for Ts in range(NSUPQ):
                q0 = Ts * QS * 128
                qw = QS * 128
                nk = Ts * QS + QS
                for hp in range(NPAIR):
                    h = 2 * hp + 1

                    gp = Ts * NPAIR + hp
                    pvps = psE if gp % 2 == 0 else psB

                    def emit_pv(eh, k, ex_tok):
                        band_j = k - Ts * QS
                        coff = max(0, band_j) * 128
                        hh = 2 * hp + eh
                        ww2 = [ex_tok]
                        if k == 0:
                            ww2.append(aev_prev.get((gp % 2, eh)))
                        pvm = kb.op(T, lambda e, k=k, hh=hh, eh=eh,
                                    coff=coff, qw=qw, nk=nk,
                                    pvps=pvps: e.matmul(
                                        pvps[0:65, eh * 512 + coff:eh * 512 + qw],
                                        v_tm4[:, k, hh, :],
                                        pt_sb2[k % 2][:, eh * 512 + coff:
                                                       eh * 512 + qw],
                                        start=(k == 0), stop=(k == nk - 1)),
                                    waits=ww2)
                        pt_user[(k % 2, eh)] = pvm
                        return pvm

                    prev_ex = [None, None]
                    for k in range(nk):
                        sa = scnt[0] % 3
                        sb = (scnt[0] + 1) % 3
                        scnt[0] += 2
                        kstep[0] += 2
                        for _ in range(2):
                            if queue and queue[0][0] <= kstep[0]:
                                queue.pop(0)[1]()
                            else:
                                break
                        band_j = k - Ts * QS
                        coff = max(0, band_j) * 128
                        stmm = {}
                        for eh, sl in ((0, sa), (1, sb)):
                            ww = [psA_user.get(sl)]
                            if Ts == 0 and hp == 0 and k == 0 and eh == 0:
                                ww += p1_end + [m13]
                            stmm[eh] = kb.op(
                                T, lambda e, k=k, hp=hp, e64=eh * 64, sl=sl,
                                coff=coff, qw=qw: e.matmul(
                                    psA[:, sl * 512 + coff:sl * 512 + qw],
                                    k_fm[e64:e64 + 64,
                                         hp * S + k * 128:hp * S + (k + 1) * 128],
                                    q_fm[e64:e64 + 64,
                                         hp * S + q0 + coff:hp * S + q0 + qw],
                                    start=True, stop=(band_j < 0)),
                                waits=ww)
                        if band_j >= 0:
                            for eh, sl in ((0, sa), (1, sb)):
                                stmm[eh] = kb.op(
                                    T, lambda e, sl=sl, coff=coff: e.matmul(
                                        psA[:, sl * 512 + coff:
                                            sl * 512 + coff + 128],
                                        ident, maskt, start=False, stop=True),
                                    waits=[t_maskt] if (Ts == 0 and hp == 0
                                                        and eh == 0) else ())
                        if k > 0:
                            emit_pv(0, k - 1, prev_ex[0])
                            emit_pv(1, k - 1, prev_ex[1])
                        for eh, sl in ((0, sa), (1, sb)):
                            hh = 2 * hp + eh
                            ex = kb.op(Sc, lambda e, sl=sl, eh=eh, coff=coff,
                                       k=k, hh=hh, qw=qw: e.activation(
                                           pt_sb2[k % 2][:, eh * 512 + coff:
                                                          eh * 512 + qw],
                                           psA[:, sl * 512 + coff:sl * 512 + qw],
                                           AF.Exp,
                                           bias=mbb2[:, Ts * H + hh:
                                                     Ts * H + hh + 1],
                                           scale=SCALE),
                                       waits=[stmm[eh],
                                              pt_user.get((k % 2, eh))])
                            psA_user[sl] = ex
                            prev_ex[eh] = ex
                    pvm0 = emit_pv(0, nk - 1, prev_ex[0])
                    pvm1 = emit_pv(1, nk - 1, prev_ex[1])
                    if True:

                        def mk_rfn(e2, pvm_e, gp=gp, qw=qw, pvps=pvps):
                            def rfn():
                                lrv = lr_all[:, e2 * 512:e2 * 512 + qw]
                                r1 = kb.op(Sc, lambda e, lrv=lrv, e2=e2,
                                           pvps=pvps, qw=qw: e.activation(
                                    lrv, pvps[64:65, e2 * 512:e2 * 512 + qw],
                                    AF.Ln, scale=2.0 ** 40, bias=tinyb[0:1, :]),
                                    waits=[pvm_e, t_tiny,
                                           pv.get(("lr_r",))
                                           if e2 == 0 else None])
                                r2 = kb.op(Sc, lambda e, lrv=lrv: e.activation(
                                    lrv, lrv, AF.Exp, scale=-1.0), waits=[r1])
                                if e2 == 1:
                                    pv[("lr_r",)] = r2
                                pv[("r2", gp, e2)] = r2
                            return rfn

                        def norm_tail(hp=hp, h=h, gp=gp, qw=qw, pvps=pvps):
                            r2 = [pv[("r2", gp, 0)], pv[("r2", gp, 1)]]
                            lrv2 = lr_all.rearrange(
                                "p (e c) -> p e c", c=512)[:, :, 0:qw]
                            rhv = rhi_all.rearrange(
                                "p (e c) -> p e c", c=512)[:, :, 0:qw]
                            rlv = rlo_all.rearrange(
                                "p (e c) -> p e c", c=512)[:, :, 0:qw]
                            rc2 = kb.op(V, lambda e, lrv2=lrv2, rhv=rhv:
                                        e.tensor_copy(rhv, lrv2),
                                        waits=[r2, pv.get(("rbs_r",))])
                            rc3 = kb.op(V, lambda e, lrv2=lrv2, rhv=rhv, rlv=rlv:
                                        e.tensor_tensor(rlv, lrv2, rhv,
                                                        ALU.subtract),
                                        waits=[rc2])
                            pv[("lr_r",)] = rc3
                            rbm = None
                            for eh2 in range(2):
                                for rrow, st_ in ((rhi_all, True), (rlo_all, False)):
                                    rbm = kb.op(T, lambda e, rrow=rrow, st_=st_,
                                                qw=qw, eh2=eh2, pvps=pvps: e.matmul(
                                                    pvps[64:128, eh2 * 512:
                                                         eh2 * 512 + qw],
                                                    onesr[0:1, 0:64],
                                                    rrow[0:1, eh2 * 512:
                                                         eh2 * 512 + qw],
                                                    start=st_, stop=not st_,
                                                    tile_position=(0, 64)),
                                                waits=[rc3, t_ones]
                                                if (st_ and eh2 == 0) else ())
                            rbev = kb.op(V, lambda e, qw=qw, pvps=pvps:
                                         e.tensor_copy(
                                rbs_all.rearrange("p (e c) -> p e c", c=512)[:, :, 0:qw],
                                pvps[64:128, 0:1024].rearrange(
                                    "p (e c) -> p e c", c=512)[:, :, 0:qw]),
                                waits=[rbm, pv.get(("rbs_r",))])
                            aevE = kb.op(V, lambda e, hp=hp, qw=qw, pvps=pvps:
                                         e.scalar_tensor_tensor(
                                             attn_sup[0:64, hp * 512:hp * 512 + qw],
                                             pvps[0:64, 0:qw], 2.0 ** 40,
                                             rbs_all[0:64, 0:qw],
                                             ALU.mult, ALU.mult),
                                         waits=[rbev] +
                                         ([pv.get(("wo_last",))] if hp == 0 else []))
                            aev_prev[(gp % 2, 0)] = aevE
                            aevO = kb.op(V, lambda e, hp=hp, qw=qw, pvps=pvps:
                                         e.scalar_tensor_tensor(
                                             attnO[0:64, 0:qw],
                                             pvps[0:64, 512:512 + qw], 2.0 ** 40,
                                             rbs_all[0:64, 512:512 + qw],
                                             ALU.mult, ALU.mult),
                                         waits=[rbev, pv.get(("aotok",))])
                            pv[("aevd",)] = aevO
                            pv[("rbs_r",)] = aevO
                            t_ao = kb.dma(Sy, "ao",
                                          attn_sup[64:128, hp * 512:hp * 512 + qw],
                                          attnO[0:64, 0:qw],
                                          waits=[aevO, pv.get(("wo_last",))])
                            aev_prev[(gp % 2, 1)] = aevO
                            pv[("aotok",)] = t_ao
                            pv[("aopair", hp)] = (aevE, t_ao)
                        queue.append((kstep[0] + 1, mk_rfn(0, pvm0)))
                        queue.append((kstep[0] + 2, mk_rfn(1, pvm1)))
                        queue.append((kstep[0] + 7, norm_tail))
                queue += [(kstep[0] + 9 + 2 * i, fn)
                          for i, fn in enumerate(make_woln(Ts))]
            for _, go in queue:
                go()

        p2_end = [pv[("x2dma", s, NT - 1)] for s in range(NSEQ)]

        w1_slots = [RW[:, k * 2048:(k + 1) * 2048].bitcast(F32)
                    for k in range(6)]
        W1SEM = ("b0", "b1", "c0", "c1", "c2", "c3")
        bz_rd = {}
        bz_ev = {}
        for i in range(W1T):
            wt = w1_slots[i % 6]
            ww = [bz_rd.get(i - 6)]
            if i < 6:
                ww.append(pv[("qkv", NSEQ - 1, NT - 1)])
            t_l = kb.dma(G, W1SEM[i % 6], wt,
                         w1[i * 128:(i + 1) * 128, :], waits=ww)
            t_a = kb.op(V, lambda e, wt=wt: e.reduce_sum(
                stat[:, ST_NSUM:ST_NSUM + 1], wt, axis=AX.X,
                apply_absolute_value=True),
                waits=[t_l] + (p2_end if i == 0 else []))
            t_a = kb.op(V, lambda e: e.tensor_scalar_mul(
                stat[:, ST_NM:ST_NM + 1], stat[:, ST_NSUM:ST_NSUM + 1], 1.0 / D),
                waits=[t_a])
            sgs = sgn[:, (i % 2) * 1024:(i % 2) * 1024 + 1024]
            t_s = kb.op(Sc, lambda e, sgs=sgs, wt=wt: e.sign(sgs, wt),
                        waits=[t_l, pv.get(("bztr", i - 2))] +
                        (p2_end if i < 2 else []))
            t_b = kb.op(V, lambda e, sgs=sgs: e.tensor_scalar_mul(
                sgs, sgs, stat[:, ST_NM:ST_NM + 1]),
                waits=[t_s, t_a])
            pv[("bzb", i)] = t_b
            bz_rd[i] = [t_a, t_s]

            def bz_tr(ii):
                sg2 = sgn[:, (ii % 2) * 1024:(ii % 2) * 1024 + 1024]
                ps = (psB, psE)[ii % 2]
                trm = None
                for dc in range(NDC):
                    trm = kb.op(T, lambda e, dc=dc, ps=ps, sg2=sg2: e.matmul(
                        ps[:, dc * 128:(dc + 1) * 128],
                        sg2[:, dc * 128:(dc + 1) * 128], ident,
                        start=True, stop=True),
                        waits=([pv[("bzb", ii)], bz_ev.get(ii - 2)] +
                               (p2_end if ii == 0 else [])) if dc == 0 else ())
                pv[("bztr", ii)] = trm
                ev = kb.op(Sc, lambda e, ps=ps, ii=ii: e.copy(
                    bw1T.rearrange("p (dc f) -> p dc f", f=FF)[:, :, ii * 128:(ii + 1) * 128],
                    ps.rearrange("p (dc q) -> p dc q", q=128)), waits=[trm])
                bz_ev[ii] = ev
            if i > 0:
                bz_tr(i - 1)
        bz_tr(W1T - 1)
        bw1_done = [bz_ev[W1T - 1], bz_ev[W1T - 2]]
        w2_slots = [RW[:, k * 8192:(k + 1) * 8192].bitcast(F32)
                    for k in range(4)]
        W2SEM = ("b2", "w0", "w1", "w2")
        bz2_rd = {}
        bw2_wtoks = []
        for i in range(W2T):
            wt2 = w2_slots[i % 4]
            ww = [bz2_rd.get(i - 4)]
            if i < 4:
                ww += [bz_rd.get(W1T - 1 - k) for k in range(6)]
            t_l = kb.dma(G, W2SEM[i % 4], wt2, w2[i * 128:(i + 1) * 128, :],
                         waits=ww)
            t_a = kb.op(V, lambda e, wt2=wt2: e.reduce_sum(
                stat[:, ST_NSUM:ST_NSUM + 1], wt2, axis=AX.X,
                apply_absolute_value=True), waits=[t_l])
            t_a = kb.op(V, lambda e: e.tensor_scalar_mul(
                stat[:, ST_NM:ST_NM + 1], stat[:, ST_NSUM:ST_NSUM + 1], 1.0 / FF),
                waits=[t_a])
            t_s = kb.op(Sc, lambda e, wt2=wt2: e.sign(sgn, wt2),
                        waits=[t_l, pv.get(("bz2tr", i - 1)),
                               pv.get(("bztr", W1T - 1)) if i == 0 else None] +
                        (bw1_done if i == 0 else []))
            t_b = kb.op(V, lambda e: e.tensor_scalar_mul(
                sgn, sgn, stat[:, ST_NM:ST_NM + 1]), waits=[t_s, t_a])
            trm = None
            NG2 = NFC // 8
            for g in range(NG2):
                ps = (psB, psE)[g % 2]
                for c8 in range(8):
                    c = g * 8 + c8
                    trm = kb.op(T, lambda e, c=c, c8=c8, ps=ps: e.matmul(
                        ps[:, c8 * 128:(c8 + 1) * 128],
                        sgn[:, c * 128:(c + 1) * 128], ident,
                        start=True, stop=True),
                        waits=([t_b, pv.get(("bz2d", i, g - 2)),
                                pv.get(("bz2d", i - 1, NG2 - 2 + g))
                                if g < 2 else None,
                                bz_ev.get(W1T - 1) if (i == 0 and g == 0) else None,
                                bz_ev.get(W1T - 2) if (i == 0 and g == 1) else None]
                               ) if c8 == 0 else ())
                sgo = (sgn_s, sgn_s2)[g % 2]
                evs = kb.op(Sc, lambda e, ps=ps, sgo=sgo: e.copy(sgo, ps),
                            waits=[trm, pv.get(("bz2d", i, g - 2)),
                                   pv.get(("bz2d", i - 1, NG2 - 2 + g))
                                   if g < 2 else None])
                t_d = kb.dma(G, ("bw", "b0", "b1", "c0")[g % 4],
                             bw2s.rearrange("(a p) o -> p a o", p=128)
                             [:, g * 8:(g + 1) * 8, i * 128:(i + 1) * 128],
                             sgo.rearrange("p (a o) -> p a o", o=128),
                             waits=[evs])
                pv[("bz2d", i, g)] = t_d
                bw2_wtoks.append(t_d)
            pv[("bz2tr", i)] = trm
            bz2_rd[i] = [t_a, t_s]

        x2_all = ("st", kb.n["st"])
        pf = {}
        pf[("ybr", 0)] = ybuf_r[0]
        pf[("ybr", 1)] = pv.get(("resid", NSEQ - 1, NT - 1))
        sbst = (RB[:, 5120:7168].bitcast(F32), sqbuf)
        pf[("sq2r", 1)] = sqbuf_r[0]
        ybb = (ybuf, xres,
               RQ[:, 32768:34816].bitcast(F32), RQ[:, 34816:36864].bitcast(F32))
        xt4 = (xt_b[0], xt_b[1],
               RW[:, 20480:22528].bitcast(F32), RW[:, 22528:24576].bitcast(F32))
        XSEM = ("x0", "x1", "k4", "k5")
        xr4 = (xr2_b[0], xr2_b[1],
               RQ[:, 36864:38912].bitcast(F32), RQ[:, 38912:40960].bitcast(F32))
        XRSEM = ("tb", "rb", "k6", "k7")
        y_toks = []

        def emit_prep(u):
            for j in range(NSUPT):
                gt = u * NSUPT + j
                xt = xt4[j]
                t_x = pf.get(("xld", u, j))
                if t_x is None:
                    ww = [x2_all, pf.get(("x2c", u - 1, j))]
                    if u == 0 and j < 2:
                        ww += [pv.get(("xcast", NSEQ - 1, NT - 2 + j))]
                    if u == 0 and j >= 2:
                        ww += [pv.get(("bz2tr", W2T - 1)),
                               t_b2 if j == 2 else t_eps]
                    t_x = kb.dma(Sy, XSEM[j], xt,
                                 x2s[gt * 128:(gt + 1) * 128, :], waits=ww)
                t_c = pf.get(("x2c", u, j))
                if t_c is None:
                    ww = [t_x, pf.get(("x2tr", u, j - 1))]
                    if u == 0 and j == 0:
                        ww.append(pv.get(("bz2tr", W2T - 1)))
                    t_c = kb.op(Sc, lambda e, xt=xt: e.copy(x2bf, xt), waits=ww)
                    pf[("x2c", u, j)] = t_c
                ww = [t_c, pf.get(("fevA", u - 1, 0)), pf.get(("fevA", u - 1, 1)),
                      pf.get(("xf_ev", u, j - 1))]
                trm = None
                for dc in range(NDC):
                    trm = kb.op(T, lambda e, dc=dc: e.matmul(
                        psA[:, dc * 128:(dc + 1) * 128],
                        x2bf[:, dc * 128:(dc + 1) * 128], ident,
                        start=True, stop=True), waits=ww if dc == 0 else ())
                pf[("x2tr", u, j)] = trm
                ev = kb.op(Sc, lambda e, j=j: e.copy(
                    x2_fm.rearrange("p (dc q) -> p dc q", q=SUP)[:, :, j * 128:(j + 1) * 128],
                    psA[:, 0:1024].rearrange("p (dc q) -> p dc q", q=128)),
                    waits=[trm, pf.get(("ffn1mm", u - 1))])
                pf[("xf_ev", u, j)] = ev

        emit_prep(0)
        for u in range(NSUPER):
            g0 = u * NSUPT
            xf_done = [pf[("xf_ev", u, j)] for j in range(NSUPT)]
            ge = {}
            f1mm = None
            for fc in range(NFC):
                ps = (psB, psE)[fc % 2]
                mm = None
                for dc in range(NDC):
                    for q0 in range(0, SUP, 512):
                        q1 = min(SUP, q0 + 512)
                        ww = ()
                        if dc == 0 and q0 == 0:
                            ww = xf_done + [ge.get(fc - 2)]
                            if u == 0 and fc < 2:
                                ww = ww + [bz_ev.get(W1T - 1), bw2_wtoks[-1],
                                           bw2_wtoks[-2]]
                            else:
                                ww = ww + [pf.get(("fevB", u - 1, 2 + (fc % 2)))]
                        mm = kb.op(T, lambda e, fc=fc, dc=dc, ps=ps, q0=q0, q1=q1:
                                   e.matmul(ps[:, q0:q1],
                                            bw1T[:, dc * FF + fc * 128:
                                                 dc * FF + (fc + 1) * 128],
                                            x2_fm[:, dc * SUP + q0:dc * SUP + q1],
                                            start=(dc == 0), stop=(dc == NDC - 1)),
                                   waits=ww)
                f1mm = mm
                ge[fc] = kb.op(Sc, lambda e, fc=fc, ps=ps: e.activation(
                    h_fm[:, fc * SUP:(fc + 1) * SUP], ps[:, 0:SUP], AF.Gelu,
                    bias=b1s[:, fc:fc + 1], scale=1.0), waits=[mm, t_b1])
            pf[("ffn1mm", u)] = f1mm
            gelu_done = [ge[fc] for fc in range(NFC)]
            tile_ps = [psA[:, 0:1024], psA[:, 1024:2048], psB, psE]
            f2 = None
            for c in range(NFC):
                bwc = RW[:, 24576 + (c % 8) * 1024:24576 + (c % 8) * 1024 + 1024]
                ww = [pf.get(("bwc", u, c - 8)) if c >= 8 else
                      pf.get(("bwc", u - 1, NFC - 8 + c))]
                if u == 0 and c < 8:
                    ww += bw2_wtoks[-4:]
                t_w2 = kb.dma(G, ("c0", "c1", "c2", "c3",
                                  "g0", "g1", "g2", "g3")[c % 8], bwc,
                              bw2s[c * 128:(c + 1) * 128, :], waits=ww)
                for j in range(NSUPT):
                    for half in range(2):
                        ww = ()
                        if half == 0:
                            ww = [t_w2, gelu_done[c]]
                            if c == 0:
                                ww += [pf.get(("fevA", u - 1, j)) if j < 2 else
                                       pf.get(("fevB", u - 1, j)),
                                       xf_done[-1], gelu_done[NFC - 1]]
                        f2 = kb.op(T, lambda e, c=c, j=j, half=half: e.matmul(
                            tile_ps[j][:, half * 512:(half + 1) * 512],
                            h_fm[:, c * SUP + j * 128:c * SUP + (j + 1) * 128],
                            bwc[:, half * 512:(half + 1) * 512],
                            start=(c == 0), stop=(c == NFC - 1)), waits=ww)
                pf[("bwc", u, c)] = f2
            if u + 1 < NSUPER:
                for j2 in range(NSUPT):
                    t_xp = kb.dma(Sy, XSEM[j2], xt4[j2],
                                  x2s[(g0 + NSUPT + j2) * 128:
                                      (g0 + NSUPT + j2 + 1) * 128, :],
                                  waits=[pf.get(("x2c", u, j2))])
                    pf[("xld", u + 1, j2)] = t_xp
                t_cp = kb.op(G, lambda e: e.tensor_copy(x2bf, xt_b[0]),
                             waits=[pf[("xld", u + 1, 0)],
                                    pf.get(("x2tr", u, NSUPT - 1))])
                pf[("x2c", u + 1, 0)] = t_cp

            for j in range(NSUPT):
                t_y = kb.op(V, lambda e, j=j, yb=ybb[j]: e.tensor_add(
                    yb, tile_ps[j], b2s),
                    waits=[f2, t_b2, pf.get(("ybr", j))])
                key = ("fevA", u, j) if j < 2 else ("fevB", u, j)
                pf[key] = t_y
                pf[("tev", u, j)] = t_y
            if u + 1 < NSUPER:
                emit_prep(u + 1)
            for j in range(NSUPT):
                gt = g0 + j
                ww = [pf.get(("residrd", u - 1, j))]
                if u == 0 and j < 2:
                    ww += [pv.get(("bz2tr", W2T - 1)), t_tiny] + bw2_wtoks[-2:]
                if u == 0 and j >= 2:
                    ww += [x2_all, t_ones if j == 2 else t_maskt]
                t_xr = kb.dma(Sy, XRSEM[j], xr4[j],
                              x2s[gt * 128:(gt + 1) * 128, :], waits=ww)
                pf[("xrt", j)] = t_xr
            for j in range(NSUPT):
                gt = g0 + j
                sq_i, tt = divmod(gt, NT)
                xr = xr4[j]
                yb = ybb[j]
                t_y = pf[("tev", u, j)]
                t_y2 = kb.op(V, lambda e, yb=yb, xr=xr: e.tensor_add(yb, yb, xr),
                             waits=[t_y, pf[("xrt", j)]])
                pf[("residrd", u, j)] = t_y2
                t_ns = kb.op(V, lambda e, yb=yb: e.reduce_sum(
                    stat[:, ST_NSUM:ST_NSUM + 1], yb, axis=AX.X, negate=True),
                    waits=[t_y2])
                t_nm = kb.op(V, lambda e: e.tensor_scalar_mul(
                    stat[:, ST_NM:ST_NM + 1], stat[:, ST_NSUM:ST_NSUM + 1], 1.0 / D),
                    waits=[t_ns])
                sb = sbst[j % 2]
                t_sq = kb.op(V, lambda e, yb=yb, sb=sb: e.scalar_tensor_tensor(
                    sb, yb, stat[:, ST_NM:ST_NM + 1], yb, ALU.add, ALU.mult,
                    accum_out=stat[:, ST_SSQ:ST_SSQ + 1]),
                    waits=[t_nm, pf.get(("sq2r", j % 2)), pf.get(("lnp",))])
                t_ln = kb.op(Sc, lambda e: e.activation(
                    stat[:, ST_LNV:ST_LNV + 1], stat[:, ST_SSQ:ST_SSQ + 1], AF.Ln,
                    scale=1.0 / D, bias=epsb),
                    waits=[t_sq, t_eps, pf.get(("top",))])
                pf[("lnp",)] = t_ln
                t_rs = kb.op(Sc, lambda e: e.activation(
                    stat[:, ST_RSTD:ST_RSTD + 1], stat[:, ST_LNV:ST_LNV + 1], AF.Exp,
                    scale=-0.5), waits=[t_ln])
                t_o = kb.op(V, lambda e, yb=yb, sb=sb: e.tensor_scalar(
                    sb, yb, stat[:, ST_NM:ST_NM + 1],
                    stat[:, ST_RSTD:ST_RSTD + 1], ALU.add, ALU.mult), waits=[t_rs])
                pf[("top",)] = t_o
                pf[("ybr", j)] = t_o
                t_st = kb.dma(Sy, ("st", "xr")[j % 2],
                              y[sq_i, tt * 128:(tt + 1) * 128, :], sb,
                              waits=[t_o])
                pf[("sq2r", j % 2)] = t_st
                y_toks.append(t_st)

        kb.wait_only(Sy, y_toks)
        kb.wait_only(Sy, [(sname, kb.n[sname]) for sname in KB.DMA_SEMS])

    with contextlib.ExitStack() as stack:
        kb.setup_sems(stack)
        block = stack.enter_context(nc.Block())
        for name in ("sync", "scalar", "vector", "tensor", "gpsimd"):
            def make(name=name):
                def f(eng):
                    kb.begin_pass(name, eng)
                    program()
                return f
            getattr(block, name)(make())

    return nc, sbuf_left


def host_prep(cfg, n_cores, L0_attn_x, L0_attn_Wq, L0_attn_Wk, L0_attn_Wv,
              L0_attn_Wo, L0_ffn_w1, L0_ffn_b1, L0_ffn_w2, L0_ffn_b2):
    NSEQ, S, D, H, DK, FF = (cfg[k] for k in ("NSEQ", "S", "D", "H", "DK", "FF"))
    NFC = FF // 128
    x = np.ascontiguousarray(np.asarray(L0_attn_x, dtype=np.float32))
    Wq = np.asarray(L0_attn_Wq, np.float32)
    Wk = np.asarray(L0_attn_Wk, np.float32)
    Wv = np.asarray(L0_attn_Wv, np.float32)
    Wo = np.asarray(L0_attn_Wo, np.float32)
    w1 = np.ascontiguousarray(np.asarray(L0_ffn_w1, np.float32))
    w2 = np.ascontiguousarray(np.asarray(L0_ffn_w2, np.float32))
    b1 = np.asarray(L0_ffn_b1, np.float32)
    b2 = np.asarray(L0_ffn_b2, np.float32)

    perm = np.concatenate([np.concatenate([h * DK + np.arange(0, DK, 2),
                                           h * DK + np.arange(1, DK, 2)])
                           for h in range(H)])
    shared = dict(
        wqT=np.ascontiguousarray(Wq[perm].T),
        wkT=np.ascontiguousarray(Wk[perm].T),
        wvT=np.ascontiguousarray(Wv.T),
        woT=np.ascontiguousarray(Wo.T),
        w1=w1, w2=w2,
        b1r=np.ascontiguousarray(b1.reshape(NFC, 128).T),
        b2b=np.ascontiguousarray(np.broadcast_to(b2, (128, D))),
        identd=np.eye(128, dtype=ml_dtypes.bfloat16),
        epsd=np.full((128, 1), LN_EPS, dtype=np.float32),
        tinyd=np.full((128, 1), 1e-30, dtype=np.float32),
        onesd=np.ones((1, 128), dtype=ml_dtypes.bfloat16),
        masktd=np.where(np.arange(128)[:, None] <= np.arange(128)[None, :],
                        0.0, NEG_BIG).astype(ml_dtypes.bfloat16),
    )
    pos = np.arange(S, dtype=np.float64)
    inv = cfg["THETA"] ** (-np.arange(0, DK, 2, dtype=np.float64) / DK)
    ang = pos[:, None] * inv[None, :]
    shared["roped"] = np.ascontiguousarray(
        np.concatenate([np.cos(ang), np.sin(ang)], axis=1).astype(ml_dtypes.bfloat16))
    B = x.shape[0]
    per = B // n_cores
    assert per == NSEQ
    return [dict(shared, xs=np.ascontiguousarray(x[c * per:(c + 1) * per]))
            for c in range(n_cores)]



def kernel(**inputs):
    from concourse.bass_utils import run_bass_kernel_spmd
    cfg = cfg_full()
    nc, _ = build_program(cfg)
    in_maps = host_prep(cfg, 8, **inputs)
    res = run_bass_kernel_spmd(nc, in_maps, core_ids=list(range(8)))
    return np.concatenate([r["y"] for r in res.results], axis=0)


# version 24
# speedup vs baseline: 1.0903x; 1.0003x over previous
import contextlib
import math
import numpy as np
import ml_dtypes
import concourse.bass as bass
import concourse.mybir as mybir

F32 = mybir.dt.float32
BF16 = mybir.dt.bfloat16
AF = mybir.ActivationFunctionType
ALU = mybir.AluOpType
AX = mybir.AxisListType

LN_EPS = 1e-5
NEG_BIG = -1.0e30


def cfg_full():
    return dict(NSEQ=2, S=2048, D=1024, H=16, DK=64, FF=4096, SUP=512,
                THETA=10000.0)


class KB:
    def __init__(self, nc):
        self.nc = nc
        self.cur = None
        self.sems = {}
        self.n = {}
        self.last_wait = {}
        self.eng = {}

    DMA_SEMS = ("k0", "k1", "k2", "k3", "k4", "k5", "k6", "k7", "ao",
                "w0", "w1", "w2", "w3",
                "x0", "x1", "xr", "st", "b0", "b1", "b2", "bw",
                "c0", "c1", "c2", "c3", "tb", "rb",
                "g0", "g1", "g2", "g3")

    def setup_sems(self, stack):
        for name in ("ct", "cv", "cs", "cg") + self.DMA_SEMS:
            self.sems[name] = stack.enter_context(self.nc.semaphore(name))

    def begin_pass(self, engname, engobj):
        self.cur = engname
        self.eng[engname] = engobj
        self.n = {k: 0 for k in self.sems}
        self.last_wait = {k: 0 for k in self.sems}

    def _waits(self, waits):
        for tok in waits:
            if tok is None:
                continue
            if isinstance(tok, (list, tuple)) and tok and \
                    isinstance(tok[0], (list, tuple)):
                self._waits(tok)
                continue
            sem_name, val = tok
            if val > self.last_wait[sem_name]:
                self.eng[self.cur].wait_ge(self.sems[sem_name], val)
                self.last_wait[sem_name] = val

    def op(self, engname, emit, waits=()):
        sem = {"tensor": "ct", "vector": "cv", "scalar": "cs",
               "gpsimd": "cg"}[engname]
        self.n[sem] += 1
        tok = (sem, self.n[sem])
        if self.cur == engname:
            self._waits(waits)
            emit(self.eng[engname]).then_inc(self.sems[sem], 1)
        return tok

    def dma(self, engname, sem, out, in_, waits=()):
        self.n[sem] += 16
        tok = (sem, self.n[sem])
        if self.cur == engname:
            self._waits(waits)
            self.eng[engname].dma_start(out=out, in_=in_).then_inc(self.sems[sem], 16)
        return tok

    def wait_only(self, engname, waits):
        if self.cur == engname:
            self._waits(waits)


def build_program(cfg):
    NSEQ, S, D, H, DK, FF, SUP = (cfg[k] for k in
                                  ("NSEQ", "S", "D", "H", "DK", "FF", "SUP"))
    NT = S // 128
    NDC = D // 128
    NFC = FF // 128
    NPAIR = H // 2
    NSUPT = SUP // 128
    NSUPER = (NSEQ * S) // SUP
    W1T = FF // 128
    W2T = D // 128
    QS = 4 if NT % 4 == 0 else NT
    NSUPQ = NT // QS
    SCALE = 1.0 / math.sqrt(DK)

    nc = bass.Bass("TRN2", target_bir_lowering=False, debug=False)

    xs = nc.dram_tensor("xs", [NSEQ, S, D], F32, kind="ExternalInput").ap()
    wqT = nc.dram_tensor("wqT", [D, D], F32, kind="ExternalInput").ap()
    wkT = nc.dram_tensor("wkT", [D, D], F32, kind="ExternalInput").ap()
    wvT = nc.dram_tensor("wvT", [D, D], F32, kind="ExternalInput").ap()
    woT = nc.dram_tensor("woT", [D, D], F32, kind="ExternalInput").ap()
    w1 = nc.dram_tensor("w1", [FF, D], F32, kind="ExternalInput").ap()
    w2 = nc.dram_tensor("w2", [D, FF], F32, kind="ExternalInput").ap()
    b1r = nc.dram_tensor("b1r", [128, NFC], F32, kind="ExternalInput").ap()
    b2b = nc.dram_tensor("b2b", [128, D], F32, kind="ExternalInput").ap()
    identd = nc.dram_tensor("identd", [128, 128], BF16, kind="ExternalInput").ap()
    masktd = nc.dram_tensor("masktd", [128, 128], BF16, kind="ExternalInput").ap()
    roped = nc.dram_tensor("roped", [S, 64], BF16, kind="ExternalInput").ap()
    onesd = nc.dram_tensor("onesd", [1, 128], BF16, kind="ExternalInput").ap()
    epsd = nc.dram_tensor("epsd", [128, 1], F32, kind="ExternalInput").ap()
    tinyd = nc.dram_tensor("tinyd", [128, 1], F32, kind="ExternalInput").ap()
    y = nc.dram_tensor("y", [NSEQ, S, D], F32, kind="ExternalOutput").ap()
    x2s = nc.dram_tensor("x2s", [NSEQ * S, D], F32, kind="Internal").ap()
    bw2s = nc.dram_tensor("bw2s", [FF, D], BF16, kind="Internal").ap()

    A = nc.alloc_sbuf_tensor
    ident = A("ident", [128, 128], BF16).ap()
    maskt = A("maskt", [128, 128], BF16).ap()
    rope = A("rope", [128, NT * 64], BF16).ap()
    b1s = A("b1s", [128, NFC], F32).ap()
    b2s = A("b2s", [128, D], F32).ap()
    stat = A("stat", [128, 16], F32).ap()
    epsb = A("epsb", [128, 1], F32).ap()
    tinyb = A("tinyb", [128, 1], F32).ap()
    onesr = A("onesr", [1, 128], BF16).ap()
    qn2 = A("qn2", [128, NT * H], F32).ap()
    kn2 = A("kn2", [128, NT * H], F32).ap()
    mbb = A("mbb", [128, NT * H], F32).ap()
    km = A("km", [128, 16], F32).ap()
    km_bf = A("km_bf", [128, 16], BF16).ap()
    km2t = A("km2t", [128, 1], F32).ap()
    km2t_bf = A("km2t_bf", [128, 1], BF16).ap()
    kmrow = A("kmrow", [1, 16], BF16).ap()
    kmb = A("kmb", [128, 16], F32).ap()
    NC2 = (NT * H + 127) // 128
    mxc = A("mxc", [128, 4], F32).ap()
    mxcb = A("mxcb", [128, 4], BF16).ap()
    mrows = A("mrows", [1, 256], BF16).ap()


    (ST_NSUM, ST_NM, ST_SSQ, ST_LNV, ST_RSTD, ST_M, ST_MB, ST_R, ST_MP) = range(9)

    RQ = A("RQ", [128, 49664], BF16).ap()
    q_fm = RQ[:, 0:NPAIR * S]
    k_fm = RQ[:, 16384:16384 + NPAIR * S]
    v_tm = RQ[:, 32768:32768 + NT * H * 65]
    v_tm4 = v_tm.rearrange("p (t h c) -> p t h c", h=H, c=65)
    bw1T = RQ[:, 0:NDC * FF]
    RW = A("RW", [128, 32768], BF16).ap()
    wq_b = RW[:, 0:8192]
    wk_b = RW[:, 8192:16384]
    wv_b = RW[:, 16384:24576]
    wo_b = RW[:, 24576:32768]
    h_fm = RW[:, 0:NFC * SUP]
    x2_fm = RW[:, 16384:16384 + NDC * SUP]
    bw2c = RW[:, 24576:28672]
    RS = A("RS", [128, 5120], F32).ap()
    xt_b = (RS[:, 0:1024], RS[:, 1024:2048])
    p_f32 = RS[:, 0:2048]
    xres = RS[:, 2048:3072]
    rbs_all = RS[0:64, 0:1024]
    lr_all = RS[0:1, 1024:2048]
    rhi_all = RS[0:1, 0:512].bitcast(BF16)
    rlo_all = RS[0:1, 512:1024].bitcast(BF16)
    sqbuf = RS[:, 3072:4096]
    ybuf = RS[:, 4096:5120]
    w2t_f = RS[:, 0:4096]
    RB = A("RB", [128, 7168], BF16).ap()
    xbf = RB[:, 0:1024]
    x_fm = RB[:, 1024:2048]
    qslots = (RB[:, 4096:5120], RB[:, 2048:3072])
    kslots = (RB[:, 5120:6144], RB[:, 3072:4096])
    attn_sup = RB[:, 0:4096]
    pt_sb2 = (RB[:, 4096:5120], RB[:, 5120:6144])
    attnO = RB[:, 6144:6656]
    sgn = RB[:, 0:4096]
    sgn_s = RB[:, 4096:5120]
    sgn_s2 = RB[:, 5120:6144]
    x2bf = RB[:, 0:1024]
    xr2_b = (RB[:, 1024:3072].bitcast(F32), RB[:, 3072:5120].bitcast(F32))
    tmpX = RB[:, 6144:7168]
    mxbf = RB[:, 6144:6144 + NT * H]

    psA = nc.alloc_psum_tensor("psA", [128, 2048], F32).ap()
    psB = nc.alloc_psum_tensor("psB", [128, 1024], F32).ap()
    psE = nc.alloc_psum_tensor("psE", [128, 1024], F32).ap()

    sbuf_left = nc.sbuf_bytes_remaining

    kb = KB(nc)
    T, V, Sc, Sy, G = "tensor", "vector", "scalar", "sync", "gpsimd"

    def program():
        pv = {}
        ybuf_r = [None]
        sqbuf_r = [None]

        t_id = kb.dma(Sy, "k0", ident, identd)
        t_maskt = kb.dma(Sy, "k7", maskt, masktd)
        t_rope = kb.dma(Sy, "k2", rope.rearrange("p (t c) -> p t c", c=64),
                        roped.rearrange("(t p) c -> p t c", p=128))
        t_b1 = kb.dma(Sy, "k3", b1s, b1r)
        t_b2 = kb.dma(Sy, "k4", b2s, b2b)
        t_eps = kb.dma(Sy, "k5", epsb, epsd)
        t_tiny = kb.dma(Sy, "tb", tinyb, tinyd)
        t_ones = kb.dma(Sy, "k6", onesr, onesd)
        t_vones = kb.op(V, lambda e: e.memset(v_tm4[:, :, :, 64:65], 1.0))
        t_mxz = kb.op(V, lambda e: e.memset(mxc, 0.0))
        t_w = []
        for wsem, dst, src in (("w0", wq_b, wqT), ("w1", wk_b, wkT),
                               ("w2", wv_b, wvT), ("w3", wo_b, woT)):
            t_w.append(kb.dma(G, wsem, dst.rearrange("p (c o) -> p c o", o=D),
                              src.rearrange("(c p) o -> p c o", p=128)))

        for s in range(NSEQ):
            def emit_xload(t):
                ww = [pv.get(("xcast", s, t - 2))]
                if s > 0 and t < 2:
                    ww.append(pv.get(("x2dma", s - 1, NT - 1)))
                t_x = kb.dma(Sy, ("x0", "x1")[t % 2], xt_b[t % 2],
                             xs[s, t * 128:(t + 1) * 128, :], waits=ww)
                ww = [t_x, pv.get(("xtr", s, t - 1))]
                if s > 0 and t == 0:
                    ww.append(pv.get(("x2dma", s - 1, NT - 1)))
                t_cast = kb.op(G, lambda e, t=t: e.tensor_copy(
                    xbf, xt_b[t % 2]), waits=ww)
                pv[("xcast", s, t)] = t_cast

            def emit_xtr(t):
                ww = [pv[("xcast", s, t)], t_id, pv.get(("kc", s, t - 1))]
                tr = None
                for dc in range(NDC):
                    tr = kb.op(T, lambda e, dc=dc: e.matmul(
                        psA[:, 1024 + dc * 128:1024 + (dc + 1) * 128],
                        xbf[:, dc * 128:(dc + 1) * 128], ident,
                        start=True, stop=True), waits=ww if dc == 0 else ())
                pv[("xtr", s, t)] = tr

            def emit_xfm(t):
                t_xfm = kb.op(Sc, lambda e: e.copy(x_fm, psA[:, 1024:2048]),
                              waits=[pv[("xtr", s, t)],
                                     pv.get(("qkv", s, t - 1))])
                pv[("xfm", s, t)] = t_xfm

            for t in range(NT):
                if t == 0:
                    emit_xload(0)
                    emit_xtr(0)
                    emit_xfm(0)
                if t + 1 < NT:
                    emit_xload(t + 1)
                w_q = [pv[("xfm", s, t)], pv.get(("qc", s, t - 1))]
                if s == 0 and t == 0:
                    w_q = w_q + t_w
                w_k = [pv.get(("kc", s, t - 1))]
                w_v = [pv.get(("vev", s, t - 1)), pv.get(("kev", s, t - 2))]
                qkv_reg = []
                qkv_half = {}
                for wi, (wb, po, ww) in enumerate(
                        ((wq_b, 0, w_q), (wk_b, 1024, w_k), (wv_b, None, w_v))):
                    mm = None
                    for half in range(2):
                        for dc in range(NDC):
                            def mmf(e, dc=dc, wb=wb, po=po, half=half):
                                pdst = psB if po is None else psA[:, po:po + 1024]
                                return e.matmul(
                                    pdst[:, half * 512:(half + 1) * 512],
                                    x_fm[:, dc * 128:(dc + 1) * 128],
                                    wb[:, dc * 1024 + half * 512:
                                       dc * 1024 + (half + 1) * 512],
                                    start=(dc == 0), stop=(dc == NDC - 1))
                            mm = kb.op(T, mmf,
                                       waits=ww if (dc == 0 and half == 0)
                                       else ())
                        qkv_half[(wi, half)] = mm
                    qkv_reg.append(mm)
                mm = qkv_reg[2]
                pv[("qkv", s, t)] = mm

                qro = qslots[t % 2]
                kro = kslots[t % 2]
                kb.op(Sc, lambda e, qro=qro: e.copy(
                    qro[:, 0:512], psA[:, 0:512]),
                    waits=[qkv_half[(0, 0)], pv.get(("trq", s, t - 2)),
                           pv.get(("qsq", s, t - 2))])
                t_qc = kb.op(Sc, lambda e, qro=qro: e.copy(
                    qro[:, 512:1024], psA[:, 512:1024]),
                    waits=[qkv_half[(0, 1)]])
                pv[("qc", s, t)] = t_qc
                kb.op(Sc, lambda e, kro=kro: e.copy(
                    kro[:, 0:512], psA[:, 1024:1536]),
                    waits=[qkv_half[(1, 0)], pv.get(("trk", s, t - 2)),
                           pv.get(("ksq", s, t - 2))])
                t_kc = kb.op(Sc, lambda e, kro=kro: e.copy(
                    kro[:, 512:1024], psA[:, 1536:2048]),
                    waits=[qkv_half[(1, 1)]])
                pv[("kc", s, t)] = t_kc
                if t + 1 < NT:
                    emit_xtr(t + 1)
                t_vev = kb.op(Sc, lambda e: e.copy(
                    v_tm4[:, t, :, 0:64],
                    psB.rearrange("p (h c) -> p h c", c=64)),
                    waits=[mm, t_vones if (s == 0 and t == 0) else None])
                pv[("vev", s, t)] = t_vev
                if t + 1 < NT:
                    emit_xfm(t + 1)

                def emit_rope(buf, cast_tok):
                    b3 = buf.rearrange("p (h i) -> p h i", i=64)
                    cosb = rope[:, t * 64:t * 64 + 32].unsqueeze(1).broadcast_to(
                        [128, H, 32])
                    sinb = rope[:, t * 64 + 32:t * 64 + 64].unsqueeze(1).broadcast_to(
                        [128, H, 32])
                    lo, hi = b3[:, :, 0:32], b3[:, :, 32:64]
                    ta = tmpX[:, 0:512].rearrange("p (h i) -> p h i", i=32)
                    tb = tmpX[:, 512:1024].rearrange("p (h i) -> p h i", i=32)
                    a1 = kb.op(V, lambda e: e.tensor_tensor(ta, lo, cosb, ALU.mult),
                               waits=[cast_tok, t_rope, pv.get(("tmpX_r",))])
                    a2 = kb.op(V, lambda e: e.tensor_tensor(tb, lo, sinb, ALU.mult),
                               waits=[a1])
                    a3 = kb.op(V, lambda e: e.tensor_tensor(lo, hi, sinb, ALU.mult),
                               waits=[a2])
                    a4 = kb.op(V, lambda e: e.tensor_tensor(lo, ta, lo, ALU.subtract),
                               waits=[a3])
                    a5 = kb.op(V, lambda e: e.tensor_tensor(ta, hi, cosb, ALU.mult),
                               waits=[a4])
                    a6 = kb.op(V, lambda e: e.tensor_add(hi, ta, tb),
                               waits=[a5])
                    pv[("tmpX_r",)] = a6
                    return a6

                def emit_norm(dst_col, ro):
                    sq = kb.op(Sc, lambda e, ro=ro: e.activation(
                        tmpX, ro, AF.Square, bias=mxc[:, 3:4]),
                        waits=[pv.get(("tmpX_r",)), t_mxz])
                    pv[("tmpX_r",)] = sq
                    rd = kb.op(V, lambda e, dst_col=dst_col: e.reduce_sum(
                        dst_col, tmpX.rearrange("p (h i) -> p h i", i=64),
                        axis=AX.X), waits=[sq, pv.get(("nrm_r",))])
                    pv[("tmpX_r",)] = rd
                    return sq, rd

                rqf = emit_rope(qro, t_qc)
                pv[("ropeqf", s, t)] = rqf
                qsq, rnq = emit_norm(qn2[:, t * H:(t + 1) * H], qro)
                pv[("qsq", s, t)] = qsq
                pv[("qn2r", s, t)] = rnq
                rkf = emit_rope(kro, t_kc)
                pv[("ropekf", s, t)] = rkf
                ksq, rnk = emit_norm(kn2[:, t * H:(t + 1) * H], kro)
                pv[("ksq", s, t)] = ksq
                pv[("kn2r", s, t)] = rnk
                def emit_tr(tt):
                    qro2, kro2 = qslots[tt % 2], kslots[tt % 2]
                    trq = None
                    for hp in range(NPAIR):
                        trq = kb.op(T, lambda e, hp=hp, qro2=qro2: e.matmul(
                            psE[:, hp * 128:(hp + 1) * 128],
                            qro2[:, hp * 128:(hp + 1) * 128], ident,
                            start=True, stop=True),
                            waits=[pv[("ropeqf", s, tt)],
                                   pv.get(("qev", s, tt - 1))] if hp == 0 else ())
                    pv[("trq", s, tt)] = trq
                    t_qev = kb.op(V, lambda e, tt=tt: e.tensor_copy(
                        q_fm.rearrange("p (hp q) -> p hp q", q=S)[:, :, tt * 128:(tt + 1) * 128],
                        psE.rearrange("p (hp q) -> p hp q", q=128)), waits=[trq])
                    pv[("qev", s, tt)] = t_qev
                    trk = None
                    for hp in range(NPAIR):
                        trk = kb.op(T, lambda e, hp=hp, kro2=kro2: e.matmul(
                            psB[:, hp * 128:(hp + 1) * 128],
                            kro2[:, hp * 128:(hp + 1) * 128], ident,
                            start=True, stop=True),
                            waits=[pv[("ropekf", s, tt)],
                                   pv.get(("vev", s, tt + 1)),
                                   pv.get(("kev", s, tt - 1))] if hp == 0 else ())
                    pv[("trk", s, tt)] = trk
                    t_kev = kb.op(Sc, lambda e, tt=tt: e.copy(
                        k_fm.rearrange("p (hp q) -> p hp q", q=S)[:, :, tt * 128:(tt + 1) * 128],
                        psB.rearrange("p (hp q) -> p hp q", q=128)), waits=[trk])
                    pv[("kev", s, tt)] = t_kev
                if t > 0:
                    emit_tr(t - 1)
            emit_tr(NT - 1)

            p1_end = [pv[("qev", s, NT - 1)], pv[("kev", s, NT - 1)],
                      pv[("vev", s, NT - 1)]]

            kn_all = [pv[("kn2r", s, t)] for t in range(NT)]
            m1 = kb.op(V, lambda e: e.reduce_max(
                km[:, 0:H], kn2.rearrange("p (t h) -> p h t", h=H), axis=AX.X),
                waits=kn_all)
            m2 = kb.op(V, lambda e: e.tensor_scalar_mul(km_bf[:, 0:H], km[:, 0:H],
                                                        1.05), waits=[m1])
            m3 = kb.op(T, lambda e: e.matmul(psB[0:H, 0:128], km_bf[:, 0:H], ident,
                                             start=True, stop=True),
                       waits=[m2, pv[("kev", s, NT - 1)], t_ones])
            m4 = kb.op(V, lambda e: e.reduce_max(km2t[0:H, 0:1], psB[0:H, 0:128],
                                                 axis=AX.X), waits=[m3])
            m5 = kb.op(V, lambda e: e.tensor_scalar_mul(km2t_bf[0:H, 0:1],
                                                        km2t[0:H, 0:1], 1.0),
                       waits=[m4])
            m6 = kb.op(T, lambda e: e.matmul(psA[0:1, 0:H],
                                             km2t_bf[0:H, 0:1],
                                             ident[0:H, 0:H],
                                             start=True, stop=True),
                       waits=[m5, pv[("qc", s, NT - 1)], pv[("kc", s, NT - 1)]])
            m7 = kb.op(Sc, lambda e: e.copy(kmrow[0:1, 0:H], psA[0:1, 0:H]),
                       waits=[m6])
            m8 = kb.op(T, lambda e: e.matmul(psA[:, 512:512 + H], onesr,
                                             kmrow[0:1, 0:H],
                                             start=True, stop=True), waits=[m7])
            m9 = kb.op(Sc, lambda e: e.copy(kmb[:, 0:H], psA[:, 512:512 + H]),
                       waits=[m8])
            m10 = kb.op(V, lambda e: e.tensor_tensor(
                mbb.rearrange("p (t h) -> p t h", h=H),
                qn2.rearrange("p (t h) -> p t h", h=H),
                kmb[:, 0:H].unsqueeze(1).broadcast_to([128, NT, H]), ALU.mult),
                waits=[m9] + [pv[("qn2r", s, t)] for t in range(NT)])
            pv[("nrm_r",)] = m10
            ma = kb.op(V, lambda e: e.tensor_scalar_mul(
                mxbf[:, 0:NT * H], mbb, 1.02), waits=[m10])
            mc = None
            for c2 in range(NC2):
                w2c = min(128, NT * H - c2 * 128)
                mt1 = kb.op(T, lambda e, c2=c2, w2c=w2c: e.matmul(
                    psA[0:w2c, 0:128], mxbf[:, c2 * 128:c2 * 128 + w2c], ident,
                    start=True, stop=True), waits=[ma, m9, mc])
                mc = kb.op(V, lambda e, c2=c2, w2c=w2c: e.reduce_max(
                    mxc[0:w2c, c2:c2 + 1], psA[0:w2c, 0:128], axis=AX.X),
                    waits=[mt1])
            md = kb.op(V, lambda e: e.tensor_scalar_mul(
                mxcb[:, 0:NC2], mxc[:, 0:NC2], 1.0), waits=[mc, t_mxz])
            mg = None
            for c2 in range(NC2):
                w2c = min(128, NT * H - c2 * 128)
                me = kb.op(T, lambda e, c2=c2: e.matmul(
                    psA[0:1, 128:256], mxcb[:, c2:c2 + 1], ident,
                    start=True, stop=True), waits=[md, mg])
                mf = kb.op(Sc, lambda e, c2=c2: e.copy(
                    mrows[0:1, c2 * 128:(c2 + 1) * 128], psA[0:1, 128:256]),
                    waits=[me])
                mg1 = kb.op(T, lambda e, c2=c2, w2c=w2c: e.matmul(
                    psA[:, 512 + c2 * 128:512 + c2 * 128 + w2c], onesr,
                    mrows[0:1, c2 * 128:c2 * 128 + w2c], start=True, stop=True),
                    waits=[mf, t_ones])
                mg = kb.op(Sc, lambda e, c2=c2, w2c=w2c: e.copy(
                    mbb[:, c2 * 128:c2 * 128 + w2c],
                    psA[:, 512 + c2 * 128:512 + c2 * 128 + w2c]), waits=[mg1])
            mbb2 = kn2[:, 0:NSUPQ * H]
            msu = kb.op(V, lambda e: e.reduce_max(
                mbb2.rearrange("p (T h) -> p T h", h=H),
                mbb.rearrange("p (T f h) -> p T h f", f=QS, h=H), axis=AX.X),
                waits=[mg] + [pv[("kn2r", s, t2)] for t2 in range(NT)])
            m11 = kb.op(Sc, lambda e: e.activation(mbb2, mbb2, AF.Ln, scale=1.0,
                                                   bias=epsb), waits=[msu, t_eps])
            m12 = kb.op(Sc, lambda e: e.activation(mbb2, mbb2, AF.Exp, scale=0.5),
                        waits=[m11])
            m13 = kb.op(V, lambda e: e.tensor_scalar_mul(mbb2, mbb2, -SCALE),
                        waits=[m12])

            psA_user = {}
            pt_user = {}
            aev_prev = {}
            scnt = [0]

            def make_woln(Ts):
                q0 = Ts * QS * 128
                clos = []
                for j in range(QS):
                    gt = Ts * QS + j
                    for piece in range(2):
                        def go(j=j, gt=gt, piece=piece, q0=q0):
                            g2 = j * 2 + piece
                            bank = psA[:, 1536:2048]
                            if piece == 0:
                                pv[("xrtok",)] = kb.dma(
                                    Sy, "xr", xres,
                                    xs[s, q0 + j * 128:q0 + (j + 1) * 128, :],
                                    waits=[pv.get(("resid", s, gt - 1)),
                                           pv.get(("resid", s - 1, NT - 1))
                                           if gt == 0 else None])
                            ww = [pv[("aopair", NPAIR - 1)],
                                  pv.get(("tyg",))]
                            womm = None
                            for hp in range(NPAIR):
                                womm = kb.op(T, lambda e, hp=hp, j=j, piece=piece,
                                             bank=bank: e.matmul(
                                                 bank,
                                                 attn_sup[:, hp * 512 + j * 128:
                                                          hp * 512 + (j + 1) * 128],
                                                 wo_b[:, hp * 1024 + piece * 512:
                                                      hp * 1024 + (piece + 1) * 512],
                                                 start=(hp == 0),
                                                 stop=(hp == NPAIR - 1)),
                                             waits=ww if hp == 0 else ())
                            if j == QS - 1 and piece == 1:
                                pv[("wo_last",)] = womm
                            t_yp = kb.op(V, lambda e, piece=piece, bank=bank:
                                         e.tensor_add(
                                             ybuf[:, piece * 512:(piece + 1) * 512],
                                             bank,
                                             xres[:, piece * 512:(piece + 1) * 512]),
                                         waits=[womm, pv[("xrtok",)],
                                                ybuf_r[0] if piece == 0 else None,
                                                pv.get(("x2dma", s, gt - 1))
                                                if piece == 0 else None])
                            pv[("tyg",)] = t_yp
                            if piece == 0:
                                return
                            t_y = t_yp
                            pv[("resid", s, gt)] = t_y
                            t_ns = kb.op(V, lambda e: e.reduce_sum(
                                stat[:, ST_NSUM:ST_NSUM + 1], ybuf, axis=AX.X,
                                negate=True), waits=[t_y])
                            t_nm = kb.op(V, lambda e: e.tensor_scalar_mul(
                                stat[:, ST_NM:ST_NM + 1],
                                stat[:, ST_NSUM:ST_NSUM + 1], 1.0 / D),
                                waits=[t_ns])
                            t_sq = kb.op(V, lambda e: e.scalar_tensor_tensor(
                                sqbuf, ybuf, stat[:, ST_NM:ST_NM + 1], ybuf,
                                ALU.add, ALU.mult,
                                accum_out=stat[:, ST_SSQ:ST_SSQ + 1]),
                                waits=[t_nm, sqbuf_r[0]])
                            t_ln = kb.op(Sc, lambda e: e.activation(
                                stat[:, ST_LNV:ST_LNV + 1],
                                stat[:, ST_SSQ:ST_SSQ + 1],
                                AF.Ln, scale=1.0 / D, bias=epsb),
                                waits=[t_sq, t_eps])
                            t_rs = kb.op(Sc, lambda e: e.activation(
                                stat[:, ST_RSTD:ST_RSTD + 1],
                                stat[:, ST_LNV:ST_LNV + 1],
                                AF.Exp, scale=-0.5), waits=[t_ln])
                            t_x2 = kb.op(V, lambda e: e.tensor_scalar(
                                sqbuf, ybuf, stat[:, ST_NM:ST_NM + 1],
                                stat[:, ST_RSTD:ST_RSTD + 1], ALU.add, ALU.mult),
                                waits=[t_rs])
                            ybuf_r[0] = t_x2
                            t_st = kb.dma(
                                Sy, "st",
                                x2s[(s * NT + gt) * 128:(s * NT + gt + 1) * 128, :],
                                sqbuf, waits=[t_x2])
                            sqbuf_r[0] = t_st
                            pv[("x2dma", s, gt)] = t_st
                        clos.append(go)
                return clos

            queue = []
            kstep = [0]
            for Ts in range(NSUPQ):
                q0 = Ts * QS * 128
                qw = QS * 128
                nk = Ts * QS + QS
                for hp in range(NPAIR):
                    h = 2 * hp + 1

                    gp = Ts * NPAIR + hp
                    pvps = psE if gp % 2 == 0 else psB

                    def emit_pv(eh, k, ex_tok):
                        band_j = k - Ts * QS
                        coff = max(0, band_j) * 128
                        hh = 2 * hp + eh
                        ww2 = [ex_tok]
                        if k == 0:
                            ww2.append(aev_prev.get((gp % 2, eh)))
                        pvm = kb.op(T, lambda e, k=k, hh=hh, eh=eh,
                                    coff=coff, qw=qw, nk=nk,
                                    pvps=pvps: e.matmul(
                                        pvps[0:65, eh * 512 + coff:eh * 512 + qw],
                                        v_tm4[:, k, hh, :],
                                        pt_sb2[k % 2][:, eh * 512 + coff:
                                                       eh * 512 + qw],
                                        start=(k == 0), stop=(k == nk - 1)),
                                    waits=ww2)
                        pt_user[(k % 2, eh)] = pvm
                        return pvm

                    prev_ex = [None, None]
                    for k in range(nk):
                        sa = scnt[0] % 3
                        sb = (scnt[0] + 1) % 3
                        scnt[0] += 2
                        kstep[0] += 2
                        for _ in range(2):
                            if queue and queue[0][0] <= kstep[0]:
                                queue.pop(0)[1]()
                            else:
                                break
                        band_j = k - Ts * QS
                        coff = max(0, band_j) * 128
                        stmm = {}
                        for eh, sl in ((0, sa), (1, sb)):
                            ww = [psA_user.get(sl)]
                            if Ts == 0 and hp == 0 and k == 0 and eh == 0:
                                ww += p1_end + [m13]
                            stmm[eh] = kb.op(
                                T, lambda e, k=k, hp=hp, e64=eh * 64, sl=sl,
                                coff=coff, qw=qw: e.matmul(
                                    psA[:, sl * 512 + coff:sl * 512 + qw],
                                    k_fm[e64:e64 + 64,
                                         hp * S + k * 128:hp * S + (k + 1) * 128],
                                    q_fm[e64:e64 + 64,
                                         hp * S + q0 + coff:hp * S + q0 + qw],
                                    start=True, stop=(band_j < 0)),
                                waits=ww)
                        if band_j >= 0:
                            for eh, sl in ((0, sa), (1, sb)):
                                stmm[eh] = kb.op(
                                    T, lambda e, sl=sl, coff=coff: e.matmul(
                                        psA[:, sl * 512 + coff:
                                            sl * 512 + coff + 128],
                                        ident, maskt, start=False, stop=True),
                                    waits=[t_maskt] if (Ts == 0 and hp == 0
                                                        and eh == 0) else ())
                        if k > 0:
                            emit_pv(0, k - 1, prev_ex[0])
                            emit_pv(1, k - 1, prev_ex[1])
                        for eh, sl in ((0, sa), (1, sb)):
                            hh = 2 * hp + eh
                            ex = kb.op(Sc, lambda e, sl=sl, eh=eh, coff=coff,
                                       k=k, hh=hh, qw=qw: e.activation(
                                           pt_sb2[k % 2][:, eh * 512 + coff:
                                                          eh * 512 + qw],
                                           psA[:, sl * 512 + coff:sl * 512 + qw],
                                           AF.Exp,
                                           bias=mbb2[:, Ts * H + hh:
                                                     Ts * H + hh + 1],
                                           scale=SCALE),
                                       waits=[stmm[eh],
                                              pt_user.get((k % 2, eh))])
                            psA_user[sl] = ex
                            prev_ex[eh] = ex
                    pvm0 = emit_pv(0, nk - 1, prev_ex[0])
                    pvm1 = emit_pv(1, nk - 1, prev_ex[1])
                    if True:

                        def mk_rfn(e2, pvm_e, gp=gp, qw=qw, pvps=pvps):
                            def rfn():
                                lrv = lr_all[:, e2 * 512:e2 * 512 + qw]
                                r1 = kb.op(Sc, lambda e, lrv=lrv, e2=e2,
                                           pvps=pvps, qw=qw: e.activation(
                                    lrv, pvps[64:65, e2 * 512:e2 * 512 + qw],
                                    AF.Ln, scale=2.0 ** 40, bias=tinyb[0:1, :]),
                                    waits=[pvm_e, t_tiny,
                                           pv.get(("lr_r",))
                                           if e2 == 0 else None])
                                r2 = kb.op(Sc, lambda e, lrv=lrv: e.activation(
                                    lrv, lrv, AF.Exp, scale=-1.0), waits=[r1])
                                if e2 == 1:
                                    pv[("lr_r",)] = r2
                                pv[("r2", gp, e2)] = r2
                            return rfn

                        def norm_tail(hp=hp, h=h, gp=gp, qw=qw, pvps=pvps):
                            r2 = [pv[("r2", gp, 0)], pv[("r2", gp, 1)]]
                            lrv2 = lr_all.rearrange(
                                "p (e c) -> p e c", c=512)[:, :, 0:qw]
                            rhv = rhi_all.rearrange(
                                "p (e c) -> p e c", c=512)[:, :, 0:qw]
                            rlv = rlo_all.rearrange(
                                "p (e c) -> p e c", c=512)[:, :, 0:qw]
                            rc2 = kb.op(V, lambda e, lrv2=lrv2, rhv=rhv:
                                        e.tensor_copy(rhv, lrv2),
                                        waits=[r2, pv.get(("rbs_r",))])
                            rc3 = kb.op(V, lambda e, lrv2=lrv2, rhv=rhv, rlv=rlv:
                                        e.tensor_tensor(rlv, lrv2, rhv,
                                                        ALU.subtract),
                                        waits=[rc2])
                            pv[("lr_r",)] = rc3
                            rbm = None
                            for eh2 in range(2):
                                for rrow, st_ in ((rhi_all, True), (rlo_all, False)):
                                    rbm = kb.op(T, lambda e, rrow=rrow, st_=st_,
                                                qw=qw, eh2=eh2, pvps=pvps: e.matmul(
                                                    pvps[64:128, eh2 * 512:
                                                         eh2 * 512 + qw],
                                                    onesr[0:1, 0:64],
                                                    rrow[0:1, eh2 * 512:
                                                         eh2 * 512 + qw],
                                                    start=st_, stop=not st_,
                                                    tile_position=(0, 64)),
                                                waits=[rc3, t_ones]
                                                if (st_ and eh2 == 0) else ())
                            rbev = kb.op(V, lambda e, qw=qw, pvps=pvps:
                                         e.tensor_copy(
                                rbs_all.rearrange("p (e c) -> p e c", c=512)[:, :, 0:qw],
                                pvps[64:128, 0:1024].rearrange(
                                    "p (e c) -> p e c", c=512)[:, :, 0:qw]),
                                waits=[rbm, pv.get(("rbs_r",))])
                            aevE = kb.op(V, lambda e, hp=hp, qw=qw, pvps=pvps:
                                         e.scalar_tensor_tensor(
                                             attn_sup[0:64, hp * 512:hp * 512 + qw],
                                             pvps[0:64, 0:qw], 2.0 ** 40,
                                             rbs_all[0:64, 0:qw],
                                             ALU.mult, ALU.mult),
                                         waits=[rbev] +
                                         ([pv.get(("wo_last",))] if hp == 0 else []))
                            aev_prev[(gp % 2, 0)] = aevE
                            aevO = kb.op(V, lambda e, hp=hp, qw=qw, pvps=pvps:
                                         e.scalar_tensor_tensor(
                                             attnO[0:64, 0:qw],
                                             pvps[0:64, 512:512 + qw], 2.0 ** 40,
                                             rbs_all[0:64, 512:512 + qw],
                                             ALU.mult, ALU.mult),
                                         waits=[rbev, pv.get(("aotok",))])
                            pv[("aevd",)] = aevO
                            pv[("rbs_r",)] = aevO
                            t_ao = kb.dma(Sy, "ao",
                                          attn_sup[64:128, hp * 512:hp * 512 + qw],
                                          attnO[0:64, 0:qw],
                                          waits=[aevO, pv.get(("wo_last",))])
                            aev_prev[(gp % 2, 1)] = aevO
                            pv[("aotok",)] = t_ao
                            pv[("aopair", hp)] = (aevE, t_ao)
                        queue.append((kstep[0] + 1, mk_rfn(0, pvm0)))
                        queue.append((kstep[0] + 2, mk_rfn(1, pvm1)))
                        queue.append((kstep[0] + 7, norm_tail))
                queue += [(kstep[0] + 9 + 2 * i, fn)
                          for i, fn in enumerate(make_woln(Ts))]
            for _, go in queue:
                go()

        p2_end = [pv[("x2dma", s, NT - 1)] for s in range(NSEQ)]

        w1_slots = [RW[:, k * 2048:(k + 1) * 2048].bitcast(F32)
                    for k in range(6)]
        W1SEM = ("b0", "b1", "c0", "c1", "c2", "c3")
        bz_rd = {}
        bz_ev = {}
        for i in range(W1T):
            wt = w1_slots[i % 6]
            ww = [bz_rd.get(i - 6)]
            if i < 6:
                ww.append(pv[("qkv", NSEQ - 1, NT - 1)])
            t_l = kb.dma(G, W1SEM[i % 6], wt,
                         w1[i * 128:(i + 1) * 128, :], waits=ww)
            t_a = kb.op(V, lambda e, wt=wt: e.reduce_sum(
                stat[:, ST_NSUM:ST_NSUM + 1], wt, axis=AX.X,
                apply_absolute_value=True),
                waits=[t_l] + (p2_end if i == 0 else []))
            t_a = kb.op(V, lambda e: e.tensor_scalar_mul(
                stat[:, ST_NM:ST_NM + 1], stat[:, ST_NSUM:ST_NSUM + 1], 1.0 / D),
                waits=[t_a])
            sgs = sgn[:, (i % 2) * 1024:(i % 2) * 1024 + 1024]
            t_s = kb.op(Sc, lambda e, sgs=sgs, wt=wt: e.sign(sgs, wt),
                        waits=[t_l, pv.get(("bztr", i - 2))] +
                        (p2_end if i < 2 else []))
            t_b = kb.op(V, lambda e, sgs=sgs: e.tensor_scalar_mul(
                sgs, sgs, stat[:, ST_NM:ST_NM + 1]),
                waits=[t_s, t_a])
            pv[("bzb", i)] = t_b
            bz_rd[i] = [t_a, t_s]

            def bz_tr(ii):
                sg2 = sgn[:, (ii % 2) * 1024:(ii % 2) * 1024 + 1024]
                ps = (psB, psE)[ii % 2]
                trm = None
                for dc in range(NDC):
                    trm = kb.op(T, lambda e, dc=dc, ps=ps, sg2=sg2: e.matmul(
                        ps[:, dc * 128:(dc + 1) * 128],
                        sg2[:, dc * 128:(dc + 1) * 128], ident,
                        start=True, stop=True),
                        waits=([pv[("bzb", ii)], bz_ev.get(ii - 2)] +
                               (p2_end if ii == 0 else [])) if dc == 0 else ())
                pv[("bztr", ii)] = trm
                ev = kb.op(Sc, lambda e, ps=ps, ii=ii: e.copy(
                    bw1T.rearrange("p (dc f) -> p dc f", f=FF)[:, :, ii * 128:(ii + 1) * 128],
                    ps.rearrange("p (dc q) -> p dc q", q=128)), waits=[trm])
                bz_ev[ii] = ev
            if i > 0:
                bz_tr(i - 1)
        bz_tr(W1T - 1)
        bw1_done = [bz_ev[W1T - 1], bz_ev[W1T - 2]]
        w2_slots = [RW[:, k * 8192:(k + 1) * 8192].bitcast(F32)
                    for k in range(4)]
        W2SEM = ("b2", "w0", "w1", "w2")
        bz2_rd = {}
        bw2_wtoks = []
        for i in range(W2T):
            wt2 = w2_slots[i % 4]
            ww = [bz2_rd.get(i - 4)]
            if i < 4:
                ww += [bz_rd.get(W1T - 1 - k) for k in range(6)]
            t_l = kb.dma(G, W2SEM[i % 4], wt2, w2[i * 128:(i + 1) * 128, :],
                         waits=ww)
            t_a = kb.op(V, lambda e, wt2=wt2: e.reduce_sum(
                stat[:, ST_NSUM:ST_NSUM + 1], wt2, axis=AX.X,
                apply_absolute_value=True), waits=[t_l])
            t_a = kb.op(V, lambda e: e.tensor_scalar_mul(
                stat[:, ST_NM:ST_NM + 1], stat[:, ST_NSUM:ST_NSUM + 1], 1.0 / FF),
                waits=[t_a])
            t_s = kb.op(Sc, lambda e, wt2=wt2: e.sign(sgn, wt2),
                        waits=[t_l, pv.get(("bz2tr", i - 1)),
                               pv.get(("bztr", W1T - 1)) if i == 0 else None] +
                        (bw1_done if i == 0 else []))
            t_b = kb.op(V, lambda e: e.tensor_scalar_mul(
                sgn, sgn, stat[:, ST_NM:ST_NM + 1]), waits=[t_s, t_a])
            trm = None
            NG2 = NFC // 8
            for g in range(NG2):
                ps = (psB, psE)[g % 2]
                for c8 in range(8):
                    c = g * 8 + c8
                    trm = kb.op(T, lambda e, c=c, c8=c8, ps=ps: e.matmul(
                        ps[:, c8 * 128:(c8 + 1) * 128],
                        sgn[:, c * 128:(c + 1) * 128], ident,
                        start=True, stop=True),
                        waits=([t_b, pv.get(("bz2d", i, g - 2)),
                                pv.get(("bz2d", i - 1, NG2 - 2 + g))
                                if g < 2 else None,
                                bz_ev.get(W1T - 1) if (i == 0 and g == 0) else None,
                                bz_ev.get(W1T - 2) if (i == 0 and g == 1) else None]
                               ) if c8 == 0 else ())
                sgo = (sgn_s, sgn_s2)[g % 2]
                evs = kb.op(Sc, lambda e, ps=ps, sgo=sgo: e.copy(sgo, ps),
                            waits=[trm, pv.get(("bz2d", i, g - 2)),
                                   pv.get(("bz2d", i - 1, NG2 - 2 + g))
                                   if g < 2 else None])
                t_d = kb.dma(G, ("bw", "b0", "b1", "c0")[g % 4],
                             bw2s.rearrange("(a p) o -> p a o", p=128)
                             [:, g * 8:(g + 1) * 8, i * 128:(i + 1) * 128],
                             sgo.rearrange("p (a o) -> p a o", o=128),
                             waits=[evs])
                pv[("bz2d", i, g)] = t_d
                bw2_wtoks.append(t_d)
            pv[("bz2tr", i)] = trm
            bz2_rd[i] = [t_a, t_s]

        x2_all = ("st", kb.n["st"])
        pf = {}
        pf[("ybr", 0)] = ybuf_r[0]
        pf[("ybr", 1)] = pv.get(("resid", NSEQ - 1, NT - 1))
        sbst = (RB[:, 5120:7168].bitcast(F32), sqbuf)
        pf[("sq2r", 1)] = sqbuf_r[0]
        ybb = (ybuf, xres,
               RQ[:, 32768:34816].bitcast(F32), RQ[:, 34816:36864].bitcast(F32))
        xt4 = (xt_b[0], xt_b[1],
               RW[:, 20480:22528].bitcast(F32), RW[:, 22528:24576].bitcast(F32))
        XSEM = ("x0", "x1", "k4", "k5")
        xr4 = (xr2_b[0], xr2_b[1],
               RQ[:, 36864:38912].bitcast(F32), RQ[:, 38912:40960].bitcast(F32))
        XRSEM = ("tb", "rb", "k6", "k7")
        y_toks = []

        def emit_prep(u):
            for j in range(NSUPT):
                gt = u * NSUPT + j
                xt = xt4[j]
                t_x = pf.get(("xld", u, j))
                if t_x is None:
                    ww = [x2_all, pf.get(("x2c", u - 1, j))]
                    if u == 0 and j < 2:
                        ww += [pv.get(("xcast", NSEQ - 1, NT - 2 + j))]
                    if u == 0 and j >= 2:
                        ww += [pv.get(("bz2tr", W2T - 1)),
                               t_b2 if j == 2 else t_eps]
                    t_x = kb.dma(Sy, XSEM[j], xt,
                                 x2s[gt * 128:(gt + 1) * 128, :], waits=ww)
                t_c = pf.get(("x2c", u, j))
                if t_c is None:
                    ww = [t_x, pf.get(("x2tr", u, j - 1))]
                    if u == 0 and j == 0:
                        ww.append(pv.get(("bz2tr", W2T - 1)))
                    t_c = kb.op(Sc, lambda e, xt=xt: e.copy(x2bf, xt), waits=ww)
                    pf[("x2c", u, j)] = t_c
                ww = [t_c, pf.get(("fevA", u - 1, 0)), pf.get(("fevA", u - 1, 1)),
                      pf.get(("xf_ev", u, j - 1))]
                trm = None
                for dc in range(NDC):
                    trm = kb.op(T, lambda e, dc=dc: e.matmul(
                        psA[:, dc * 128:(dc + 1) * 128],
                        x2bf[:, dc * 128:(dc + 1) * 128], ident,
                        start=True, stop=True), waits=ww if dc == 0 else ())
                pf[("x2tr", u, j)] = trm
                ev = kb.op(Sc, lambda e, j=j: e.copy(
                    x2_fm.rearrange("p (dc q) -> p dc q", q=SUP)[:, :, j * 128:(j + 1) * 128],
                    psA[:, 0:1024].rearrange("p (dc q) -> p dc q", q=128)),
                    waits=[trm, pf.get(("ffn1mm", u - 1))])
                pf[("xf_ev", u, j)] = ev

        emit_prep(0)
        for u in range(NSUPER):
            g0 = u * NSUPT
            xf_done = [pf[("xf_ev", u, j)] for j in range(NSUPT)]
            ge = {}
            f1mm = None
            for fc in range(NFC):
                ps = (psB, psE)[fc % 2]
                mm = None
                for dc in range(NDC):
                    for q0 in range(0, SUP, 512):
                        q1 = min(SUP, q0 + 512)
                        ww = ()
                        if dc == 0 and q0 == 0:
                            ww = xf_done + [ge.get(fc - 2)]
                            if u == 0 and fc < 2:
                                ww = ww + [bz_ev.get(W1T - 1), bw2_wtoks[-1],
                                           bw2_wtoks[-2]]
                            else:
                                ww = ww + [pf.get(("fevB", u - 1, 2 + (fc % 2)))]
                        mm = kb.op(T, lambda e, fc=fc, dc=dc, ps=ps, q0=q0, q1=q1:
                                   e.matmul(ps[:, q0:q1],
                                            bw1T[:, dc * FF + fc * 128:
                                                 dc * FF + (fc + 1) * 128],
                                            x2_fm[:, dc * SUP + q0:dc * SUP + q1],
                                            start=(dc == 0), stop=(dc == NDC - 1)),
                                   waits=ww)
                f1mm = mm
                ge[fc] = kb.op(Sc, lambda e, fc=fc, ps=ps: e.activation(
                    h_fm[:, fc * SUP:(fc + 1) * SUP], ps[:, 0:SUP], AF.Gelu,
                    bias=b1s[:, fc:fc + 1], scale=1.0), waits=[mm, t_b1])
            pf[("ffn1mm", u)] = f1mm
            gelu_done = [ge[fc] for fc in range(NFC)]
            tile_ps = [psA[:, 0:1024], psA[:, 1024:2048], psB, psE]
            f2 = None
            for c in range(NFC):
                bwc = RW[:, 24576 + (c % 8) * 1024:24576 + (c % 8) * 1024 + 1024]
                ww = [pf.get(("bwc", u, c - 8)) if c >= 8 else
                      pf.get(("bwc", u - 1, NFC - 8 + c))]
                if u == 0 and c < 8:
                    ww += bw2_wtoks[-4:]
                t_w2 = kb.dma(G, ("c0", "c1", "c2", "c3",
                                  "g0", "g1", "g2", "g3")[c % 8], bwc,
                              bw2s[c * 128:(c + 1) * 128, :], waits=ww)
                for j in range(NSUPT):
                    for half in range(2):
                        ww = ()
                        if half == 0:
                            ww = [t_w2, gelu_done[c]]
                            if c == 0:
                                ww += [pf.get(("fevA", u - 1, j)) if j < 2 else
                                       pf.get(("fevB", u - 1, j)),
                                       xf_done[-1], gelu_done[NFC - 1]]
                        f2 = kb.op(T, lambda e, c=c, j=j, half=half: e.matmul(
                            tile_ps[j][:, half * 512:(half + 1) * 512],
                            h_fm[:, c * SUP + j * 128:c * SUP + (j + 1) * 128],
                            bwc[:, half * 512:(half + 1) * 512],
                            start=(c == 0), stop=(c == NFC - 1)), waits=ww)
                pf[("bwc", u, c)] = f2
            if u + 1 < NSUPER:
                for j2 in range(NSUPT):
                    t_xp = kb.dma(Sy, XSEM[j2], xt4[j2],
                                  x2s[(g0 + NSUPT + j2) * 128:
                                      (g0 + NSUPT + j2 + 1) * 128, :],
                                  waits=[pf.get(("x2c", u, j2))])
                    pf[("xld", u + 1, j2)] = t_xp
                t_cp = kb.op(G, lambda e: e.tensor_copy(x2bf, xt_b[0]),
                             waits=[pf[("xld", u + 1, 0)],
                                    pf.get(("x2tr", u, NSUPT - 1))])
                pf[("x2c", u + 1, 0)] = t_cp

            for j in range(NSUPT):
                t_y = kb.op(V, lambda e, j=j, yb=ybb[j]: e.tensor_add(
                    yb, tile_ps[j], b2s),
                    waits=[f2, t_b2, pf.get(("ybr", j))])
                key = ("fevA", u, j) if j < 2 else ("fevB", u, j)
                pf[key] = t_y
                pf[("tev", u, j)] = t_y
            if u + 1 < NSUPER:
                emit_prep(u + 1)
            for j in range(NSUPT):
                gt = g0 + j
                ww = [pf.get(("residrd", u - 1, j))]
                if u == 0 and j < 2:
                    ww += [pv.get(("bz2tr", W2T - 1)), t_tiny] + bw2_wtoks[-2:]
                if u == 0 and j >= 2:
                    ww += [x2_all, t_ones if j == 2 else t_maskt]
                t_xr = kb.dma(Sy, XRSEM[j], xr4[j],
                              x2s[gt * 128:(gt + 1) * 128, :], waits=ww)
                pf[("xrt", j)] = t_xr
            for j in range(NSUPT):
                gt = g0 + j
                sq_i, tt = divmod(gt, NT)
                xr = xr4[j]
                yb = ybb[j]
                t_y = pf[("tev", u, j)]
                t_y2 = kb.op(V, lambda e, yb=yb, xr=xr: e.tensor_add(yb, yb, xr),
                             waits=[t_y, pf[("xrt", j)]])
                pf[("residrd", u, j)] = t_y2
                t_ns = kb.op(V, lambda e, yb=yb: e.reduce_sum(
                    stat[:, ST_NSUM:ST_NSUM + 1], yb, axis=AX.X, negate=True),
                    waits=[t_y2])
                t_nm = kb.op(V, lambda e: e.tensor_scalar_mul(
                    stat[:, ST_NM:ST_NM + 1], stat[:, ST_NSUM:ST_NSUM + 1], 1.0 / D),
                    waits=[t_ns])
                sb = sbst[j % 2]
                t_sq = kb.op(V, lambda e, yb=yb, sb=sb: e.scalar_tensor_tensor(
                    sb, yb, stat[:, ST_NM:ST_NM + 1], yb, ALU.add, ALU.mult,
                    accum_out=stat[:, ST_SSQ:ST_SSQ + 1]),
                    waits=[t_nm, pf.get(("sq2r", j % 2)), pf.get(("lnp",))])
                t_ln = kb.op(Sc, lambda e: e.activation(
                    stat[:, ST_LNV:ST_LNV + 1], stat[:, ST_SSQ:ST_SSQ + 1], AF.Ln,
                    scale=1.0 / D, bias=epsb),
                    waits=[t_sq, t_eps, pf.get(("top",))])
                pf[("lnp",)] = t_ln
                t_rs = kb.op(Sc, lambda e: e.activation(
                    stat[:, ST_RSTD:ST_RSTD + 1], stat[:, ST_LNV:ST_LNV + 1], AF.Exp,
                    scale=-0.5), waits=[t_ln])
                t_o = kb.op(V, lambda e, yb=yb, sb=sb: e.tensor_scalar(
                    sb, yb, stat[:, ST_NM:ST_NM + 1],
                    stat[:, ST_RSTD:ST_RSTD + 1], ALU.add, ALU.mult), waits=[t_rs])
                pf[("top",)] = t_o
                pf[("ybr", j)] = t_o
                t_st = kb.dma(Sy, ("st", "xr")[j % 2],
                              y[sq_i, tt * 128:(tt + 1) * 128, :], sb,
                              waits=[t_o])
                pf[("sq2r", j % 2)] = t_st
                y_toks.append(t_st)

        kb.wait_only(Sy, y_toks)
        kb.wait_only(Sy, [(sname, kb.n[sname]) for sname in KB.DMA_SEMS])

    with contextlib.ExitStack() as stack:
        kb.setup_sems(stack)
        block = stack.enter_context(nc.Block())
        for name in ("sync", "scalar", "vector", "tensor", "gpsimd"):
            def make(name=name):
                def f(eng):
                    kb.begin_pass(name, eng)
                    program()
                return f
            getattr(block, name)(make())

    return nc, sbuf_left


def host_prep(cfg, n_cores, L0_attn_x, L0_attn_Wq, L0_attn_Wk, L0_attn_Wv,
              L0_attn_Wo, L0_ffn_w1, L0_ffn_b1, L0_ffn_w2, L0_ffn_b2):
    NSEQ, S, D, H, DK, FF = (cfg[k] for k in ("NSEQ", "S", "D", "H", "DK", "FF"))
    NFC = FF // 128
    x = np.ascontiguousarray(np.asarray(L0_attn_x, dtype=np.float32))
    Wq = np.asarray(L0_attn_Wq, np.float32)
    Wk = np.asarray(L0_attn_Wk, np.float32)
    Wv = np.asarray(L0_attn_Wv, np.float32)
    Wo = np.asarray(L0_attn_Wo, np.float32)
    w1 = np.ascontiguousarray(np.asarray(L0_ffn_w1, np.float32))
    w2 = np.ascontiguousarray(np.asarray(L0_ffn_w2, np.float32))
    b1 = np.asarray(L0_ffn_b1, np.float32)
    b2 = np.asarray(L0_ffn_b2, np.float32)

    perm = np.concatenate([np.concatenate([h * DK + np.arange(0, DK, 2),
                                           h * DK + np.arange(1, DK, 2)])
                           for h in range(H)])
    shared = dict(
        wqT=np.ascontiguousarray(Wq[perm].T),
        wkT=np.ascontiguousarray(Wk[perm].T),
        wvT=np.ascontiguousarray(Wv.T),
        woT=np.ascontiguousarray(Wo.T),
        w1=w1, w2=w2,
        b1r=np.ascontiguousarray(b1.reshape(NFC, 128).T),
        b2b=np.ascontiguousarray(np.broadcast_to(b2, (128, D))),
        identd=np.eye(128, dtype=ml_dtypes.bfloat16),
        epsd=np.full((128, 1), LN_EPS, dtype=np.float32),
        tinyd=np.full((128, 1), 1e-30, dtype=np.float32),
        onesd=np.ones((1, 128), dtype=ml_dtypes.bfloat16),
        masktd=np.where(np.arange(128)[:, None] <= np.arange(128)[None, :],
                        0.0, NEG_BIG).astype(ml_dtypes.bfloat16),
    )
    pos = np.arange(S, dtype=np.float64)
    inv = cfg["THETA"] ** (-np.arange(0, DK, 2, dtype=np.float64) / DK)
    ang = pos[:, None] * inv[None, :]
    shared["roped"] = np.ascontiguousarray(
        np.concatenate([np.cos(ang), np.sin(ang)], axis=1).astype(ml_dtypes.bfloat16))
    B = x.shape[0]
    per = B // n_cores
    assert per == NSEQ
    return [dict(shared, xs=np.ascontiguousarray(x[c * per:(c + 1) * per]))
            for c in range(n_cores)]



def kernel(**inputs):
    from concourse.bass_utils import run_bass_kernel_spmd
    cfg = cfg_full()
    nc, _ = build_program(cfg)
    in_maps = host_prep(cfg, 8, **inputs)
    res = run_bass_kernel_spmd(nc, in_maps, core_ids=list(range(8)))
    return np.concatenate([r["y"] for r in res.results], axis=0)
